# version 01
# baseline (speedup 1.0000x reference)
import contextlib
import numpy as np
import ml_dtypes
import concourse.bass as bass
import concourse.mybir as mybir
from concourse.bass_utils import run_bass_kernel_spmd

F32 = mybir.dt.float32
BF16 = mybir.dt.bfloat16
AF = mybir.ActivationFunctionType
ALU = mybir.AluOpType
AX = mybir.AxisListType
BF16NP = ml_dtypes.bfloat16

N_CORES = 8
B = 2
S = 16384
D = 1024
TLOC = S // N_CORES
NT = TLOC // 128


ENGINES = ("tensor", "vector", "scalar", "gpsimd", "sync")


class Prog:
    def __init__(self, name="prog"):
        self.nc = bass.Bass("TRN2", target_bir_lowering=False)
        self.name = name
        self.stack = contextlib.ExitStack()
        self.ops = []
        self.counts = {e: 0 for e in ENGINES}
        self.dma_counts = {}
        self.deps = {}
        self.waited = {e: {} for e in ENGINES}
        self.sem_handles = {}
        self._finalized = False

    def dram(self, name, shape, dtype, kind="Internal"):
        t = self.nc.dram_tensor(name, list(shape), dtype, kind=kind)
        return t

    def sbuf(self, name, shape, dtype):
        return self.stack.enter_context(self.nc.sbuf_tensor(name, list(shape), dtype))

    def psum(self, name, shape, dtype=F32):
        return self.stack.enter_context(self.nc.psum_tensor(name, list(shape), dtype))

    @staticmethod
    def _key(item):
        if isinstance(item, tuple):
            h, sk = item
        else:
            h, sk = item, None
        name = getattr(h, "name", None)
        if name is None:
            name = h.tensor.name if hasattr(h, "tensor") else str(h)
        return (name, sk)

    def _entry(self, name):
        return self.deps.setdefault(name, {"w": {}, "r": {}})

    def _conflicts(self, table, sk):
        out = {}
        for (sk2, sem), val in table.items():
            if sk is None or sk2 is None or sk == sk2:
                if out.get(sem, 0) < val:
                    out[sem] = val
        return out

    def _record(self, table, name, sk, sem, val):
        k = (sk, sem)
        if table.get(k, 0) < val:
            table[k] = val

    def op(self, engine, fn, reads=(), writes=(), dma=False, dma_sem=None):
        needed = {}
        rkeys = [self._key(r) for r in reads]
        wkeys = [self._key(w) for w in writes]
        for (name, sk) in rkeys:
            e = self._entry(name)
            for sem, val in self._conflicts(e["w"], sk).items():
                if needed.get(sem, 0) < val:
                    needed[sem] = val
        for (name, sk) in wkeys:
            e = self._entry(name)
            for tab in (e["w"], e["r"]):
                for sem, val in self._conflicts(tab, sk).items():
                    if needed.get(sem, 0) < val:
                        needed[sem] = val
        waits = []
        wt = self.waited[engine]
        for sem, val in sorted(needed.items()):
            if sem.startswith("eng_" + engine) and not dma:
                pass
            if wt.get(sem, 0) < val:
                waits.append((sem, val))
                wt[sem] = val
        if dma:
            sem = "d_" + dma_sem
            self.dma_counts[sem] = self.dma_counts.get(sem, 0) + 16
            val_after = self.dma_counts[sem]
        else:
            sem = "eng_" + engine
            self.counts[engine] += 1
            val_after = self.counts[engine]
        incs = [(sem, 16 if dma else 1)]
        for (name, sk) in rkeys:
            self._record(self._entry(name)["r"], name, sk, sem, val_after)
        for (name, sk) in wkeys:
            self._record(self._entry(name)["w"], name, sk, sem, val_after)
        self.ops.append((engine, fn, waits, incs))
        return (sem, val_after)

    def dma(self, queue, out, in_, reads=(), writes=(), sem=None):
        if sem is None:
            pool = list(writes) or list(reads)
            sem = self._key(pool[0])[0]
        return self.op(queue, lambda e: e.dma_start(out=out, in_=in_),
                       reads=reads, writes=writes, dma=True, dma_sem=sem)

    def barrier_wait(self, engine, token):
        sem, val = token
        wt = self.waited[engine]
        if wt.get(sem, 0) >= val:
            return
        wt[sem] = val
        self.ops.append((engine, None, [(sem, val)], []))

    def finalize(self, final_tokens=()):
        assert not self._finalized
        self._finalized = True
        nc = self.nc
        sems = {}
        for e in ENGINES:
            if self.counts[e]:
                sems["eng_" + e] = self.stack.enter_context(nc.semaphore("eng_" + e))
        for sname in self.dma_counts:
            sems[sname] = self.stack.enter_context(nc.semaphore(sname))
        self.sem_handles = sems
        block = self.stack.enter_context(nc.Block())
        per_engine = {e: [] for e in ENGINES}
        for rec in self.ops:
            per_engine[rec[0]].append(rec)
        fin_engine = "gpsimd"
        final_tokens = list(final_tokens) + [(s, v) for s, v in self.dma_counts.items()]
        for e in ENGINES:
            eng_ops = per_engine[e]
            fins = [t for t in final_tokens] if e == fin_engine else []
            if not eng_ops and not fins:
                continue

            def make(eng_name, eng_ops, fins):
                def body(eng):
                    for (_, fn, waits, incs) in eng_ops:
                        for (sem, val) in waits:
                            if sem in sems:
                                eng.wait_ge(sems[sem], val)
                        if fn is not None:
                            inst = fn(eng)
                            for (sem, amt) in incs:
                                inst.then_inc(sems[sem], amt)
                    for (sem, val) in fins:
                        if sem in sems:
                            eng.wait_ge(sems[sem], val)
                return body

            getattr(block, e)(make(e, eng_ops, fins))
        return nc

    def close(self):
        self.stack.close()



def ln_stats(p, x_tile, sq_scratch, stat, n_feat=D, eps=1e-5):
    v, s = "vector", "scalar"
    p.op(v, lambda e: e.reduce_sum(out=stat["sum"][:], in_=x_tile[:], axis=AX.X),
         reads=[x_tile], writes=[stat["sum"]])
    p.op(s, lambda e: e.activation(out=sq_scratch[:], in_=x_tile[:], func=AF.Square,
                                   accum_out=stat["sumsq"][:]),
         reads=[x_tile], writes=[sq_scratch, stat["sumsq"]])
    p.op(v, lambda e: e.tensor_scalar(out=stat["mean"][:], in0=stat["sum"][:],
                                      scalar1=1.0 / n_feat, scalar2=None, op0=ALU.mult),
         reads=[stat["sum"]], writes=[stat["mean"]])
    p.op(v, lambda e: e.tensor_mul(out=stat["m2"][:], in0=stat["mean"][:], in1=stat["mean"][:]),
         reads=[stat["mean"]], writes=[stat["m2"]])
    p.op(v, lambda e: e.tensor_scalar(out=stat["nm2"][:], in0=stat["m2"][:], scalar1=-1.0,
                                      scalar2=None, op0=ALU.mult),
         reads=[stat["m2"]], writes=[stat["nm2"]])
    p.op(v, lambda e: e.scalar_tensor_tensor(out=stat["var"][:], in0=stat["sumsq"][:],
                                             scalar=1.0 / n_feat, in1=stat["nm2"][:],
                                             op0=ALU.mult, op1=ALU.add),
         reads=[stat["sumsq"], stat["nm2"]], writes=[stat["var"]])
    p.op(v, lambda e: e.tensor_scalar(out=stat["vare"][:], in0=stat["var"][:], scalar1=eps,
                                      scalar2=None, op0=ALU.add),
         reads=[stat["var"]], writes=[stat["vare"]])
    p.op(s, lambda e: e.activation(out=stat["std"][:], in_=stat["vare"][:], func=AF.Sqrt),
         reads=[stat["vare"]], writes=[stat["std"]])
    p.op(v, lambda e: e.reciprocal(out=stat["rstd"][:], in_=stat["std"][:]),
         reads=[stat["std"]], writes=[stat["rstd"]])
    p.op(v, lambda e: e.tensor_mul(out=stat["mr"][:], in0=stat["mean"][:], in1=stat["rstd"][:]),
         reads=[stat["mean"], stat["rstd"]], writes=[stat["mr"]])
    p.op(v, lambda e: e.tensor_scalar(out=stat["negmur"][:], in0=stat["mr"][:], scalar1=-1.0,
                                      scalar2=None, op0=ALU.mult),
         reads=[stat["mr"]], writes=[stat["negmur"]])


def make_stat(p, tag):
    names = ("sum", "sumsq", "mean", "m2", "nm2", "var", "vare", "std", "rstd", "mr", "negmur")
    return {n: p.sbuf(f"{tag}_{n}", [128, 1], F32) for n in names}



def build_moe(tag="moe"):
    p = Prog(tag)
    x_in = p.dram("x", [B, TLOC, D], F32, kind="ExternalInput").ap()
    We = p.dram("We", [8, D, D], F32, kind="ExternalInput").ap()
    be = p.dram("be", [8, D], F32, kind="ExternalInput").ap()
    Wr = p.dram("Wr", [8, D], F32, kind="ExternalInput").ap()
    br = p.dram("br", [8], F32, kind="ExternalInput").ap()
    ident_in = p.dram("ident_in", [128, 128], F32, kind="ExternalInput").ap()
    x_out = p.dram("xo", [B, TLOC, D], F32, kind="ExternalOutput").ap()

    ident = p.sbuf("ident", [128, 128], F32)
    identb = p.sbuf("identb", [128, 128], BF16)
    ones_row = p.sbuf("ones_row", [1, 128], BF16)
    p.dma("sync", ident[:], ident_in, writes=[ident])
    p.op("vector", lambda e: e.tensor_copy(out=identb[:], in_=ident[:]), reads=[ident], writes=[identb])
    p.op("vector", lambda e: e.memset(ones_row[:], 1.0), writes=[ones_row])

    WeT = p.sbuf("WeT", [128, 8, 8, 1024], BF16)
    wld = [p.sbuf(f"wld{i}", [128, 1024], F32) for i in range(2)]
    ex_ps = [p.psum(f"ex_ps{i}", [128, 512], F32) for i in range(4)]
    tr_ps = [ex_ps[0], ex_ps[1]]
    for e_i in range(8):
        for oc in range(8):
            w = wld[(e_i * 8 + oc) % 2]
            p.dma("sync", w[:], We[e_i, oc * 128:(oc + 1) * 128, :], writes=[w])
            for kb in range(8):
                tp = tr_ps[kb % 2]
                sl = w[:, kb * 128:(kb + 1) * 128]
                p.op("tensor", lambda e, sl=sl, tp=tp: e.transpose(out=tp[:, 0:128], in_=sl, identity=ident[:]),
                     reads=[w, ident], writes=[tp])
                dsl = WeT[:, e_i, kb, oc * 128:(oc + 1) * 128]
                p.op("scalar", lambda e, dsl=dsl, tp=tp: e.activation(out=dsl, in_=tp[:, 0:128], func=AF.Copy),
                     reads=[tp], writes=[(WeT, (e_i, kb, oc))])
    WrT = p.sbuf("WrT", [128, 8, 8], BF16)
    wr_s = p.sbuf("wr_s", [8, 1024], F32)
    p.dma("sync", wr_s[:], Wr, writes=[wr_s])
    for kb in range(8):
        tp = tr_ps[kb % 2]
        sl = wr_s[:, kb * 128:(kb + 1) * 128]
        p.op("tensor", lambda e, sl=sl, tp=tp: e.transpose(out=tp[:, 0:8], in_=sl, identity=ident[0:8, 0:8]),
             reads=[wr_s, ident], writes=[tp])
        dsl = WrT[:, kb, :]
        p.op("scalar", lambda e, dsl=dsl, tp=tp: e.activation(out=dsl, in_=tp[:, 0:8], func=AF.Copy),
             reads=[tp], writes=[(WrT, kb)])
    beS = p.sbuf("beS", [8, 1024], BF16)
    be_f = p.sbuf("be_f", [8, 1024], F32)
    p.dma("sync", be_f[:], be, writes=[be_f])
    p.op("vector", lambda e: e.tensor_copy(out=beS[:], in_=be_f[:]), reads=[be_f], writes=[beS])
    brS = p.sbuf("brS", [1, 8], BF16)
    br_f = p.sbuf("br_f", [1, 8], F32)
    p.dma("sync", br_f[:], br.unsqueeze(0), writes=[br_f])
    p.op("vector", lambda e: e.tensor_copy(out=brS[:], in_=br_f[:]), reads=[br_f], writes=[brS])

    NPAR = 2
    xt = [p.sbuf(f"xt{i}", [128, D], F32) for i in range(NPAR)]
    sq = [p.sbuf(f"sq{i}", [128, D], F32) for i in range(NPAR)]
    lnb = [p.sbuf(f"lnb{i}", [128, D], BF16) for i in range(NPAR)]
    lnT = [p.sbuf(f"lnT{i}", [128, 8, 128], BF16) for i in range(NPAR)]
    stats = [make_stat(p, f"st{i}") for i in range(NPAR)]
    tr_bps = [p.psum(f"trb{i}", [128, 128], BF16) for i in range(2)]
    rt_ps = p.psum("rt_ps", [128, 8], F32)
    rw = p.sbuf("rw", [128, 8], F32)
    rwm = p.sbuf("rwm", [128, 8], F32)
    m1 = p.sbuf("m1", [128, 1], F32)
    m2 = p.sbuf("m2", [128, 1], F32)
    t8a = p.sbuf("t8a", [128, 8], F32)
    t8b = p.sbuf("t8b", [128, 8], F32)
    t8c = p.sbuf("t8c", [128, 8], F32)
    ssum = p.sbuf("ssum", [128, 1], F32)
    srec = p.sbuf("srec", [128, 1], F32)
    rwT_ps = p.psum("rwT_ps", [8, 128], F32)
    rwT = p.sbuf("rwT", [8, 128], BF16)
    acc = [p.sbuf(f"acc{i}", [128, D], F32) for i in range(3)]
    out_t = [p.sbuf(f"out_t{i}", [128, D], F32) for i in range(NPAR)]
    out_tokens = []

    for b in range(B):
        for ti in range(NT):
            par = (b * NT + ti) % NPAR
            X = xt[par]
            p.dma("sync", X[:], x_in[b, ti * 128:(ti + 1) * 128, :], writes=[X])
            st = stats[par]
            ln_stats(p, X, sq[par], st)
            LNB = lnb[par]
            p.op("scalar", lambda e, X=X, LNB=LNB, st=st:
                 e.activation(out=LNB[:], in_=X[:], func=AF.Identity,
                              bias=st["negmur"][:, 0:1], scale=st["rstd"][:, 0:1]),
                 reads=[X, st["negmur"], st["rstd"]], writes=[LNB])
            LT = lnT[par]
            for kb in range(8):
                TPS = tr_bps[kb % 2]
                sl = LNB[:, kb * 128:(kb + 1) * 128]
                p.op("tensor", lambda e, sl=sl, TPS=TPS: e.transpose(out=TPS[:], in_=sl, identity=identb[:]),
                     reads=[LNB, identb], writes=[TPS])
                dsl = LT[:, kb, :]
                p.op("scalar", lambda e, dsl=dsl, TPS=TPS: e.activation(out=dsl, in_=TPS[:], func=AF.Copy),
                     reads=[TPS], writes=[(LT, kb)])
            def rt_mm(e, LT=LT):
                e.matmul(out=rt_ps[:], lhsT=ones_row[:], rhs=brS[:], start=True, stop=False)
                last = None
                for kb in range(8):
                    last = e.matmul(out=rt_ps[:], lhsT=LT[:, kb, :], rhs=WrT[:, kb, :],
                                    start=False, stop=(kb == 7))
                return last
            p.op("tensor", rt_mm, reads=[ones_row, brS, LT, WrT], writes=[rt_ps])
            p.op("vector", lambda e: e.tensor_reduce(out=m1[:], in_=rt_ps[:], axis=AX.X, op=ALU.max, negate=True),
                 reads=[rt_ps], writes=[m1])
            p.op("scalar", lambda e: e.activation(out=rw[:], in_=rt_ps[:], func=AF.Exp, bias=m1[:, 0:1]),
                 reads=[rt_ps, m1], writes=[rw])
            p.op("vector", lambda e: e.reduce_sum(out=ssum[:], in_=rw[:], axis=AX.X),
                 reads=[rw], writes=[ssum])
            p.op("vector", lambda e: e.reciprocal(out=srec[:], in_=ssum[:]),
                 reads=[ssum], writes=[srec])
            p.op("vector", lambda e: e.tensor_reduce(out=m1[:], in_=rw[:], axis=AX.X, op=ALU.max, negate=True),
                 reads=[rw], writes=[m1])
            p.op("vector", lambda e: e.tensor_scalar(out=t8a[:], in0=rw[:], scalar1=m1[:, 0:1],
                                                     scalar2=None, op0=ALU.add),
                 reads=[rw, m1], writes=[t8a])
            p.op("vector", lambda e: e.tensor_scalar(out=t8b[:], in0=t8a[:], scalar1=0.0,
                                                     scalar2=None, op0=ALU.is_lt),
                 reads=[t8a], writes=[t8b])
            p.op("vector", lambda e: e.tensor_mul(out=t8c[:], in0=t8b[:], in1=rw[:]),
                 reads=[t8b, rw], writes=[t8c])
            p.op("vector", lambda e: e.tensor_reduce(out=m2[:], in_=t8c[:], axis=AX.X, op=ALU.max, negate=True),
                 reads=[t8c], writes=[m2])
            p.op("vector", lambda e: e.tensor_scalar(out=t8a[:], in0=rw[:], scalar1=m2[:, 0:1],
                                                     scalar2=None, op0=ALU.add),
                 reads=[rw, m2], writes=[t8a])
            p.op("vector", lambda e: e.tensor_scalar(out=t8b[:], in0=t8a[:], scalar1=0.0,
                                                     scalar2=None, op0=ALU.is_ge),
                 reads=[t8a], writes=[t8b])
            p.op("vector", lambda e: e.tensor_mul(out=rwm[:], in0=t8b[:], in1=rw[:]),
                 reads=[t8b, rw], writes=[rwm])
            p.op("vector", lambda e: e.tensor_scalar(out=rw[:], in0=rwm[:], scalar1=srec[:, 0:1],
                                                     scalar2=None, op0=ALU.mult),
                 reads=[rwm, srec], writes=[rw])
            p.op("tensor", lambda e: e.transpose(out=rwT_ps[:, :], in_=rw[:], identity=ident[:]),
                 reads=[rw, ident], writes=[rwT_ps])
            p.op("scalar", lambda e: e.activation(out=rwT[:], in_=rwT_ps[:], func=AF.Copy),
                 reads=[rwT_ps], writes=[rwT])
            for oc in range(2):
                pse = ex_ps[oc * 2]
                p.op("tensor", lambda e, oc=oc, pse=pse: e.matmul(out=pse[:], lhsT=rwT[:],
                                                                  rhs=beS[:, oc * 512:(oc + 1) * 512],
                                                                  start=True, stop=True),
                     reads=[rwT, beS], writes=[pse])
                p.op("vector", lambda e, oc=oc, pse=pse: e.tensor_copy(out=acc[0][:, oc * 512:(oc + 1) * 512],
                                                                       in_=pse[:]),
                     reads=[pse], writes=[(acc[0], oc)])
            prev = acc[0]
            for e_i in range(8):
                nxt = acc[1 + (e_i % 2)]
                for oc in range(2):
                    pse = ex_ps[oc * 2 + (e_i % 2)]
                    def emm(e, e_i=e_i, oc=oc, LT=LT, pse=pse):
                        last = None
                        for kb in range(8):
                            last = e.matmul(out=pse[:], lhsT=LT[:, kb, :],
                                            rhs=WeT[:, e_i, kb, oc * 512:(oc + 1) * 512],
                                            start=(kb == 0), stop=(kb == 7))
                        return last
                    p.op("tensor", emm, reads=[LT, WeT], writes=[pse])
                    p.op("vector", lambda e, oc=oc, pse=pse, prev=prev, nxt=nxt, e_i=e_i:
                         e.scalar_tensor_tensor(out=nxt[:, oc * 512:(oc + 1) * 512],
                                                in0=pse[:], scalar=rw[:, e_i:e_i + 1],
                                                in1=prev[:, oc * 512:(oc + 1) * 512],
                                                op0=ALU.mult, op1=ALU.add),
                         reads=[pse, rw, (prev, oc)], writes=[(nxt, oc)])
                prev = nxt
            OT = out_t[par]
            p.op("vector", lambda e, OT=OT, prev=prev, X=X:
                 e.scalar_tensor_tensor(out=OT[:], in0=prev[:], scalar=0.5, in1=X[:],
                                        op0=ALU.mult, op1=ALU.add),
                 reads=[prev, X], writes=[OT])
            tok = p.dma("gpsimd", x_out[b, ti * 128:(ti + 1) * 128, :], OT[:], reads=[OT])
            out_tokens.append(tok)

    p.finalize(final_tokens=out_tokens[-1:])
    return p



CHUNK = 512

def build_mamba_front(tag="mfront"):
    p = Prog(tag)
    H = 3
    TE = TLOC + 2 * H
    NCH = TLOC // CHUNK
    x_in = p.dram("x", [B, TE, D], F32, kind="ExternalInput").ap()
    Win = {d_: p.dram(f"Win_{d_}", [4096, D], F32, kind="ExternalInput").ap() for d_ in "fb"}
    convw = {d_: p.dram(f"convw_{d_}", [128, 16, 4], F32, kind="ExternalInput").ap() for d_ in "fb"}
    convb = {d_: p.dram(f"convb_{d_}", [128, 16], F32, kind="ExternalInput").ap() for d_ in "fb"}
    xproj = {d_: p.dram(f"xproj_{d_}", [96, 2048], F32, kind="ExternalInput").ap() for d_ in "fb"}
    dtw = {d_: p.dram(f"dtw_{d_}", [2048, 64], F32, kind="ExternalInput").ap() for d_ in "fb"}
    dtb = {d_: p.dram(f"dtb_{d_}", [128, 16], F32, kind="ExternalInput").ap() for d_ in "fb"}
    ident_in = p.dram("ident_in", [128, 128], F32, kind="ExternalInput").ap()
    uT_o = {d_: p.dram(f"uT_{d_}", [B, 2048, TLOC], BF16, kind="ExternalOutput").ap() for d_ in "fb"}
    zT_o = {d_: p.dram(f"zT_{d_}", [B, 2048, TLOC], BF16, kind="ExternalOutput").ap() for d_ in "fb"}
    dT_o = {d_: p.dram(f"dT_{d_}", [B, 2048, TLOC], BF16, kind="ExternalOutput").ap() for d_ in "fb"}
    duT_o = {d_: p.dram(f"duT_{d_}", [B, 2048, TLOC], BF16, kind="ExternalOutput").ap() for d_ in "fb"}
    bcT_o = {d_: p.dram(f"bcT_{d_}", [B, 32, TLOC], BF16, kind="ExternalOutput").ap() for d_ in "fb"}

    ident = p.sbuf("ident", [128, 128], F32)
    identb = p.sbuf("identb", [128, 128], BF16)
    p.dma("sync", ident[:], ident_in, writes=[ident])
    p.op("vector", lambda e: e.tensor_copy(out=identb[:], in_=ident[:]), reads=[ident], writes=[identb])

    lnT = p.sbuf("lnT", [128, 8, TE + 2], BF16)
    WinT = p.sbuf("WinT", [128, 8, 2048], BF16)
    xprojT = p.sbuf("xprojT", [128, 16, 96], BF16)
    dtwT = p.sbuf("dtwT", [64, 2048], BF16)
    cwcol = p.sbuf("cwcol", [128, 16, 4], F32)
    cbcol = p.sbuf("cbcol", [128, 16], F32)
    dtbcol = p.sbuf("dtbcol", [128, 16], F32)
    xsT = p.sbuf("xsT", [128, 16, CHUNK + H], BF16)
    uT = p.sbuf("uTs", [128, 16, CHUNK], BF16)
    dTc = p.sbuf("dTc", [128, 16, CHUNK], BF16)
    duTc = p.sbuf("duTc", [128, 16, CHUNK], BF16)
    conv_a = p.sbuf("conv_a", [128, CHUNK], BF16)
    conv_b2 = p.sbuf("conv_b2", [128, CHUNK], BF16)
    conv_c = p.sbuf("conv_c", [128, CHUNK], BF16)
    conv_d2 = p.sbuf("conv_d2", [128, CHUNK], BF16)
    zev = [p.sbuf(f"zev{i}", [128, CHUNK], BF16) for i in range(2)]
    xdb = p.sbuf("xdb", [96, CHUNK], BF16)
    bcs = p.sbuf("bcs", [32, CHUNK], BF16)
    sp0 = p.sbuf("sp0", [128, CHUNK], F32)
    sp1 = p.sbuf("sp1", [128, CHUNK], F32)
    sp2 = p.sbuf("sp2", [128, CHUNK], F32)
    xt = [p.sbuf(f"xt{i}", [128, D], F32) for i in range(2)]
    sq = [p.sbuf(f"sq{i}", [128, D], F32) for i in range(2)]
    lnb = [p.sbuf(f"lnb{i}", [128, D], BF16) for i in range(2)]
    stats = [make_stat(p, f"st{i}") for i in range(2)]
    wld = [p.sbuf(f"wld{i}", [128, 1024], F32) for i in range(2)]
    wsm = p.sbuf("wsm", [96, 2048], F32)
    mm_ps = [p.psum(f"mm_ps{i}", [128, 512], F32) for i in range(4)]
    _tr0 = p.psum("tr_ps0", [128, 128], F32)
    tr_ps = [_tr0, _tr0]
    trb_ps = [p.psum(f"trb_ps{i}", [128, 128], BF16) for i in range(2)]
    xd_ps = p.psum("xd_ps", [96, 512], F32)

    def wprep_win(d_, half):
        for rb in range(16):
            w = wld[rb % 2]
            p.dma("sync", w[:], Win[d_][half * 2048 + rb * 128:half * 2048 + (rb + 1) * 128, :], writes=[w])
            for kb in range(8):
                tp = tr_ps[kb % 2]
                sl = w[:, kb * 128:(kb + 1) * 128]
                p.op("tensor", lambda e, sl=sl, tp=tp: e.transpose(out=tp[:], in_=sl, identity=ident[:]),
                     reads=[w, ident], writes=[tp])
                dsl = WinT[:, kb, rb * 128:(rb + 1) * 128]
                p.op("scalar", lambda e, dsl=dsl, tp=tp: e.activation(out=dsl, in_=tp[:], func=AF.Copy),
                     reads=[tp], writes=[(WinT, rb)])
    def wprep_dir(d_):
        p.dma("sync", wsm[:], xproj[d_], writes=[wsm])
        for kb in range(16):
            tp = tr_ps[kb % 2]
            sl = wsm[:, kb * 128:(kb + 1) * 128]
            p.op("tensor", lambda e, sl=sl, tp=tp: e.transpose(out=tp[:, 0:96], in_=sl,
                                                               identity=ident[0:96, 0:96]),
                 reads=[wsm, ident], writes=[tp])
            dsl = xprojT[:, kb, :]
            p.op("scalar", lambda e, dsl=dsl, tp=tp: e.activation(out=dsl, in_=tp[:, 0:96], func=AF.Copy),
                 reads=[tp], writes=[(xprojT, kb)])
        for rb in range(16):
            w = wld[rb % 2]
            p.dma("sync", w[:, 0:64], dtw[d_][rb * 128:(rb + 1) * 128, :], writes=[w])
            tp = tr_ps[rb % 2]
            p.op("tensor", lambda e, w=w, tp=tp: e.transpose(out=tp[0:64, 0:128], in_=w[:, 0:64],
                                                             identity=ident[:]),
                 reads=[w, ident], writes=[tp])
            dsl = dtwT[:, rb * 128:(rb + 1) * 128]
            p.op("scalar", lambda e, dsl=dsl, tp=tp: e.activation(out=dsl, in_=tp[0:64, 0:128], func=AF.Copy),
                 reads=[tp], writes=[(dtwT, rb)])
        p.dma("sync", cwcol[:], convw[d_], writes=[cwcol])
        p.dma("sync", cbcol[:], convb[d_], writes=[cbcol])
        p.dma("sync", dtbcol[:], dtb[d_], writes=[dtbcol])

    def phase1_b(b):
        def do_tile(tok0, ntok, col0):
            par = (tok0 // 128) % 2
            X, SQ, LNB, st = xt[par], sq[par], lnb[par], stats[par]
            p.dma("sync", X[0:ntok, :], x_in[b, tok0:tok0 + ntok, :], writes=[X])
            ln_stats_n(p, X, SQ, st, ntok)
            p.op("scalar", lambda e, X=X, LNB=LNB, st=st, ntok=ntok:
                 e.activation(out=LNB[0:ntok, :], in_=X[0:ntok, :], func=AF.Identity,
                              bias=st["negmur"][0:ntok, 0:1], scale=st["rstd"][0:ntok, 0:1]),
                 reads=[X, st["negmur"], st["rstd"]], writes=[LNB])
            for kb in range(8):
                tp = trb_ps[kb % 2]
                sl = LNB[0:ntok, kb * 128:(kb + 1) * 128]
                idb = identb[0:ntok, 0:ntok]
                p.op("tensor", lambda e, sl=sl, tp=tp, ntok=ntok, idb=idb:
                     e.transpose(out=tp[:, 0:ntok], in_=sl, identity=idb),
                     reads=[LNB, identb], writes=[tp])
                dsl = lnT[:, kb, col0:col0 + ntok]
                p.op("scalar", lambda e, dsl=dsl, tp=tp, ntok=ntok:
                     e.activation(out=dsl, in_=tp[:, 0:ntok], func=AF.Copy),
                     reads=[tp], writes=[(lnT, (kb, col0))])
        do_tile(0, H, 0)
        for ti in range(NT):
            do_tile(H + ti * 128, 128, H + ti * 128)
        do_tile(H + TLOC, H, H + TLOC)

    def inproj_cols(b, col0, n, ech_list, dst, dst_off, psum_sel):
        for i, ech in enumerate(ech_list):
            ps = mm_ps[psum_sel(i)]
            def f(e, ech=ech, ps=ps, col0=col0, n=n):
                last = None
                for kb in range(8):
                    last = e.matmul(out=ps[:, 0:n], lhsT=WinT[:, kb, ech * 128:(ech + 1) * 128],
                                    rhs=lnT[:, kb, col0:col0 + n], start=(kb == 0), stop=(kb == 7))
                return last
            p.op("tensor", f, reads=[WinT, lnT], writes=[ps])
            yield i, ech, ps

    out_sems = []

    for b in range(B):
        phase1_b(b)
        for d_ in "fb":
            wprep_dir(d_)
            wprep_win(d_, 0)
            fwd = (d_ == "f")
            chunk_order = range(NCH) if fwd else range(NCH - 1, -1, -1)
            first = True
            for ci in chunk_order:
                c0 = ci * CHUNK
                if fwd:
                    if first:
                        for i, ech, ps in inproj_cols(b, c0, H, range(16), xsT, 0, lambda i: i % 2):
                            dsl = xsT[:, ech, 0:H]
                            p.op("vector", lambda e, dsl=dsl, ps=ps: e.tensor_copy(out=dsl, in_=ps[:, 0:H]),
                                 reads=[ps], writes=[(xsT, ("h", ech))])
                    else:
                        p.op("vector", lambda e: e.tensor_copy(out=xsT[:, :, 0:H],
                                                               in_=xsT[:, :, CHUNK:CHUNK + H]),
                             reads=[xsT], writes=[xsT])
                    body_off = H
                else:
                    if first:
                        for i, ech, ps in inproj_cols(b, c0 + CHUNK + H, H, range(16), xsT, 0, lambda i: i % 2):
                            dsl = xsT[:, ech, CHUNK:CHUNK + H]
                            p.op("vector", lambda e, dsl=dsl, ps=ps: e.tensor_copy(out=dsl, in_=ps[:, 0:H]),
                                 reads=[ps], writes=[(xsT, ("h", ech))])
                    else:
                        p.op("vector", lambda e: e.tensor_copy(out=xsT[:, :, CHUNK:CHUNK + H],
                                                               in_=xsT[:, :, 0:H]),
                             reads=[xsT], writes=[xsT])
                    body_off = 0
                first = False
                for i, ech, ps in inproj_cols(b, H + c0, CHUNK, range(16), xsT, body_off, lambda i: i % 2):
                    dsl = xsT[:, ech, body_off:body_off + CHUNK]
                    p.op("vector", lambda e, dsl=dsl, ps=ps: e.tensor_copy(out=dsl, in_=ps[:]),
                         reads=[ps], writes=[(xsT, ("b", ech))])
                bufs = [conv_a, conv_b2, conv_c, conv_d2]
                for kb in range(16):
                    ks = [0, 1, 2, 3] if fwd else [3, 2, 1, 0]
                    p.op("vector", lambda e, kb=kb, ks=ks, b0=bufs[0]: e.tensor_scalar(
                        out=b0[:], in0=xsT[:, kb, ks[0]:ks[0] + CHUNK],
                        scalar1=cwcol[:, kb, 0:1], scalar2=None, op0=ALU.mult),
                         reads=[xsT, cwcol], writes=[bufs[0]])
                    for j in range(1, 4):
                        p.op("vector", lambda e, kb=kb, j=j, ks=ks, bj=bufs[j], bp=bufs[j - 1]:
                             e.scalar_tensor_tensor(
                                 out=bj[:], in0=xsT[:, kb, ks[j]:ks[j] + CHUNK],
                                 scalar=cwcol[:, kb, j:j + 1], in1=bp[:],
                                 op0=ALU.mult, op1=ALU.add),
                             reads=[xsT, cwcol, bufs[j - 1]], writes=[bufs[j]])
                    p.op("scalar", lambda e, kb=kb: e.activation(
                        out=uT[:, kb, :], in_=conv_d2[:], func=AF.Silu, bias=cbcol[:, kb:kb + 1]),
                         reads=[conv_d2, cbcol], writes=[(uT, kb)])
                    p.dma("gpsimd", uT_o[d_][b, kb * 128:(kb + 1) * 128, c0:c0 + CHUNK], uT[:, kb, :],
                          reads=[(uT, kb)], sem=f"ust{kb}")
                def xp(e):
                    last = None
                    for kb in range(16):
                        last = e.matmul(out=xd_ps[:], lhsT=xprojT[:, kb, :], rhs=uT[:, kb, :],
                                        start=(kb == 0), stop=(kb == 15))
                    return last
                p.op("tensor", xp, reads=[uT, xprojT], writes=[xd_ps])
                p.op("vector", lambda e: e.tensor_copy(out=xdb[:], in_=xd_ps[:]),
                     reads=[xd_ps], writes=[xdb])
                p.op("vector", lambda e: e.tensor_copy(out=bcs[:], in_=xd_ps[64:96, :]),
                     reads=[xd_ps], writes=[bcs])
                p.dma("gpsimd", bcT_o[d_][b, :, c0:c0 + CHUNK], bcs[:], reads=[bcs], sem="bcst")
                for dch in range(16):
                    ps = mm_ps[dch % 2]
                    p.op("tensor", lambda e, dch=dch, ps=ps: e.matmul(
                        out=ps[:], lhsT=dtwT[:, dch * 128:(dch + 1) * 128], rhs=xdb[0:64, :],
                        start=True, stop=True),
                         reads=[dtwT, xdb], writes=[ps])
                    p.op("scalar", lambda e, ps=ps, dch=dch: e.activation(
                        out=sp0[:], in_=ps[:], func=AF.Identity, bias=dtbcol[:, dch:dch + 1]),
                         reads=[ps, dtbcol], writes=[sp0])
                    p.op("scalar", lambda e: e.activation(out=sp1[:], in_=sp0[:], func=AF.Abs),
                         reads=[sp0], writes=[sp1])
                    p.op("scalar", lambda e: e.activation(out=sp2[:], in_=sp1[:], func=AF.Exp, scale=-1.0),
                         reads=[sp1], writes=[sp2])
                    p.op("scalar", lambda e: e.activation(out=sp1[:], in_=sp2[:], func=AF.Ln, bias=1.0),
                         reads=[sp2], writes=[sp1])
                    p.op("vector", lambda e: e.tensor_scalar(out=sp2[:], in0=sp0[:], scalar1=0.0,
                                                             scalar2=None, op0=ALU.max),
                         reads=[sp0], writes=[sp2])
                    p.op("vector", lambda e, dch=dch: e.tensor_add(out=dTc[:, dch, :], in0=sp2[:], in1=sp1[:]),
                         reads=[sp2, sp1], writes=[(dTc, dch)])
                    p.op("vector", lambda e, dch=dch: e.tensor_mul(out=duTc[:, dch, :], in0=dTc[:, dch, :],
                                                                   in1=uT[:, dch, :]),
                         reads=[(dTc, dch), (uT, dch)], writes=[(duTc, dch)])
                    p.dma("gpsimd", dT_o[d_][b, dch * 128:(dch + 1) * 128, c0:c0 + CHUNK], dTc[:, dch, :],
                          reads=[(dTc, dch)], sem=f"dst{dch}")
                    p.dma("gpsimd", duT_o[d_][b, dch * 128:(dch + 1) * 128, c0:c0 + CHUNK], duTc[:, dch, :],
                          reads=[(duTc, dch)], sem=f"dust{dch}")
            wprep_win(d_, 1)
            for ci in range(NCH):
                c0 = ci * CHUNK
                for i, ech, ps in inproj_cols(b, H + c0, CHUNK, range(16), None, 0, lambda i: 2 + i % 2):
                    zv = zev[i % 2]
                    p.op("vector", lambda e, zv=zv, ps=ps: e.tensor_copy(out=zv[:], in_=ps[:]),
                         reads=[ps], writes=[zv])
                    p.dma("gpsimd", zT_o[d_][b, ech * 128:(ech + 1) * 128, c0:c0 + CHUNK], zv[:],
                          reads=[zv], sem=f"zst{i % 2}")
    p.finalize()
    return p


def ln_stats_n(p, x_tile, sq_scratch, stat, ntok, n_feat=D, eps=1e-5):
    v, s = "vector", "scalar"
    p.op(v, lambda e: e.reduce_sum(out=stat["sum"][0:ntok, :], in_=x_tile[0:ntok, :], axis=AX.X),
         reads=[x_tile], writes=[stat["sum"]])
    p.op(s, lambda e: e.activation(out=sq_scratch[0:ntok, :], in_=x_tile[0:ntok, :], func=AF.Square,
                                   accum_out=stat["sumsq"][0:ntok, :]),
         reads=[x_tile], writes=[sq_scratch, stat["sumsq"]])
    p.op(v, lambda e: e.tensor_scalar(out=stat["mean"][0:ntok, :], in0=stat["sum"][0:ntok, :],
                                      scalar1=1.0 / n_feat, scalar2=None, op0=ALU.mult),
         reads=[stat["sum"]], writes=[stat["mean"]])
    p.op(v, lambda e: e.tensor_mul(out=stat["m2"][0:ntok, :], in0=stat["mean"][0:ntok, :],
                                   in1=stat["mean"][0:ntok, :]),
         reads=[stat["mean"]], writes=[stat["m2"]])
    p.op(v, lambda e: e.tensor_scalar(out=stat["nm2"][0:ntok, :], in0=stat["m2"][0:ntok, :], scalar1=-1.0,
                                      scalar2=None, op0=ALU.mult),
         reads=[stat["m2"]], writes=[stat["nm2"]])
    p.op(v, lambda e: e.scalar_tensor_tensor(out=stat["var"][0:ntok, :], in0=stat["sumsq"][0:ntok, :],
                                             scalar=1.0 / n_feat, in1=stat["nm2"][0:ntok, :],
                                             op0=ALU.mult, op1=ALU.add),
         reads=[stat["sumsq"], stat["nm2"]], writes=[stat["var"]])
    p.op(v, lambda e: e.tensor_scalar(out=stat["vare"][0:ntok, :], in0=stat["var"][0:ntok, :], scalar1=eps,
                                      scalar2=None, op0=ALU.add),
         reads=[stat["var"]], writes=[stat["vare"]])
    p.op(s, lambda e: e.activation(out=stat["std"][0:ntok, :], in_=stat["vare"][0:ntok, :], func=AF.Sqrt),
         reads=[stat["vare"]], writes=[stat["std"]])
    p.op(v, lambda e: e.reciprocal(out=stat["rstd"][0:ntok, :], in_=stat["std"][0:ntok, :]),
         reads=[stat["std"]], writes=[stat["rstd"]])
    p.op(v, lambda e: e.tensor_mul(out=stat["mr"][0:ntok, :], in0=stat["mean"][0:ntok, :],
                                   in1=stat["rstd"][0:ntok, :]),
         reads=[stat["mean"], stat["rstd"]], writes=[stat["mr"]])
    p.op(v, lambda e: e.tensor_scalar(out=stat["negmur"][0:ntok, :], in0=stat["mr"][0:ntok, :], scalar1=-1.0,
                                      scalar2=None, op0=ALU.mult),
         reads=[stat["mr"]], writes=[stat["negmur"]])



SCH = 512

def build_scan(T=S, tag="scan"):
    p = Prog(tag)
    NCHS = T // SCH
    dT_i = p.dram("dT", [1024, T], BF16, kind="ExternalInput").ap()
    duT_i = p.dram("duT", [1024, T], BF16, kind="ExternalInput").ap()
    bcT_i = p.dram("bcT", [32, T], BF16, kind="ExternalInput").ap()
    alog_i = p.dram("alog", [8, 128, 16], F32, kind="ExternalInput").ap()
    sel_i = p.dram("sel", [32, 32, 128], BF16, kind="ExternalInput").ap()
    yT_o = p.dram("yT", [1024, T], BF16, kind="ExternalOutput").ap()

    SEL = p.sbuf("SEL", [32, 32, 128], BF16)
    p.dma("sync", SEL[:], sel_i, writes=[SEL])
    acols = p.sbuf("acols", [128, 8, 16], F32)
    atmp = p.sbuf("atmp", [128, 16], F32)
    atmp2 = p.sbuf("atmp2", [128, 16], F32)
    for db in range(8):
        p.dma("sync", atmp[:], alog_i[db], writes=[atmp])
        p.op("scalar", lambda e: e.activation(out=atmp2[:], in_=atmp[:], func=AF.Exp),
             reads=[atmp], writes=[atmp2])
        p.op("vector", lambda e, db=db: e.tensor_scalar(out=acols[:, db, :], in0=atmp2[:], scalar1=-1.0,
                                                        scalar2=None, op0=ALU.mult),
             reads=[atmp2], writes=[(acols, db)])

    bcc = p.sbuf("bcc", [32, SCH], BF16)
    Bb = p.sbuf("Bb", [128, 16, SCH], BF16)
    Cb = p.sbuf("Cb", [128, 16, SCH], BF16)
    bc_ps = [p.psum(f"bc_ps{i}", [128, SCH], F32) for i in range(2)]
    dA = [p.sbuf(f"dA{i}", [128, 16, SCH], F32) for i in range(2)]
    dBu = [p.sbuf(f"dBu{i}", [128, 16, SCH], BF16) for i in range(2)]
    hb = p.sbuf("hb", [128, 16, SCH], BF16)
    ta = p.sbuf("ta", [128, 8, SCH], BF16)
    tb = p.sbuf("tb", [128, 4, SCH], BF16)
    tc = p.sbuf("tc", [128, 2, SCH], BF16)
    yv = p.sbuf("yv", [128, SCH], BF16)
    dTt = [p.sbuf(f"dTt{i}", [128, SCH], BF16) for i in range(2)]
    duTt = [p.sbuf(f"duTt{i}", [128, SCH], BF16) for i in range(2)]
    carry = p.sbuf("carry", [128, 8, 16], F32)
    p.op("vector", lambda e: e.memset(carry[:], 0.0), writes=[carry])

    for ci in range(NCHS):
        c0 = ci * SCH
        p.dma("sync", bcc[:], bcT_i[:, c0:c0 + SCH], writes=[bcc])
        for s in range(16):
            ps = bc_ps[s % 2]
            p.op("tensor", lambda e, s=s, ps=ps: e.matmul(out=ps[:], lhsT=SEL[:, s, :], rhs=bcc[:],
                                                          start=True, stop=True),
                 reads=[SEL, bcc], writes=[ps])
            p.op("scalar", lambda e, s=s, ps=ps: e.activation(out=Bb[:, s, :], in_=ps[:], func=AF.Copy),
                 reads=[ps], writes=[(Bb, s)])
            ps2 = bc_ps[(s + 1) % 2]
            p.op("tensor", lambda e, s=s, ps2=ps2: e.matmul(out=ps2[:], lhsT=SEL[:, 16 + s, :], rhs=bcc[:],
                                                            start=True, stop=True),
                 reads=[SEL, bcc], writes=[ps2])
            p.op("scalar", lambda e, s=s, ps2=ps2: e.activation(out=Cb[:, s, :], in_=ps2[:], func=AF.Copy),
                 reads=[ps2], writes=[(Cb, s)])
        for db in range(8):
            par = db % 2
            dt_, du_ = dTt[par], duTt[par]
            p.dma("sync", dt_[:], dT_i[db * 128:(db + 1) * 128, c0:c0 + SCH], writes=[dt_])
            p.dma("sync", du_[:], duT_i[db * 128:(db + 1) * 128, c0:c0 + SCH], writes=[du_])
            DA, DB = dA[par], dBu[par]
            for s in range(16):
                p.op("scalar", lambda e, s=s, db=db, DA=DA, dt_=dt_: e.activation(
                    out=DA[:, s, :], in_=dt_[:], func=AF.Exp, scale=acols[:, db, s:s + 1]),
                     reads=[dt_, (acols, db)], writes=[(DA, s)])
                p.op("vector", lambda e, s=s, DB=DB, du_=du_: e.tensor_mul(
                    out=DB[:, s, :], in0=du_[:], in1=Bb[:, s, :]),
                     reads=[du_, (Bb, s)], writes=[(DB, s)])
            for s in range(16):
                p.op("vector", lambda e, s=s, db=db, DA=DA, DB=DB: e.tensor_tensor_scan(
                    out=hb[:, s, :], data0=DA[:, s, :], data1=DB[:, s, :],
                    initial=carry[:, db, s:s + 1], op0=ALU.mult, op1=ALU.add),
                     reads=[(DA, s), (DB, s), (carry, db)], writes=[(hb, s)])
            p.op("vector", lambda e, db=db: e.tensor_copy(
                out=carry[:, db, :], in_=bass.AP(hb[:].tensor, hb[:].offset + SCH - 1,
                                                 [[hb[:].ap[0][0], 128], [SCH, 16]])),
                 reads=[hb], writes=[(carry, db)])
            p.op("vector", lambda e, DB=DB: e.tensor_mul(out=DB[:], in0=hb[:], in1=Cb[:]),
                 reads=[hb, Cb], writes=[DB])
            p.op("vector", lambda e, DB=DB: e.tensor_add(out=ta[:], in0=DB[:, 0:8, :], in1=DB[:, 8:16, :]),
                 reads=[DB], writes=[ta])
            p.op("vector", lambda e: e.tensor_add(out=tb[:], in0=ta[:, 0:4, :], in1=ta[:, 4:8, :]),
                 reads=[ta], writes=[tb])
            p.op("vector", lambda e: e.tensor_add(out=tc[:], in0=tb[:, 0:2, :], in1=tb[:, 2:4, :]),
                 reads=[tb], writes=[tc])
            p.op("vector", lambda e: e.tensor_add(out=yv[:], in0=tc[:, 0, :], in1=tc[:, 1, :]),
                 reads=[tc], writes=[yv])
            p.dma("gpsimd", yT_o[db * 128:(db + 1) * 128, c0:c0 + SCH], yv[:], reads=[yv], sem=f"yst{par}")
    p.finalize()
    return p



def build_mamba_back(tag="mback"):
    p = Prog(tag)
    NCH = TLOC // CHUNK
    x_in = p.dram("x", [B, TLOC, D], F32, kind="ExternalInput").ap()
    yT_i = {d_: p.dram(f"yT_{d_}", [B, 2048, TLOC], BF16, kind="ExternalInput").ap() for d_ in "fb"}
    uT_i = {d_: p.dram(f"uT_{d_}", [B, 2048, TLOC], BF16, kind="ExternalInput").ap() for d_ in "fb"}
    zT_i = {d_: p.dram(f"zT_{d_}", [B, 2048, TLOC], BF16, kind="ExternalInput").ap() for d_ in "fb"}
    Dv = {d_: p.dram(f"Dv_{d_}", [128, 16], F32, kind="ExternalInput").ap() for d_ in "fb"}
    outw = {d_: p.dram(f"outw_{d_}", [1024, 2048], F32, kind="ExternalInput").ap() for d_ in "fb"}
    fw = p.dram("fw", [1024, 2048], F32, kind="ExternalInput").ap()
    fb_ = p.dram("fb", [1, 1024], F32, kind="ExternalInput").ap()
    ident_in = p.dram("ident_in", [128, 128], F32, kind="ExternalInput").ap()
    x_out = p.dram("xo", [B, TLOC, D], F32, kind="ExternalOutput").ap()

    ident = p.sbuf("ident", [128, 128], F32)
    p.dma("sync", ident[:], ident_in, writes=[ident])
    ones_row = p.sbuf("ones_row", [1, 128], BF16)
    p.op("vector", lambda e: e.memset(ones_row[:], 1.0), writes=[ones_row])
    fbS = p.sbuf("fbS", [1, 1024], BF16)
    fbf = p.sbuf("fbf", [1, 1024], F32)
    p.dma("sync", fbf[:], fb_, writes=[fbf])
    p.op("vector", lambda e: e.tensor_copy(out=fbS[:], in_=fbf[:]), reads=[fbf], writes=[fbS])
    Dcol = {}
    for d_ in "fb":
        Dcol[d_] = p.sbuf(f"Dcol{d_}", [128, 16], F32)
        p.dma("sync", Dcol[d_][:], Dv[d_], writes=[Dcol[d_]])

    owT = {d_: p.sbuf(f"owT{d_}", [128, 16, 1024], BF16) for d_ in "fb"}
    fwT = p.sbuf("fwT", [128, 16, 1024], BF16)
    wld = p.sbuf("wld", [128, 2048], F32)
    tr_ps = p.psum("tr_ps", [128, 128], F32)
    def prep_T(dst, W, nrow_blocks, ncol_blocks):
        for rb in range(nrow_blocks):
            p.dma("sync", wld[:, 0:ncol_blocks * 128], W[rb * 128:(rb + 1) * 128, :], writes=[wld])
            for kb in range(ncol_blocks):
                sl = wld[:, kb * 128:(kb + 1) * 128]
                p.op("tensor", lambda e, sl=sl: e.transpose(out=tr_ps[:], in_=sl, identity=ident[:]),
                     reads=[wld, ident], writes=[tr_ps])
                dsl = dst[:, kb, rb * 128:(rb + 1) * 128]
                p.op("scalar", lambda e, dsl=dsl: e.activation(out=dsl, in_=tr_ps[:], func=AF.Copy),
                     reads=[tr_ps], writes=[(dst, (kb, rb))])
    for d_ in "fb":
        prep_T(owT[d_], outw[d_], 8, 16)
    prep_T(fwT, fw, 8, 16)

    yt = p.sbuf("yt", [128, CHUNK], BF16)
    ut = p.sbuf("ut", [128, CHUNK], BF16)
    zt = p.sbuf("zt", [128, CHUNK], BF16)
    sz = p.sbuf("sz", [128, CHUNK], F32)
    g1 = p.sbuf("g1", [128, CHUNK], F32)
    yg = p.sbuf("yg", [128, 16, CHUNK], BF16)
    oT = p.sbuf("oT", [128, 16, CHUNK], BF16)
    xr = [p.sbuf(f"xr{i}", [128, D], F32) for i in range(2)]
    xo_t = [p.sbuf(f"xo_t{i}", [128, D], F32) for i in range(2)]
    mm_ps = [p.psum(f"mm_ps{i}", [128, 512], F32) for i in range(4)]

    for b in range(B):
        for ci in range(NCH):
            c0 = ci * CHUNK
            for di, d_ in enumerate("fb"):
                for kb in range(16):
                    p.dma("sync", yt[:], yT_i[d_][b, kb * 128:(kb + 1) * 128, c0:c0 + CHUNK], writes=[yt])
                    p.dma("sync", ut[:], uT_i[d_][b, kb * 128:(kb + 1) * 128, c0:c0 + CHUNK], writes=[ut])
                    p.dma("sync", zt[:], zT_i[d_][b, kb * 128:(kb + 1) * 128, c0:c0 + CHUNK], writes=[zt])
                    p.op("scalar", lambda e: e.activation(out=sz[:], in_=zt[:], func=AF.Silu),
                         reads=[zt], writes=[sz])
                    p.op("vector", lambda e, kb=kb, d_=d_: e.scalar_tensor_tensor(
                        out=g1[:], in0=ut[:], scalar=Dcol[d_][:, kb:kb + 1], in1=yt[:],
                        op0=ALU.mult, op1=ALU.add),
                         reads=[ut, Dcol[d_], yt], writes=[g1])
                    p.op("vector", lambda e, kb=kb: e.tensor_mul(out=yg[:, kb, :], in0=g1[:], in1=sz[:]),
                         reads=[g1, sz], writes=[(yg, kb)])
                for och in range(8):
                    ps = mm_ps[och % 2]
                    def f(e, och=och, d_=d_, ps=ps):
                        last = None
                        for kb in range(16):
                            last = e.matmul(out=ps[:], lhsT=owT[d_][:, kb, och * 128:(och + 1) * 128],
                                            rhs=yg[:, kb, :], start=(kb == 0), stop=(kb == 15))
                        return last
                    p.op("tensor", f, reads=[yg, owT[d_]], writes=[ps])
                    p.op("scalar", lambda e, och=och, di=di, ps=ps: e.activation(
                        out=oT[:, di * 8 + och, :], in_=ps[:], func=AF.Copy),
                         reads=[ps], writes=[(oT, di * 8 + och)])
            for tt in range(CHUNK // 128):
                X = xr[tt % 2]
                XO = xo_t[tt % 2]
                p.dma("sync", X[:], x_in[b, c0 + tt * 128:c0 + (tt + 1) * 128, :], writes=[X])
                for oc in range(2):
                    ps = mm_ps[2 + oc]
                    def f2(e, tt=tt, oc=oc, ps=ps):
                        e.matmul(out=ps[:], lhsT=ones_row[:], rhs=fbS[:, oc * 512:(oc + 1) * 512],
                                 start=True, stop=False)
                        last = None
                        for kb in range(16):
                            last = e.matmul(out=ps[:], lhsT=oT[:, kb, tt * 128:(tt + 1) * 128],
                                            rhs=fwT[:, kb, oc * 512:(oc + 1) * 512],
                                            start=False, stop=(kb == 15))
                        return last
                    p.op("tensor", f2, reads=[oT, fwT, ones_row, fbS], writes=[ps])
                    p.op("vector", lambda e, oc=oc, ps=ps, X=X, XO=XO: e.tensor_add(
                        out=XO[:, oc * 512:(oc + 1) * 512], in0=ps[:], in1=X[:, oc * 512:(oc + 1) * 512]),
                         reads=[ps, X], writes=[(XO, oc)])
                p.dma("gpsimd", x_out[b, c0 + tt * 128:c0 + (tt + 1) * 128, :], XO[:],
                      reads=[XO], sem=f"xst{tt % 2}")
    p.finalize()
    return p



def build_l1ffn(tag="l1ffn"):
    p = Prog(tag)
    NCH = TLOC // CHUNK
    x_in = p.dram("x", [B, TLOC + 1, D], F32, kind="ExternalInput").ap()
    Wr_i = p.dram("Wr", [D, D], F32, kind="ExternalInput").ap()
    Wk_i = p.dram("Wk", [D, D], F32, kind="ExternalInput").ap()
    Wv_i = p.dram("Wv", [D, D], F32, kind="ExternalInput").ap()
    mur_i = p.dram("mur", [128, 8], F32, kind="ExternalInput").ap()
    muk_i = p.dram("muk", [128, 8], F32, kind="ExternalInput").ap()
    ident_in = p.dram("ident_in", [128, 128], F32, kind="ExternalInput").ap()
    x_out = p.dram("xo", [B, TLOC, D], F32, kind="ExternalOutput").ap()

    ident = p.sbuf("ident", [128, 128], F32)
    identb = p.sbuf("identb", [128, 128], BF16)
    p.dma("sync", ident[:], ident_in, writes=[ident])
    p.op("vector", lambda e: e.tensor_copy(out=identb[:], in_=ident[:]), reads=[ident], writes=[identb])
    mur = p.sbuf("mur_s", [128, 8], F32)
    muk = p.sbuf("muk_s", [128, 8], F32)
    murm = p.sbuf("murm", [128, 8], F32)
    mukm = p.sbuf("mukm", [128, 8], F32)
    p.dma("sync", mur[:], mur_i, writes=[mur])
    p.dma("sync", muk[:], muk_i, writes=[muk])
    for src_, dst_ in ((mur, murm), (muk, mukm)):
        p.op("vector", lambda e, src_=src_, dst_=dst_: e.tensor_scalar(
            out=dst_[:], in0=src_[:], scalar1=-1.0, scalar2=1.0, op0=ALU.mult, op1=ALU.add),
             reads=[src_], writes=[dst_])

    WT = {n: p.sbuf(f"WT{n}", [128, 8, D], BF16) for n in "rkv"}
    wld = p.sbuf("wld", [128, D], F32)
    tr_ps = p.psum("tr_ps", [128, 128], F32)
    for n, W in (("r", Wr_i), ("k", Wk_i), ("v", Wv_i)):
        for rb in range(8):
            p.dma("sync", wld[:], W[rb * 128:(rb + 1) * 128, :], writes=[wld])
            for kb in range(8):
                sl = wld[:, kb * 128:(kb + 1) * 128]
                p.op("tensor", lambda e, sl=sl: e.transpose(out=tr_ps[:], in_=sl, identity=ident[:]),
                     reads=[wld, ident], writes=[tr_ps])
                dsl = WT[n][:, kb, rb * 128:(rb + 1) * 128]
                p.op("scalar", lambda e, dsl=dsl: e.activation(out=dsl, in_=tr_ps[:], func=AF.Copy),
                     reads=[tr_ps], writes=[(WT[n], (kb, rb))])

    xt = [p.sbuf(f"xt{i}", [128, D], F32) for i in range(2)]
    sq = [p.sbuf(f"sq{i}", [128, D], F32) for i in range(2)]
    lnb = [p.sbuf(f"lnb{i}", [128, D], BF16) for i in range(2)]
    stats = [make_stat(p, f"st{i}") for i in range(2)]
    trb_ps = [p.psum(f"trb_ps{i}", [128, 128], BF16) for i in range(2)]
    lnT = p.sbuf("lnT", [128, 8, CHUNK + 1], BF16)
    xrT = p.sbuf("xrT", [128, 8, CHUNK], BF16)
    xkT = p.sbuf("xkT", [128, 8, CHUNK], BF16)
    t1 = p.sbuf("t1", [128, CHUNK], BF16)
    kaT = p.sbuf("kaT", [128, 8, CHUNK], BF16)
    krel = p.sbuf("krel", [128, CHUNK], F32)
    rS = p.sbuf("rS", [128, 1024], F32)
    xc = [p.sbuf(f"xc{i}", [128, D], F32) for i in range(2)]
    vo = p.sbuf("vo", [128, D], F32)
    xo_t = [p.sbuf(f"xo_t{i}", [128, D], F32) for i in range(2)]
    mm_ps = [p.psum(f"mm_ps{i}", [128, 512], F32) for i in range(4)]

    for b in range(B):
        for ci in range(NCH):
            c0 = ci * CHUNK
            if ci == 0:
                X = xt[0]
                p.dma("sync", X[0:1, :], x_in[b, 0:1, :], writes=[X])
                st0 = stats[0]
                ln_stats_n(p, X, sq[0], st0, 1)
                p.op("scalar", lambda e, X=X, st0=st0: e.activation(
                    out=lnb[0][0:1, :], in_=X[0:1, :], func=AF.Identity,
                    bias=st0["negmur"][0:1, 0:1], scale=st0["rstd"][0:1, 0:1]),
                     reads=[X, st0["negmur"], st0["rstd"]], writes=[lnb[0]])
                for kb in range(8):
                    tp = trb_ps[kb % 2]
                    p.op("tensor", lambda e, kb=kb, tp=tp: e.transpose(
                        out=tp[:, 0:1], in_=lnb[0][0:1, kb * 128:(kb + 1) * 128], identity=identb[0:1, 0:1]),
                         reads=[lnb[0], identb], writes=[tp])
                    p.op("scalar", lambda e, kb=kb, tp=tp: e.activation(
                        out=lnT[:, kb, 0:1], in_=tp[:, 0:1], func=AF.Copy),
                         reads=[tp], writes=[(lnT, ("h", kb))])
            else:
                p.op("vector", lambda e: e.tensor_copy(out=lnT[:, :, 0:1], in_=lnT[:, :, CHUNK:CHUNK + 1]),
                     reads=[lnT], writes=[lnT])
            for tt in range(CHUNK // 128):
                par = tt % 2
                X, SQ, LNB, st = xt[par], sq[par], lnb[par], stats[par]
                p.dma("sync", X[:], x_in[b, 1 + c0 + tt * 128:1 + c0 + (tt + 1) * 128, :], writes=[X])
                ln_stats_n(p, X, SQ, st, 128)
                p.op("scalar", lambda e, X=X, LNB=LNB, st=st: e.activation(
                    out=LNB[:], in_=X[:], func=AF.Identity,
                    bias=st["negmur"][:, 0:1], scale=st["rstd"][:, 0:1]),
                     reads=[X, st["negmur"], st["rstd"]], writes=[LNB])
                for kb in range(8):
                    tp = trb_ps[kb % 2]
                    p.op("tensor", lambda e, kb=kb, tp=tp, LNB=LNB: e.transpose(
                        out=tp[:], in_=LNB[:, kb * 128:(kb + 1) * 128], identity=identb[:]),
                         reads=[LNB, identb], writes=[tp])
                    p.op("scalar", lambda e, kb=kb, tp=tp, tt=tt: e.activation(
                        out=lnT[:, kb, 1 + tt * 128:1 + (tt + 1) * 128], in_=tp[:], func=AF.Copy),
                         reads=[tp], writes=[(lnT, (kb, tt))])
            for kb in range(8):
                p.op("vector", lambda e, kb=kb: e.tensor_scalar(
                    out=t1[:], in0=lnT[:, kb, 1:1 + CHUNK], scalar1=mur[:, kb:kb + 1],
                    scalar2=None, op0=ALU.mult),
                     reads=[lnT, mur], writes=[t1])
                p.op("vector", lambda e, kb=kb: e.scalar_tensor_tensor(
                    out=xrT[:, kb, :], in0=lnT[:, kb, 0:CHUNK], scalar=murm[:, kb:kb + 1],
                    in1=t1[:], op0=ALU.mult, op1=ALU.add),
                     reads=[lnT, murm, t1], writes=[(xrT, kb)])
                p.op("vector", lambda e, kb=kb: e.tensor_scalar(
                    out=t1[:], in0=lnT[:, kb, 1:1 + CHUNK], scalar1=muk[:, kb:kb + 1],
                    scalar2=None, op0=ALU.mult),
                     reads=[lnT, muk], writes=[t1])
                p.op("vector", lambda e, kb=kb: e.scalar_tensor_tensor(
                    out=xkT[:, kb, :], in0=lnT[:, kb, 0:CHUNK], scalar=mukm[:, kb:kb + 1],
                    in1=t1[:], op0=ALU.mult, op1=ALU.add),
                     reads=[lnT, mukm, t1], writes=[(xkT, kb)])
            for och in range(8):
                ps = mm_ps[och % 2]
                def fk(e, och=och, ps=ps):
                    last = None
                    for kb in range(8):
                        last = e.matmul(out=ps[:], lhsT=WT["k"][:, kb, och * 128:(och + 1) * 128],
                                        rhs=xkT[:, kb, :], start=(kb == 0), stop=(kb == 7))
                    return last
                p.op("tensor", fk, reads=[xkT, WT["k"]], writes=[ps])
                p.op("scalar", lambda e, ps=ps: e.activation(out=krel[:], in_=ps[:], func=AF.Relu),
                     reads=[ps], writes=[krel])
                p.op("scalar", lambda e, och=och: e.activation(out=kaT[:, och, :], in_=krel[:], func=AF.Square),
                     reads=[krel], writes=[(kaT, och)])
            for tt in range(CHUNK // 128):
                for oc in range(2):
                    ps = mm_ps[oc]
                    def fr(e, tt=tt, oc=oc, ps=ps):
                        last = None
                        for kb in range(8):
                            last = e.matmul(out=ps[:], lhsT=xrT[:, kb, tt * 128:(tt + 1) * 128],
                                            rhs=WT["r"][:, kb, oc * 512:(oc + 1) * 512],
                                            start=(kb == 0), stop=(kb == 7))
                        return last
                    p.op("tensor", fr, reads=[xrT, WT["r"]], writes=[ps])
                    p.op("scalar", lambda e, oc=oc, ps=ps: e.activation(
                        out=rS[:, oc * 512:(oc + 1) * 512], in_=ps[:], func=AF.Sigmoid),
                         reads=[ps], writes=[(rS, oc)])
                for oc in range(2):
                    ps = mm_ps[2 + oc]
                    def fv(e, tt=tt, oc=oc, ps=ps):
                        last = None
                        for kb in range(8):
                            last = e.matmul(out=ps[:], lhsT=kaT[:, kb, tt * 128:(tt + 1) * 128],
                                            rhs=WT["v"][:, kb, oc * 512:(oc + 1) * 512],
                                            start=(kb == 0), stop=(kb == 7))
                        return last
                    p.op("tensor", fv, reads=[kaT, WT["v"]], writes=[ps])
                    p.op("vector", lambda e, oc=oc, ps=ps: e.tensor_mul(
                        out=vo[:, oc * 512:(oc + 1) * 512],
                        in0=rS[:, oc * 512:(oc + 1) * 512], in1=ps[:]),
                         reads=[(rS, oc), ps], writes=[(vo, oc)])
                XO = xo_t[tt % 2]
                X2 = xc[tt % 2]
                p.dma("sync", X2[:], x_in[b, 1 + c0 + tt * 128:1 + c0 + (tt + 1) * 128, :], writes=[X2])
                p.op("vector", lambda e, XO=XO, X2=X2: e.scalar_tensor_tensor(
                    out=XO[:], in0=vo[:], scalar=0.5, in1=X2[:], op0=ALU.mult, op1=ALU.add),
                     reads=[vo, X2], writes=[XO])
                p.dma("gpsimd", x_out[b, c0 + tt * 128:c0 + (tt + 1) * 128, :], XO[:],
                      reads=[XO], sem=f"xst{tt % 2}")
    p.finalize()
    return p



def build_l1attn_front(tag="l1af"):
    p = Prog(tag)
    H = 3
    NCH = TLOC // CHUNK
    x_in = p.dram("x", [B, TLOC + H, D], F32, kind="ExternalInput").ap()
    Wq_i = p.dram("Wq", [D, D], F32, kind="ExternalInput").ap()
    Wk_i = p.dram("Wk", [D, D], F32, kind="ExternalInput").ap()
    Wv_i = p.dram("Wv", [D, D], F32, kind="ExternalInput").ap()
    bqkv_i = p.dram("bqkv", [128, 3, 8], F32, kind="ExternalInput").ap()
    cw_i = p.dram("cw", [4, 1024, 256], F32, kind="ExternalInput").ap()
    cb_i = p.dram("cb", [1, 1024], F32, kind="ExternalInput").ap()
    hmask_i = p.dram("hmask", [128, 1], F32, kind="ExternalInput").ap()
    lncols_i = p.dram("lncols", [128, 6, 8], F32, kind="ExternalInput").ap()
    ident_in = p.dram("ident_in", [128, 128], F32, kind="ExternalInput").ap()
    g1_o = p.dram("g1", [B, TLOC, D], F32, kind="ExternalOutput").ap()
    qT_o = p.dram("qT", [B, D, TLOC], BF16, kind="ExternalOutput").ap()
    kT_o = p.dram("kT", [B, D, TLOC], BF16, kind="ExternalOutput").ap()
    vT_o = p.dram("vT", [B, D, TLOC], BF16, kind="ExternalOutput").ap()

    ident = p.sbuf("ident", [128, 128], F32)
    identb = p.sbuf("identb", [128, 128], BF16)
    p.dma("sync", ident[:], ident_in, writes=[ident])
    p.op("vector", lambda e: e.tensor_copy(out=identb[:], in_=ident[:]), reads=[ident], writes=[identb])
    bqkv = p.sbuf("bqkv_s", [128, 3, 8], F32)
    p.dma("sync", bqkv[:], bqkv_i, writes=[bqkv])
    lncols = p.sbuf("lncols_s", [128, 6, 8], F32)
    p.dma("sync", lncols[:], lncols_i, writes=[lncols])
    cbS = p.sbuf("cbS", [1, 1024], BF16)
    cbf = p.sbuf("cbf", [1, 1024], F32)
    p.dma("sync", cbf[:], cb_i, writes=[cbf])
    p.op("vector", lambda e: e.tensor_copy(out=cbS[:], in_=cbf[:]), reads=[cbf], writes=[cbS])
    ones_row = p.sbuf("ones_row", [1, 128], BF16)
    p.op("vector", lambda e: e.memset(ones_row[:], 1.0), writes=[ones_row])

    WT = {n: p.sbuf(f"WT{n}", [128, 8, D], BF16) for n in "qkv"}
    wld = p.sbuf("wld", [128, D], F32)
    tr_ps = p.psum("tr_ps", [128, 128], F32)
    for n, W in (("q", Wq_i), ("k", Wk_i), ("v", Wv_i)):
        for rb in range(8):
            p.dma("sync", wld[:], W[rb * 128:(rb + 1) * 128, :], writes=[wld])
            for kb in range(8):
                sl = wld[:, kb * 128:(kb + 1) * 128]
                p.op("tensor", lambda e, sl=sl: e.transpose(out=tr_ps[:], in_=sl, identity=ident[:]),
                     reads=[wld, ident], writes=[tr_ps])
                dsl = WT[n][:, kb, rb * 128:(rb + 1) * 128]
                p.op("scalar", lambda e, dsl=dsl: e.activation(out=dsl, in_=tr_ps[:], func=AF.Copy),
                     reads=[tr_ps], writes=[(WT[n], (kb, rb))])
    WcT = p.sbuf("WcT", [128, 4, 4, 2, 256], BF16)
    for g in range(4):
        for ob in range(2):
            for k_ in range(4):
                p.dma("sync", wld[:, 0:256], cw_i[k_, g * 256 + ob * 128:g * 256 + (ob + 1) * 128, :],
                      writes=[wld], sem="wld_c")
                for cb in range(2):
                    sl = wld[:, cb * 128:(cb + 1) * 128]
                    p.op("tensor", lambda e, sl=sl: e.transpose(out=tr_ps[:], in_=sl, identity=ident[:]),
                         reads=[wld, ident], writes=[tr_ps])
                    dsl = WcT[:, g, k_, cb, ob * 128:(ob + 1) * 128]
                    p.op("scalar", lambda e, dsl=dsl: e.activation(out=dsl, in_=tr_ps[:], func=AF.Copy),
                         reads=[tr_ps], writes=[(WcT, (g, k_, cb, ob))])

    hmask = p.sbuf("hmask_s", [128, 1], F32)
    p.dma("sync", hmask[:], hmask_i, writes=[hmask])
    hhalo = p.sbuf("hhalo", [128, 8, 4], BF16)
    xt = [p.sbuf(f"xt{i}", [128, D], F32) for i in range(2)]
    sq = [p.sbuf(f"sq{i}", [128, D], F32) for i in range(2)]
    lnb = [p.sbuf(f"lnb{i}", [128, D], BF16) for i in range(2)]
    stats = [make_stat(p, f"st{i}") for i in range(2)]
    trb_ps = [p.psum(f"trb_ps{i}", [128, 128], BF16) for i in range(2)]
    hcT = p.sbuf("hcT", [128, 8, CHUNK + H], BF16)
    haT = p.sbuf("haT", [128, 8, CHUNK], BF16)
    g1t = [p.sbuf(f"g1t{i}", [128, D], F32) for i in range(2)]
    qkv_ev = p.sbuf("qkv_ev", [128, CHUNK], BF16)
    cps = [p.psum(f"cps{i}", [128, 512], F32) for i in range(2)]
    mm_ps = [p.psum(f"mm_ps{i}", [128, 512], F32) for i in range(2)]

    g0buf = [p.sbuf(f"g0buf{i}", [128, D], F32) for i in range(2)]
    g0keep = p.sbuf("g0keep", [128, CHUNK // 128, D], F32)

    def ln_tr(b, tok0, ntok, col0, dst, scale_idx, bias_idx, x_from=None, par=0, keep_tt=None):
        X, SQ, LNB, st = xt[par], sq[par], lnb[par], stats[par]
        if x_from is None:
            p.dma("sync", X[0:ntok, :], x_in[b, tok0:tok0 + ntok, :], writes=[X])
            G0 = g0buf[par]
            ln_stats_n(p, X, SQ, st, ntok)
            p.op("scalar", lambda e, X=X, G0=G0, st=st, ntok=ntok: e.activation(
                out=G0[0:ntok, :], in_=X[0:ntok, :], func=AF.Identity,
                bias=st["negmur"][0:ntok, 0:1], scale=st["rstd"][0:ntok, 0:1]),
                 reads=[X, st["negmur"], st["rstd"]], writes=[G0])
            if keep_tt is not None:
                p.op("vector", lambda e, G0=G0, keep_tt=keep_tt: e.tensor_copy(
                    out=g0keep[0:ntok, keep_tt, :], in_=G0[0:ntok, :]),
                     reads=[G0], writes=[(g0keep, keep_tt)])
            src_t = G0
        else:
            src_t = x_from
        ln_stats_n(p, src_t, SQ, st, ntok)
        p.op("scalar", lambda e, src_t=src_t, LNB=LNB, st=st, ntok=ntok: e.activation(
            out=LNB[0:ntok, :], in_=src_t[0:ntok, :], func=AF.Identity,
            bias=st["negmur"][0:ntok, 0:1], scale=st["rstd"][0:ntok, 0:1]),
             reads=[src_t, st["negmur"], st["rstd"]], writes=[LNB])
        for kb in range(8):
            tp = trb_ps[kb % 2]
            idb = identb[0:ntok, 0:ntok]
            p.op("tensor", lambda e, kb=kb, tp=tp, LNB=LNB, ntok=ntok, idb=idb: e.transpose(
                out=tp[:, 0:ntok], in_=LNB[0:ntok, kb * 128:(kb + 1) * 128], identity=idb),
                 reads=[LNB, identb], writes=[tp])
            dsl = dst[:, kb, col0:col0 + ntok]
            p.op("scalar", lambda e, dsl=dsl, tp=tp, kb=kb, ntok=ntok: e.activation(
                out=dsl, in_=tp[:, 0:ntok], func=AF.Identity,
                scale=lncols[:, scale_idx, kb:kb + 1], bias=lncols[:, bias_idx, kb:kb + 1]),
                 reads=[tp, lncols], writes=[(dst, (kb, col0))])

    for b in range(B):
        for ci in range(NCH):
            c0 = ci * CHUNK
            if ci == 0:
                ln_tr(b, 0, H, 0, hhalo, 0, 1)
                p.op("vector", lambda e: e.tensor_scalar(out=hcT[:, :, 0:H], in0=hhalo[:, :, 0:H],
                                                         scalar1=hmask[:, 0:1], scalar2=None, op0=ALU.mult),
                     reads=[hhalo, hmask], writes=[hcT])
            else:
                p.op("vector", lambda e: e.tensor_copy(out=hcT[:, :, 0:H], in_=hcT[:, :, CHUNK:CHUNK + H]),
                     reads=[hcT], writes=[hcT])
            for tt in range(CHUNK // 128):
                ln_tr(b, H + c0 + tt * 128, 128, H + tt * 128, hcT, 0, 1, par=tt % 2, keep_tt=tt)
            for tt in range(CHUNK // 128):
                for g in range(4):
                    ps = cps[g // 2]
                    colr = (g % 2) * 256
                    def fc(e, g=g, tt=tt, ps=ps, colr=colr):
                        e.matmul(out=ps[:, colr:colr + 256], lhsT=ones_row[:],
                                 rhs=cbS[:, g * 256:(g + 1) * 256], start=True, stop=False)
                        last = None
                        for k_ in range(4):
                            for cb in range(2):
                                last = e.matmul(out=ps[:, colr:colr + 256],
                                                lhsT=hcT[:, g * 2 + cb, tt * 128 + k_:tt * 128 + k_ + 128],
                                                rhs=WcT[:, g, k_, cb, :],
                                                start=False, stop=(k_ == 3 and cb == 1))
                        return last
                    p.op("tensor", fc, reads=[hcT, WcT, ones_row, cbS], writes=[(cps[g // 2], g % 2)])
                G1 = g1t[tt % 2]
                for half in range(2):
                    p.op("vector", lambda e, half=half, G1=G1, tt=tt: e.tensor_add(
                        out=G1[:, half * 512:(half + 1) * 512], in0=cps[half][:],
                        in1=g0keep[:, tt, half * 512:(half + 1) * 512]),
                         reads=[(cps[half], 0), (cps[half], 1), (g0keep, tt)], writes=[(G1, half)])
                p.dma("gpsimd", g1_o[b, c0 + tt * 128:c0 + (tt + 1) * 128, :], G1[:],
                      reads=[G1], sem=f"g1st{tt % 2}")
                ln_tr(b, 0, 128, tt * 128, haT, 2, 3, x_from=G1, par=tt % 2)
            for ni, n in enumerate("qkv"):
                dramo = {"q": qT_o, "k": kT_o, "v": vT_o}[n]
                for och in range(8):
                    ps = mm_ps[och % 2]
                    def fq(e, n=n, och=och, ps=ps):
                        last = None
                        for kb in range(8):
                            last = e.matmul(out=ps[:], lhsT=WT[n][:, kb, och * 128:(och + 1) * 128],
                                            rhs=haT[:, kb, :], start=(kb == 0), stop=(kb == 7))
                        return last
                    p.op("tensor", fq, reads=[haT, WT[n]], writes=[ps])
                    p.op("scalar", lambda e, ni=ni, och=och, ps=ps: e.activation(
                        out=qkv_ev[:], in_=ps[:], func=AF.Identity, bias=bqkv[:, ni, och:och + 1]),
                         reads=[ps, bqkv], writes=[qkv_ev])
                    p.dma("gpsimd", dramo[b, och * 128:(och + 1) * 128, c0:c0 + CHUNK], qkv_ev[:],
                          reads=[qkv_ev], sem="qkvst")
    p.finalize()
    return p



def build_pooled(T=S, tag="pooled"):
    p = Prog(tag)
    PCH = 2048
    NCHP = T // PCH
    qT_i = p.dram("qT", [B, 128, T], BF16, kind="ExternalInput").ap()
    kT_i = p.dram("kT", [B, 128, T], BF16, kind="ExternalInput").ap()
    vT_i = p.dram("vT", [B, 128, T], BF16, kind="ExternalInput").ap()
    rs_i = p.dram("rs", [1, T], F32, kind="ExternalInput").ap()
    pkvT_o = p.dram("pkvT", [B, 128, T], BF16, kind="ExternalOutput").ap()

    ones = p.sbuf("ones", [128, PCH], F32)
    p.op("vector", lambda e: e.memset(ones[:], 1.0), writes=[ones])
    rsb = p.sbuf("rsb", [128, PCH], F32)
    rsl = p.sbuf("rsl", [1, PCH], BF16)
    rslf = p.sbuf("rslf", [1, PCH], F32)
    one_col = p.sbuf("one_col", [1, 128], BF16)
    p.op("vector", lambda e: e.memset(one_col[:], 1.0), writes=[one_col])
    rs_ps = p.psum("rs_ps", [128, 512], F32)
    qt = p.sbuf("qt", [128, PCH], BF16)
    kt = p.sbuf("kt", [128, PCH], BF16)
    vt = p.sbuf("vt", [128, PCH], BF16)
    cum = p.sbuf("cum", [128, PCH], F32)
    pq = p.sbuf("pq", [128, PCH], F32)
    pqk = p.sbuf("pqk", [128, PCH], F32)
    cum2 = p.sbuf("cum2", [128, PCH], F32)
    pk = p.sbuf("pk", [128, PCH], F32)
    pkv = p.sbuf("pkv", [128, PCH], BF16)
    qcar = p.sbuf("qcar", [128, 1], F32)
    kcar = p.sbuf("kcar", [128, 1], F32)

    for b in range(B):
        p.op("vector", lambda e: e.memset(qcar[:], 0.0), writes=[qcar])
        p.op("vector", lambda e: e.memset(kcar[:], 0.0), writes=[kcar])
        for ci in range(NCHP):
            c0 = ci * PCH
            p.dma("sync", qt[:], qT_i[b, :, c0:c0 + PCH], writes=[qt])
            p.dma("sync", kt[:], kT_i[b, :, c0:c0 + PCH], writes=[kt])
            p.dma("sync", vt[:], vT_i[b, :, c0:c0 + PCH], writes=[vt])
            rbview = bass.AP(rs_i.tensor, rs_i.offset + c0, [[0, 128], [1, PCH]])
            p.dma("sync", rsb[:], rbview, writes=[rsb])
            p.op("vector", lambda e: e.tensor_tensor_scan(out=cum[:], data0=ones[:], data1=qt[:],
                                                          initial=qcar[:, 0:1], op0=ALU.mult, op1=ALU.add),
                 reads=[ones, qt, qcar], writes=[cum])
            p.op("vector", lambda e: e.tensor_copy(out=qcar[:], in_=cum[:, PCH - 1:PCH]),
                 reads=[cum], writes=[qcar])
            p.op("vector", lambda e: e.tensor_mul(out=pq[:], in0=cum[:], in1=rsb[:]),
                 reads=[cum, rsb], writes=[pq])
            p.op("vector", lambda e: e.tensor_mul(out=pqk[:], in0=pq[:], in1=kt[:]),
                 reads=[pq, kt], writes=[pqk])
            p.op("vector", lambda e: e.tensor_tensor_scan(out=cum2[:], data0=ones[:], data1=pqk[:],
                                                          initial=kcar[:, 0:1], op0=ALU.mult, op1=ALU.add),
                 reads=[ones, pqk, kcar], writes=[cum2])
            p.op("vector", lambda e: e.tensor_copy(out=kcar[:], in_=cum2[:, PCH - 1:PCH]),
                 reads=[cum2], writes=[kcar])
            p.op("vector", lambda e: e.tensor_mul(out=pk[:], in0=cum2[:], in1=rsb[:]),
                 reads=[cum2, rsb], writes=[pk])
            p.op("vector", lambda e: e.tensor_mul(out=pkv[:], in0=pk[:], in1=vt[:]),
                 reads=[pk, vt], writes=[pkv])
            p.dma("gpsimd", pkvT_o[b, :, c0:c0 + PCH], pkv[:], reads=[pkv], sem="pkvst")
    p.finalize()
    return p



def build_l1attn_back(tag="l1ab"):
    p = Prog(tag)
    CHI = 256
    NCH = TLOC // CHI
    x_in = p.dram("x", [B, TLOC, D], F32, kind="ExternalInput").ap()
    g1_i = p.dram("g1", [B, TLOC, D], F32, kind="ExternalInput").ap()
    pkvT_i = p.dram("pkvT", [B, D, TLOC], BF16, kind="ExternalInput").ap()
    Wb_i = p.dram("Wb", [4096, D], F32, kind="ExternalInput").ap()
    Wu_i = p.dram("Wu", [D, 4096], F32, kind="ExternalInput").ap()
    bb_i = p.dram("bb", [128, 32], F32, kind="ExternalInput").ap()
    bu_i = p.dram("bu", [1, D], F32, kind="ExternalInput").ap()
    lncols_i = p.dram("lncols", [128, 2, 8], F32, kind="ExternalInput").ap()
    ident_in = p.dram("ident_in", [128, 128], F32, kind="ExternalInput").ap()
    x_out = p.dram("xo", [B, TLOC, D], F32, kind="ExternalOutput").ap()

    ident = p.sbuf("ident", [128, 128], F32)
    identb = p.sbuf("identb", [128, 128], BF16)
    p.dma("sync", ident[:], ident_in, writes=[ident])
    p.op("vector", lambda e: e.tensor_copy(out=identb[:], in_=ident[:]), reads=[ident], writes=[identb])
    bbcol = p.sbuf("bbcol", [128, 32], F32)
    p.dma("sync", bbcol[:], bb_i, writes=[bbcol])
    lncols = p.sbuf("lncols_s", [128, 2, 8], F32)
    p.dma("sync", lncols[:], lncols_i, writes=[lncols])
    buS = p.sbuf("buS", [1, D], BF16)
    buf_ = p.sbuf("buf_", [1, D], F32)
    p.dma("sync", buf_[:], bu_i, writes=[buf_])
    p.op("vector", lambda e: e.tensor_copy(out=buS[:], in_=buf_[:]), reads=[buf_], writes=[buS])
    ones_row = p.sbuf("ones_row", [1, 128], BF16)
    p.op("vector", lambda e: e.memset(ones_row[:], 1.0), writes=[ones_row])

    WbT = p.sbuf("WbT", [128, 8, 4096], BF16)
    WuT = p.sbuf("WuT", [128, 32, D], BF16)
    wld = p.sbuf("wld", [128, 1024], F32)
    tr_ps = p.psum("tr_ps", [128, 128], F32)
    for rb in range(32):
        p.dma("sync", wld[:], Wb_i[rb * 128:(rb + 1) * 128, :], writes=[wld])
        for kb in range(8):
            sl = wld[:, kb * 128:(kb + 1) * 128]
            p.op("tensor", lambda e, sl=sl: e.transpose(out=tr_ps[:], in_=sl, identity=ident[:]),
                 reads=[wld, ident], writes=[tr_ps])
            dsl = WbT[:, kb, rb * 128:(rb + 1) * 128]
            p.op("scalar", lambda e, dsl=dsl: e.activation(out=dsl, in_=tr_ps[:], func=AF.Copy),
                 reads=[tr_ps], writes=[(WbT, (kb, rb))])
    for rb in range(8):
        for qc in range(4):
            p.dma("sync", wld[:], Wu_i[rb * 128:(rb + 1) * 128, qc * 1024:(qc + 1) * 1024], writes=[wld])
            for kq in range(8):
                kb = qc * 8 + kq
                sl = wld[:, kq * 128:(kq + 1) * 128]
                p.op("tensor", lambda e, sl=sl: e.transpose(out=tr_ps[:], in_=sl, identity=ident[:]),
                     reads=[wld, ident], writes=[tr_ps])
                dsl = WuT[:, kb, rb * 128:(rb + 1) * 128]
                p.op("scalar", lambda e, dsl=dsl: e.activation(out=dsl, in_=tr_ps[:], func=AF.Copy),
                     reads=[tr_ps], writes=[(WuT, (kb, rb))])

    xt = [p.sbuf(f"xt{i}", [128, D], F32) for i in range(2)]
    sq = [p.sbuf(f"sq{i}", [128, D], F32) for i in range(2)]
    lnb = [p.sbuf(f"lnb{i}", [128, D], BF16) for i in range(2)]
    stats = [make_stat(p, f"st{i}") for i in range(2)]
    trb_ps = [p.psum(f"trb_ps{i}", [128, 128], BF16) for i in range(2)]
    pkvt = p.sbuf("pkvt", [128, 128], BF16)
    g2t = [p.sbuf(f"g2t{i}", [128, D], F32) for i in range(2)]
    g2keep = p.sbuf("g2keep", [128, CHI // 128, D], F32)
    hbT = p.sbuf("hbT", [128, 8, CHI], BF16)
    bhT = p.sbuf("bhT", [128, 32, CHI], BF16)
    mm_ps = [p.psum(f"mm_ps{i}", [128, 512], F32) for i in range(4)]
    xo_t = [p.sbuf(f"xo_t{i}", [128, D], F32) for i in range(2)]

    def ln_tr_aff(src_t, col0, dst, par):
        SQ, LNB, st = sq[par], lnb[par], stats[par]
        ln_stats_n(p, src_t, SQ, st, 128)
        p.op("scalar", lambda e, src_t=src_t, LNB=LNB, st=st: e.activation(
            out=LNB[:], in_=src_t[:], func=AF.Identity,
            bias=st["negmur"][:, 0:1], scale=st["rstd"][:, 0:1]),
             reads=[src_t, st["negmur"], st["rstd"]], writes=[LNB])
        for kb in range(8):
            tp = trb_ps[kb % 2]
            p.op("tensor", lambda e, kb=kb, tp=tp, LNB=LNB: e.transpose(
                out=tp[:], in_=LNB[:, kb * 128:(kb + 1) * 128], identity=identb[:]),
                 reads=[LNB, identb], writes=[tp])
            dsl = dst[:, kb, col0:col0 + 128]
            p.op("scalar", lambda e, dsl=dsl, tp=tp, kb=kb: e.activation(
                out=dsl, in_=tp[:], func=AF.Identity,
                scale=lncols[:, 0, kb:kb + 1], bias=lncols[:, 1, kb:kb + 1]),
                 reads=[tp, lncols], writes=[(dst, (kb, col0))])

    for b in range(B):
        for ci in range(NCH):
            c0 = ci * CHI
            for tt in range(CHI // 128):
                par = tt % 2
                G2 = g2t[par]
                X = xt[par]
                p.dma("sync", X[:], g1_i[b, c0 + tt * 128:c0 + (tt + 1) * 128, :], writes=[X])
                for kb in range(8):
                    p.dma("sync", pkvt[:], pkvT_i[b, kb * 128:(kb + 1) * 128,
                                                         c0 + tt * 128:c0 + (tt + 1) * 128],
                          writes=[pkvt])
                    tp = trb_ps[kb % 2]
                    p.op("tensor", lambda e, tp=tp: e.transpose(out=tp[:], in_=pkvt[:],
                                                                identity=identb[:]),
                         reads=[pkvt, identb], writes=[tp])
                    p.op("vector", lambda e, kb=kb, tp=tp, G2=G2, X=X: e.tensor_add(
                        out=G2[:, kb * 128:(kb + 1) * 128], in0=tp[:], in1=X[:, kb * 128:(kb + 1) * 128]),
                         reads=[tp, X], writes=[(G2, kb)])
                p.op("vector", lambda e, tt=tt, G2=G2: e.tensor_copy(out=g2keep[:, tt, :], in_=G2[:]),
                     reads=[G2], writes=[(g2keep, tt)])
                ln_tr_aff(G2, tt * 128, hbT, par)
            for och in range(32):
                ps = mm_ps[och % 2]
                def fb1(e, och=och, ps=ps):
                    last = None
                    for kb in range(8):
                        last = e.matmul(out=ps[:, 0:CHI], lhsT=WbT[:, kb, och * 128:(och + 1) * 128],
                                        rhs=hbT[:, kb, :], start=(kb == 0), stop=(kb == 7))
                    return last
                p.op("tensor", fb1, reads=[hbT, WbT], writes=[ps])
                p.op("scalar", lambda e, och=och, ps=ps: e.activation(
                    out=bhT[:, och, :], in_=ps[:, 0:CHI], func=AF.Gelu, bias=bbcol[:, och:och + 1]),
                     reads=[ps, bbcol], writes=[(bhT, och)])
            for tt in range(CHI // 128):
                XO = xo_t[tt % 2]
                X4 = xt[tt % 2]
                p.dma("sync", X4[:], x_in[b, c0 + tt * 128:c0 + (tt + 1) * 128, :], writes=[X4],
                      sem=f"x4ld{tt % 2}")
                for oc in range(2):
                    ps = mm_ps[2 + oc]
                    def fo(e, tt=tt, oc=oc, ps=ps):
                        e.matmul(out=ps[:], lhsT=ones_row[:], rhs=buS[:, oc * 512:(oc + 1) * 512],
                                 start=True, stop=False)
                        last = None
                        for kb in range(32):
                            last = e.matmul(out=ps[:], lhsT=bhT[:, kb, tt * 128:(tt + 1) * 128],
                                            rhs=WuT[:, kb, oc * 512:(oc + 1) * 512],
                                            start=False, stop=(kb == 31))
                        return last
                    p.op("tensor", fo, reads=[bhT, WuT, ones_row, buS], writes=[ps])
                    p.op("vector", lambda e, oc=oc, ps=ps, tt=tt, XO=XO: e.tensor_add(
                        out=XO[:, oc * 512:(oc + 1) * 512], in0=ps[:],
                        in1=g2keep[:, tt, oc * 512:(oc + 1) * 512]),
                         reads=[ps, (g2keep, tt)], writes=[(XO, oc)])
                p.op("vector", lambda e, XO=XO, X4=X4: e.tensor_add(out=XO[:], in0=XO[:], in1=X4[:]),
                     reads=[XO, X4], writes=[XO])
                p.dma("gpsimd", x_out[b, c0 + tt * 128:c0 + (tt + 1) * 128, :], XO[:],
                      reads=[XO], sem=f"xst{tt % 2}")
    p.finalize()
    return p



_PROG_CACHE = {}

def _prog(name, builder):
    if name not in _PROG_CACHE:
        _PROG_CACHE[name] = builder()
    return _PROG_CACHE[name]

def _ident_np():
    return np.eye(128, dtype=np.float32)

def _col8(v):
    return np.ascontiguousarray(np.asarray(v).reshape(8, 128).T)

def _col16(v):
    return np.ascontiguousarray(np.asarray(v).reshape(16, 128).T)

def _col32(v):
    return np.ascontiguousarray(np.asarray(v).reshape(32, 128).T)

def _sel_np():
    sel = np.zeros((32, 32, 128), np.float32)
    for j in range(32):
        sel[j, j, :] = 1.0
    return sel.astype(BF16NP)

def _run(p, in_maps):
    return run_bass_kernel_spmd(p.nc, in_maps, core_ids=list(range(N_CORES)))


def run_moe(x_full, We, be, Wr, br):
    p = _prog("moe", build_moe)
    in_maps = []
    for c in range(N_CORES):
        sl = x_full[:, c * TLOC:(c + 1) * TLOC, :]
        in_maps.append(dict(x=np.ascontiguousarray(sl), We=We, be=be, Wr=Wr, br=br,
                            ident_in=_ident_np()))
    res = _run(p, in_maps)
    out = np.empty_like(x_full)
    for c in range(N_CORES):
        out[:, c * TLOC:(c + 1) * TLOC, :] = res.results[c]["xo"]
    return out


def run_mfront(x1, ins):
    p = _prog("mfront", build_mamba_front)
    in_maps = []
    for c in range(N_CORES):
        lo, hi = c * TLOC, (c + 1) * TLOC
        xh = np.zeros((B, TLOC + 6, D), np.float32)
        xh[:, 3:3 + TLOC] = x1[:, lo:hi]
        if lo > 0:
            xh[:, 0:3] = x1[:, lo - 3:lo]
        if hi < S:
            xh[:, 3 + TLOC:] = x1[:, hi:hi + 3]
        d = dict(x=xh, ident_in=_ident_np())
        for d_, pref in (("f", "fwd_"), ("b", "bwd_")):
            d[f"Win_{d_}"] = ins[f"L0_attn_{pref}in_proj_w"]
            d[f"convw_{d_}"] = np.ascontiguousarray(
                ins[f"L0_attn_{pref}conv_w"][:, 0, :].reshape(16, 128, 4).transpose(1, 0, 2))
            d[f"convb_{d_}"] = _col16(ins[f"L0_attn_{pref}conv_b"])
            d[f"xproj_{d_}"] = ins[f"L0_attn_{pref}x_proj_w"]
            d[f"dtw_{d_}"] = ins[f"L0_attn_{pref}dt_proj_w"]
            d[f"dtb_{d_}"] = _col16(ins[f"L0_attn_{pref}dt_proj_b"])
        in_maps.append(d)
    res = _run(p, in_maps)
    st = {}
    for name in ("uT", "zT", "dT", "duT", "bcT"):
        for d_ in "fb":
            st[f"{name}_{d_}"] = np.concatenate(
                [np.asarray(res.results[c][f"{name}_{d_}"]) for c in range(N_CORES)], axis=2)
    return st


def run_scan(st, ins):
    p = _prog("scan", lambda: build_scan(T=S))
    in_maps, meta = [], []
    for d_ in "fb":
        alog = np.asarray(ins[f'L0_attn_{"fwd" if d_ == "f" else "bwd"}_A_log'], np.float32)
        for b in range(B):
            for half in range(2):
                dsl = slice(half * 1024, (half + 1) * 1024)
                dT = st[f'dT_{d_}'][b, dsl, :]
                duT = st[f'duT_{d_}'][b, dsl, :]
                bcT = st[f'bcT_{d_}'][b]
                if d_ == "b":
                    dT = dT[:, ::-1]; duT = duT[:, ::-1]; bcT = bcT[:, ::-1]
                in_maps.append(dict(dT=np.ascontiguousarray(dT), duT=np.ascontiguousarray(duT),
                                    bcT=np.ascontiguousarray(bcT),
                                    alog=np.ascontiguousarray(alog[dsl].reshape(8, 128, 16)),
                                    sel=_sel_np()))
                meta.append((d_, b, half))
    res = _run(p, in_maps)
    out = {d_: np.zeros((B, 2048, S), BF16NP) for d_ in "fb"}
    for c, (d_, b, half) in enumerate(meta):
        y = np.asarray(res.results[c]["yT"])
        if d_ == "b":
            y = y[:, ::-1]
        out[d_][b, half * 1024:(half + 1) * 1024, :] = y
    return out


def run_mback(x1, st, ys, ins):
    p = _prog("mback", build_mamba_back)
    in_maps = []
    for c in range(N_CORES):
        lo, hi = c * TLOC, (c + 1) * TLOC
        d = dict(x=np.ascontiguousarray(x1[:, lo:hi]),
                 outw_f=ins['L0_attn_fwd_out_proj_w'], outw_b=ins['L0_attn_bwd_out_proj_w'],
                 fw=ins['L0_attn_fusion_w'], fb=np.asarray(ins['L0_attn_fusion_b'])[None, :],
                 ident_in=_ident_np(),
                 Dv_f=_col16(ins['L0_attn_fwd_D']), Dv_b=_col16(ins['L0_attn_bwd_D']))
        for d_ in "fb":
            d[f"yT_{d_}"] = np.ascontiguousarray(ys[d_][:, :, lo:hi])
            d[f"uT_{d_}"] = np.ascontiguousarray(st[f'uT_{d_}'][:, :, lo:hi])
            d[f"zT_{d_}"] = np.ascontiguousarray(st[f'zT_{d_}'][:, :, lo:hi])
        in_maps.append(d)
    res = _run(p, in_maps)
    out = np.empty_like(x1)
    for c in range(N_CORES):
        out[:, c * TLOC:(c + 1) * TLOC, :] = res.results[c]["xo"]
    return out


def run_l1ffn_launch(x_full, ins):
    p = _prog("l1ffn", build_l1ffn)
    in_maps = []
    for c in range(N_CORES):
        lo, hi = c * TLOC, (c + 1) * TLOC
        xh = np.zeros((B, TLOC + 1, D), np.float32)
        xh[:, 1:] = x_full[:, lo:hi]
        if lo > 0:
            xh[:, 0] = x_full[:, lo - 1]
        in_maps.append(dict(x=xh, Wr=ins['L1_ffn_W_r'], Wk=ins['L1_ffn_W_k'], Wv=ins['L1_ffn_W_v'],
                            mur=_col8(ins['L1_ffn_mu_r']), muk=_col8(ins['L1_ffn_mu_k']),
                            ident_in=_ident_np()))
    res = _run(p, in_maps)
    out = np.empty_like(x_full)
    for c in range(N_CORES):
        out[:, c * TLOC:(c + 1) * TLOC, :] = res.results[c]["xo"]
    return out


def run_l1af(x4, ins):
    p = _prog("l1af", build_l1attn_front)
    lncols = np.zeros((128, 6, 8), np.float32)
    lncols[:, 0] = _col8(ins['L1_attn_ln_conv_g']); lncols[:, 1] = _col8(ins['L1_attn_ln_conv_b'])
    lncols[:, 2] = _col8(ins['L1_attn_ln_attn_g']); lncols[:, 3] = _col8(ins['L1_attn_ln_attn_b'])
    bqkv = np.zeros((128, 3, 8), np.float32)
    bqkv[:, 0] = _col8(ins['L1_attn_bq']); bqkv[:, 1] = _col8(ins['L1_attn_bk'])
    bqkv[:, 2] = _col8(ins['L1_attn_bv'])
    in_maps = []
    for c in range(N_CORES):
        lo, hi = c * TLOC, (c + 1) * TLOC
        xh = np.zeros((B, TLOC + 3, D), np.float32)
        xh[:, 3:] = x4[:, lo:hi]
        if lo > 0:
            xh[:, 0:3] = x4[:, lo - 3:lo]
        in_maps.append(dict(x=xh, Wq=ins['L1_attn_Wq'], Wk=ins['L1_attn_Wk'], Wv=ins['L1_attn_Wv'],
                            bqkv=bqkv, cw=np.ascontiguousarray(np.asarray(ins['L1_attn_conv_w']).transpose(2, 0, 1)),
                            cb=np.asarray(ins['L1_attn_conv_b'])[None, :], lncols=lncols,
                            ident_in=_ident_np(),
                            hmask=np.full((128, 1), 0.0 if lo == 0 else 1.0, np.float32)))
    res = _run(p, in_maps)
    g1 = np.concatenate([np.asarray(res.results[c]["g1"]) for c in range(N_CORES)], axis=1)
    qT = np.concatenate([np.asarray(res.results[c]["qT"]) for c in range(N_CORES)], axis=2)
    kT = np.concatenate([np.asarray(res.results[c]["kT"]) for c in range(N_CORES)], axis=2)
    vT = np.concatenate([np.asarray(res.results[c]["vT"]) for c in range(N_CORES)], axis=2)
    return g1, qT, kT, vT


def run_pooled(qT, kT, vT):
    p = _prog("pooled", lambda: build_pooled(T=S))
    rs = (1.0 / np.arange(1, S + 1, dtype=np.float32))[None, :]
    in_maps = [dict(qT=np.ascontiguousarray(qT[:, c * 128:(c + 1) * 128]),
                    kT=np.ascontiguousarray(kT[:, c * 128:(c + 1) * 128]),
                    vT=np.ascontiguousarray(vT[:, c * 128:(c + 1) * 128]), rs=rs)
               for c in range(N_CORES)]
    res = _run(p, in_maps)
    return np.concatenate([np.asarray(res.results[c]["pkvT"]) for c in range(N_CORES)], axis=1)


def run_l1ab(x4, g1, pkvT, ins):
    p = _prog("l1ab", build_l1attn_back)
    lncols = np.zeros((128, 2, 8), np.float32)
    lncols[:, 0] = _col8(ins['L1_attn_ln_boom_g']); lncols[:, 1] = _col8(ins['L1_attn_ln_boom_b'])
    in_maps = []
    for c in range(N_CORES):
        lo, hi = c * TLOC, (c + 1) * TLOC
        in_maps.append(dict(x=np.ascontiguousarray(x4[:, lo:hi]),
                            g1=np.ascontiguousarray(g1[:, lo:hi]),
                            pkvT=np.ascontiguousarray(pkvT[:, :, lo:hi]),
                            Wb=ins['L1_attn_Wb'], Wu=ins['L1_attn_Wu'],
                            bb=_col32(ins['L1_attn_bb']),
                            bu=np.asarray(ins['L1_attn_bu'])[None, :], lncols=lncols,
                            ident_in=_ident_np()))
    res = _run(p, in_maps)
    out = np.empty_like(x4)
    for c in range(N_CORES):
        out[:, c * TLOC:(c + 1) * TLOC, :] = res.results[c]["xo"]
    return out


def kernel(**inputs):
    x0 = np.ascontiguousarray(np.asarray(inputs["L0_attn_x"], np.float32))
    moe_args = (np.asarray(inputs["L0_ffn_W_experts"], np.float32),
                np.asarray(inputs["L0_ffn_b_experts"], np.float32),
                np.asarray(inputs["L0_ffn_W_router"], np.float32),
                np.asarray(inputs["L0_ffn_b_router"], np.float32))
    ins = {k: np.asarray(v) for k, v in inputs.items()}
    x1 = run_moe(x0, *moe_args)
    st = run_mfront(x1, ins)
    ys = run_scan(st, ins)
    x2 = run_mback(x1, st, ys, ins)
    x3 = run_moe(x2, *moe_args)
    x4 = run_l1ffn_launch(x3, ins)
    g1, qT, kT, vT = run_l1af(x4, ins)
    pkvT = run_pooled(qT, kT, vT)
    x5 = run_l1ab(x4, g1, pkvT, ins)
    x6 = run_l1ffn_launch(x5, ins)
    return x6


# version 02
# speedup vs baseline: 1.0070x; 1.0070x over previous
import contextlib
import numpy as np
import ml_dtypes
import concourse.bass as bass
import concourse.mybir as mybir
from concourse.bass_utils import run_bass_kernel_spmd

F32 = mybir.dt.float32
BF16 = mybir.dt.bfloat16
AF = mybir.ActivationFunctionType
ALU = mybir.AluOpType
AX = mybir.AxisListType
BF16NP = ml_dtypes.bfloat16

N_CORES = 8
B = 2
S = 16384
D = 1024
TLOC = S // N_CORES
NT = TLOC // 128


ENGINES = ("tensor", "vector", "scalar", "gpsimd", "sync")


class Prog:
    def __init__(self, name="prog"):
        self.nc = bass.Bass("TRN2", target_bir_lowering=False)
        self.name = name
        self.stack = contextlib.ExitStack()
        self.ops = []
        self.counts = {e: 0 for e in ENGINES}
        self.dma_counts = {}
        self.deps = {}
        self.waited = {e: {} for e in ENGINES}
        self.sem_handles = {}
        self._finalized = False

    def dram(self, name, shape, dtype, kind="Internal"):
        t = self.nc.dram_tensor(name, list(shape), dtype, kind=kind)
        return t

    def sbuf(self, name, shape, dtype):
        pre = getattr(self, "_stage_name", "")
        return self.stack.enter_context(self.nc.sbuf_tensor(pre + name, list(shape), dtype))

    def psum(self, name, shape, dtype=F32):
        pre = getattr(self, "_stage_name", "")
        return self.stack.enter_context(self.nc.psum_tensor(pre + name, list(shape), dtype))

    @staticmethod
    def _key(item):
        if isinstance(item, tuple):
            h, sk = item
        else:
            h, sk = item, None
        name = getattr(h, "name", None)
        if name is None:
            name = h.tensor.name if hasattr(h, "tensor") else str(h)
        return (name, sk)

    def _entry(self, name):
        return self.deps.setdefault(name, {"w": {}, "r": {}})

    def _conflicts(self, table, sk):
        out = {}
        for (sk2, sem), val in table.items():
            if sk is None or sk2 is None or sk == sk2:
                if out.get(sem, 0) < val:
                    out[sem] = val
        return out

    def _record(self, table, name, sk, sem, val):
        k = (sk, sem)
        if table.get(k, 0) < val:
            table[k] = val

    def op(self, engine, fn, reads=(), writes=(), dma=False, dma_sem=None):
        needed = {}
        rkeys = [self._key(r) for r in reads]
        wkeys = [self._key(w) for w in writes]
        for (name, sk) in rkeys:
            e = self._entry(name)
            for sem, val in self._conflicts(e["w"], sk).items():
                if needed.get(sem, 0) < val:
                    needed[sem] = val
        for (name, sk) in wkeys:
            e = self._entry(name)
            for tab in (e["w"], e["r"]):
                for sem, val in self._conflicts(tab, sk).items():
                    if needed.get(sem, 0) < val:
                        needed[sem] = val
        waits = []
        wt = self.waited[engine]
        for sem, val in sorted(needed.items()):
            if sem.startswith("eng_" + engine) and not dma:
                pass
            if wt.get(sem, 0) < val:
                waits.append((sem, val))
                wt[sem] = val
        if dma:
            slots = getattr(self, "_dma_slots", None)
            if slots is None:
                self._dma_slots = slots = {}
            if dma_sem not in slots:
                slots[dma_sem] = f"ds{len(slots)}"
            sem = slots[dma_sem]
            self.dma_counts[sem] = self.dma_counts.get(sem, 0) + 16
            val_after = self.dma_counts[sem]
        else:
            sem = "eng_" + engine
            self.counts[engine] += 1
            val_after = self.counts[engine]
        incs = [(sem, 16 if dma else 1)]
        for (name, sk) in rkeys:
            self._record(self._entry(name)["r"], name, sk, sem, val_after)
        for (name, sk) in wkeys:
            self._record(self._entry(name)["w"], name, sk, sem, val_after)
        self.ops.append((engine, fn, waits, incs))
        return (sem, val_after)

    def dma(self, queue, out, in_, reads=(), writes=(), sem=None):
        if sem is None:
            pool = list(writes) or list(reads)
            sem = self._key(pool[0])[0]
        return self.op(queue, lambda e: e.dma_start(out=out, in_=in_),
                       reads=reads, writes=writes, dma=True, dma_sem=sem)

    def barrier_wait(self, engine, token):
        sem, val = token
        wt = self.waited[engine]
        if wt.get(sem, 0) >= val:
            return
        wt[sem] = val
        self.ops.append((engine, None, [(sem, val)], []))

    def barrier(self):
        toks = [("eng_" + e, self.counts[e]) for e in ENGINES if self.counts[e]]
        toks += [(s, v) for s, v in self.dma_counts.items()]
        for e in ENGINES:
            for t in toks:
                self.barrier_wait(e, t)

    @contextlib.contextmanager
    def stage(self, name):
        old_stack = self.stack
        self.stack = contextlib.ExitStack()
        old_slot = getattr(self, "_dma_slots", None)
        old_name = getattr(self, "_stage_name", "")
        self._dma_slots = {}
        self._stage_name = name + "_"
        try:
            yield
        finally:
            self.barrier()
            self.stack.close()
            self.stack = old_stack
            self._dma_slots = old_slot
            self._stage_name = old_name

    def dyn_dma(self, queue, out, in_, reg, reads=(), writes=(), sem=None):
        return self.dma(queue, out, in_, reads=reads, writes=writes, sem=sem)

    def coll(self, kind, op_, ins_ap, outs_ap, reads=(), writes=()):
        def f(e):
            return e.collective_compute(kind, op_, replica_groups=[list(range(N_CORES))],
                                        ins=[a.opt() for a in ins_ap], outs=[a.opt() for a in outs_ap])
        return self.op("gpsimd", f, reads=reads, writes=writes)

    def finalize(self, final_tokens=()):
        assert not self._finalized
        self._finalized = True
        nc = self.nc
        sems = {}
        for e in ENGINES:
            if self.counts[e]:
                sems["eng_" + e] = self.stack.enter_context(nc.semaphore("eng_" + e))
        for sname in self.dma_counts:
            sems[sname] = self.stack.enter_context(nc.semaphore(sname))
        self.sem_handles = sems
        block = self.stack.enter_context(nc.Block())
        per_engine = {e: [] for e in ENGINES}
        for rec in self.ops:
            per_engine[rec[0]].append(rec)
        fin_engine = "gpsimd"
        final_tokens = list(final_tokens) + [(s, v) for s, v in self.dma_counts.items()]
        for e in ENGINES:
            eng_ops = per_engine[e]
            fins = [t for t in final_tokens] if e == fin_engine else []
            if not eng_ops and not fins:
                continue

            def make(eng_name, eng_ops, fins):
                def body(eng):
                    for (_, fn, waits, incs) in eng_ops:
                        for (sem, val) in waits:
                            if sem in sems:
                                eng.wait_ge(sems[sem], val)
                        if fn is not None:
                            inst = fn(eng)
                            for (sem, amt) in incs:
                                inst.then_inc(sems[sem], amt)
                    for (sem, val) in fins:
                        if sem in sems:
                            eng.wait_ge(sems[sem], val)
                return body

            getattr(block, e)(make(e, eng_ops, fins))
        return nc

    def close(self):
        self.stack.close()



def ln_stats(p, x_tile, sq_scratch, stat, n_feat=D, eps=1e-5):
    v, s = "vector", "scalar"
    p.op(v, lambda e: e.reduce_sum(out=stat["sum"][:], in_=x_tile[:], axis=AX.X),
         reads=[x_tile], writes=[stat["sum"]])
    p.op(s, lambda e: e.activation(out=sq_scratch[:], in_=x_tile[:], func=AF.Square,
                                   accum_out=stat["sumsq"][:]),
         reads=[x_tile], writes=[sq_scratch, stat["sumsq"]])
    p.op(v, lambda e: e.tensor_scalar(out=stat["mean"][:], in0=stat["sum"][:],
                                      scalar1=1.0 / n_feat, scalar2=None, op0=ALU.mult),
         reads=[stat["sum"]], writes=[stat["mean"]])
    p.op(v, lambda e: e.tensor_mul(out=stat["m2"][:], in0=stat["mean"][:], in1=stat["mean"][:]),
         reads=[stat["mean"]], writes=[stat["m2"]])
    p.op(v, lambda e: e.tensor_scalar(out=stat["nm2"][:], in0=stat["m2"][:], scalar1=-1.0,
                                      scalar2=None, op0=ALU.mult),
         reads=[stat["m2"]], writes=[stat["nm2"]])
    p.op(v, lambda e: e.scalar_tensor_tensor(out=stat["var"][:], in0=stat["sumsq"][:],
                                             scalar=1.0 / n_feat, in1=stat["nm2"][:],
                                             op0=ALU.mult, op1=ALU.add),
         reads=[stat["sumsq"], stat["nm2"]], writes=[stat["var"]])
    p.op(v, lambda e: e.tensor_scalar(out=stat["vare"][:], in0=stat["var"][:], scalar1=eps,
                                      scalar2=None, op0=ALU.add),
         reads=[stat["var"]], writes=[stat["vare"]])
    p.op(s, lambda e: e.activation(out=stat["std"][:], in_=stat["vare"][:], func=AF.Sqrt),
         reads=[stat["vare"]], writes=[stat["std"]])
    p.op(v, lambda e: e.reciprocal(out=stat["rstd"][:], in_=stat["std"][:]),
         reads=[stat["std"]], writes=[stat["rstd"]])
    p.op(v, lambda e: e.tensor_mul(out=stat["mr"][:], in0=stat["mean"][:], in1=stat["rstd"][:]),
         reads=[stat["mean"], stat["rstd"]], writes=[stat["mr"]])
    p.op(v, lambda e: e.tensor_scalar(out=stat["negmur"][:], in0=stat["mr"][:], scalar1=-1.0,
                                      scalar2=None, op0=ALU.mult),
         reads=[stat["mr"]], writes=[stat["negmur"]])


def make_stat(p, tag):
    names = ("sum", "sumsq", "mean", "m2", "nm2", "var", "vare", "std", "rstd", "mr", "negmur")
    return {n: p.sbuf(f"{tag}_{n}", [128, 1], F32) for n in names}



def emit_moe(p, io):
    x_in = io["x"]; We = io["We"]; be = io["be"]; Wr = io["Wr"]; br = io["br"]
    ident_in = io["ident_in"]; x_out = io["xo"]

    ident = p.sbuf("ident", [128, 128], F32)
    identb = p.sbuf("identb", [128, 128], BF16)
    ones_row = p.sbuf("ones_row", [1, 128], BF16)
    p.dma("sync", ident[:], ident_in, writes=[ident])
    p.op("vector", lambda e: e.tensor_copy(out=identb[:], in_=ident[:]), reads=[ident], writes=[identb])
    p.op("vector", lambda e: e.memset(ones_row[:], 1.0), writes=[ones_row])

    WeT = p.sbuf("WeT", [128, 8, 8, 1024], BF16)
    wld = [p.sbuf(f"wld{i}", [128, 1024], F32) for i in range(2)]
    ex_ps = [p.psum(f"ex_ps{i}", [128, 512], F32) for i in range(4)]
    tr_ps = [ex_ps[0], ex_ps[1]]
    for e_i in range(8):
        for oc in range(8):
            w = wld[(e_i * 8 + oc) % 2]
            p.dma("sync", w[:], We[e_i, oc * 128:(oc + 1) * 128, :], writes=[w])
            for kb in range(8):
                tp = tr_ps[kb % 2]
                sl = w[:, kb * 128:(kb + 1) * 128]
                p.op("tensor", lambda e, sl=sl, tp=tp: e.transpose(out=tp[:, 0:128], in_=sl, identity=ident[:]),
                     reads=[w, ident], writes=[tp])
                dsl = WeT[:, e_i, kb, oc * 128:(oc + 1) * 128]
                p.op("scalar", lambda e, dsl=dsl, tp=tp: e.activation(out=dsl, in_=tp[:, 0:128], func=AF.Copy),
                     reads=[tp], writes=[(WeT, (e_i, kb, oc))])
    WrT = p.sbuf("WrT", [128, 8, 8], BF16)
    wr_s = p.sbuf("wr_s", [8, 1024], F32)
    p.dma("sync", wr_s[:], Wr, writes=[wr_s])
    for kb in range(8):
        tp = tr_ps[kb % 2]
        sl = wr_s[:, kb * 128:(kb + 1) * 128]
        p.op("tensor", lambda e, sl=sl, tp=tp: e.transpose(out=tp[:, 0:8], in_=sl, identity=ident[0:8, 0:8]),
             reads=[wr_s, ident], writes=[tp])
        dsl = WrT[:, kb, :]
        p.op("scalar", lambda e, dsl=dsl, tp=tp: e.activation(out=dsl, in_=tp[:, 0:8], func=AF.Copy),
             reads=[tp], writes=[(WrT, kb)])
    beS = p.sbuf("beS", [8, 1024], BF16)
    be_f = p.sbuf("be_f", [8, 1024], F32)
    p.dma("sync", be_f[:], be, writes=[be_f])
    p.op("vector", lambda e: e.tensor_copy(out=beS[:], in_=be_f[:]), reads=[be_f], writes=[beS])
    brS = p.sbuf("brS", [1, 8], BF16)
    br_f = p.sbuf("br_f", [1, 8], F32)
    p.dma("sync", br_f[:], br.unsqueeze(0), writes=[br_f])
    p.op("vector", lambda e: e.tensor_copy(out=brS[:], in_=br_f[:]), reads=[br_f], writes=[brS])

    NPAR = 2
    xt = [p.sbuf(f"xt{i}", [128, D], F32) for i in range(NPAR)]
    sq = [p.sbuf(f"sq{i}", [128, D], F32) for i in range(NPAR)]
    lnb = [p.sbuf(f"lnb{i}", [128, D], BF16) for i in range(NPAR)]
    lnT = [p.sbuf(f"lnT{i}", [128, 8, 128], BF16) for i in range(NPAR)]
    stats = [make_stat(p, f"st{i}") for i in range(NPAR)]
    tr_bps = [p.psum(f"trb{i}", [128, 128], BF16) for i in range(2)]
    rt_ps = p.psum("rt_ps", [128, 8], F32)
    rw = p.sbuf("rw", [128, 8], F32)
    rwm = p.sbuf("rwm", [128, 8], F32)
    m1 = p.sbuf("m1", [128, 1], F32)
    m2 = p.sbuf("m2", [128, 1], F32)
    t8a = p.sbuf("t8a", [128, 8], F32)
    t8b = p.sbuf("t8b", [128, 8], F32)
    t8c = p.sbuf("t8c", [128, 8], F32)
    ssum = p.sbuf("ssum", [128, 1], F32)
    srec = p.sbuf("srec", [128, 1], F32)
    rwT_ps = p.psum("rwT_ps", [8, 128], F32)
    rwT = p.sbuf("rwT", [8, 128], BF16)
    acc = [p.sbuf(f"acc{i}", [128, D], F32) for i in range(3)]
    out_t = [p.sbuf(f"out_t{i}", [128, D], F32) for i in range(NPAR)]
    out_tokens = []

    for b in range(B):
        for ti in range(NT):
            par = (b * NT + ti) % NPAR
            X = xt[par]
            p.dma("sync", X[:], x_in[b, ti * 128:(ti + 1) * 128, :], writes=[X])
            st = stats[par]
            ln_stats(p, X, sq[par], st)
            LNB = lnb[par]
            p.op("scalar", lambda e, X=X, LNB=LNB, st=st:
                 e.activation(out=LNB[:], in_=X[:], func=AF.Identity,
                              bias=st["negmur"][:, 0:1], scale=st["rstd"][:, 0:1]),
                 reads=[X, st["negmur"], st["rstd"]], writes=[LNB])
            LT = lnT[par]
            for kb in range(8):
                TPS = tr_bps[kb % 2]
                sl = LNB[:, kb * 128:(kb + 1) * 128]
                p.op("tensor", lambda e, sl=sl, TPS=TPS: e.transpose(out=TPS[:], in_=sl, identity=identb[:]),
                     reads=[LNB, identb], writes=[TPS])
                dsl = LT[:, kb, :]
                p.op("scalar", lambda e, dsl=dsl, TPS=TPS: e.activation(out=dsl, in_=TPS[:], func=AF.Copy),
                     reads=[TPS], writes=[(LT, kb)])
            def rt_mm(e, LT=LT):
                e.matmul(out=rt_ps[:], lhsT=ones_row[:], rhs=brS[:], start=True, stop=False)
                last = None
                for kb in range(8):
                    last = e.matmul(out=rt_ps[:], lhsT=LT[:, kb, :], rhs=WrT[:, kb, :],
                                    start=False, stop=(kb == 7))
                return last
            p.op("tensor", rt_mm, reads=[ones_row, brS, LT, WrT], writes=[rt_ps])
            p.op("vector", lambda e: e.tensor_reduce(out=m1[:], in_=rt_ps[:], axis=AX.X, op=ALU.max, negate=True),
                 reads=[rt_ps], writes=[m1])
            p.op("scalar", lambda e: e.activation(out=rw[:], in_=rt_ps[:], func=AF.Exp, bias=m1[:, 0:1]),
                 reads=[rt_ps, m1], writes=[rw])
            p.op("vector", lambda e: e.reduce_sum(out=ssum[:], in_=rw[:], axis=AX.X),
                 reads=[rw], writes=[ssum])
            p.op("vector", lambda e: e.reciprocal(out=srec[:], in_=ssum[:]),
                 reads=[ssum], writes=[srec])
            p.op("vector", lambda e: e.tensor_reduce(out=m1[:], in_=rw[:], axis=AX.X, op=ALU.max, negate=True),
                 reads=[rw], writes=[m1])
            p.op("vector", lambda e: e.tensor_scalar(out=t8a[:], in0=rw[:], scalar1=m1[:, 0:1],
                                                     scalar2=None, op0=ALU.add),
                 reads=[rw, m1], writes=[t8a])
            p.op("vector", lambda e: e.tensor_scalar(out=t8b[:], in0=t8a[:], scalar1=0.0,
                                                     scalar2=None, op0=ALU.is_lt),
                 reads=[t8a], writes=[t8b])
            p.op("vector", lambda e: e.tensor_mul(out=t8c[:], in0=t8b[:], in1=rw[:]),
                 reads=[t8b, rw], writes=[t8c])
            p.op("vector", lambda e: e.tensor_reduce(out=m2[:], in_=t8c[:], axis=AX.X, op=ALU.max, negate=True),
                 reads=[t8c], writes=[m2])
            p.op("vector", lambda e: e.tensor_scalar(out=t8a[:], in0=rw[:], scalar1=m2[:, 0:1],
                                                     scalar2=None, op0=ALU.add),
                 reads=[rw, m2], writes=[t8a])
            p.op("vector", lambda e: e.tensor_scalar(out=t8b[:], in0=t8a[:], scalar1=0.0,
                                                     scalar2=None, op0=ALU.is_ge),
                 reads=[t8a], writes=[t8b])
            p.op("vector", lambda e: e.tensor_mul(out=rwm[:], in0=t8b[:], in1=rw[:]),
                 reads=[t8b, rw], writes=[rwm])
            p.op("vector", lambda e: e.tensor_scalar(out=rw[:], in0=rwm[:], scalar1=srec[:, 0:1],
                                                     scalar2=None, op0=ALU.mult),
                 reads=[rwm, srec], writes=[rw])
            p.op("tensor", lambda e: e.transpose(out=rwT_ps[:, :], in_=rw[:], identity=ident[:]),
                 reads=[rw, ident], writes=[rwT_ps])
            p.op("scalar", lambda e: e.activation(out=rwT[:], in_=rwT_ps[:], func=AF.Copy),
                 reads=[rwT_ps], writes=[rwT])
            for oc in range(2):
                pse = ex_ps[oc * 2]
                p.op("tensor", lambda e, oc=oc, pse=pse: e.matmul(out=pse[:], lhsT=rwT[:],
                                                                  rhs=beS[:, oc * 512:(oc + 1) * 512],
                                                                  start=True, stop=True),
                     reads=[rwT, beS], writes=[pse])
                p.op("vector", lambda e, oc=oc, pse=pse: e.tensor_copy(out=acc[0][:, oc * 512:(oc + 1) * 512],
                                                                       in_=pse[:]),
                     reads=[pse], writes=[(acc[0], oc)])
            prev = acc[0]
            for e_i in range(8):
                nxt = acc[1 + (e_i % 2)]
                for oc in range(2):
                    pse = ex_ps[oc * 2 + (e_i % 2)]
                    def emm(e, e_i=e_i, oc=oc, LT=LT, pse=pse):
                        last = None
                        for kb in range(8):
                            last = e.matmul(out=pse[:], lhsT=LT[:, kb, :],
                                            rhs=WeT[:, e_i, kb, oc * 512:(oc + 1) * 512],
                                            start=(kb == 0), stop=(kb == 7))
                        return last
                    p.op("tensor", emm, reads=[LT, WeT], writes=[pse])
                    p.op("vector", lambda e, oc=oc, pse=pse, prev=prev, nxt=nxt, e_i=e_i:
                         e.scalar_tensor_tensor(out=nxt[:, oc * 512:(oc + 1) * 512],
                                                in0=pse[:], scalar=rw[:, e_i:e_i + 1],
                                                in1=prev[:, oc * 512:(oc + 1) * 512],
                                                op0=ALU.mult, op1=ALU.add),
                         reads=[pse, rw, (prev, oc)], writes=[(nxt, oc)])
                prev = nxt
            OT = out_t[par]
            p.op("vector", lambda e, OT=OT, prev=prev, X=X:
                 e.scalar_tensor_tensor(out=OT[:], in0=prev[:], scalar=0.5, in1=X[:],
                                        op0=ALU.mult, op1=ALU.add),
                 reads=[prev, X], writes=[OT])
            tok = p.dma("gpsimd", x_out[b, ti * 128:(ti + 1) * 128, :], OT[:], reads=[OT])
            out_tokens.append(tok)

    return


def build_moe(tag="moe"):
    p = Prog(tag)
    io = dict(
        x=p.dram("x", [B, TLOC, D], F32, kind="ExternalInput").ap(),
        We=p.dram("We", [8, D, D], F32, kind="ExternalInput").ap(),
        be=p.dram("be", [8, D], F32, kind="ExternalInput").ap(),
        Wr=p.dram("Wr", [8, D], F32, kind="ExternalInput").ap(),
        br=p.dram("br", [8], F32, kind="ExternalInput").ap(),
        ident_in=p.dram("ident_in", [128, 128], F32, kind="ExternalInput").ap(),
        xo=p.dram("xo", [B, TLOC, D], F32, kind="ExternalOutput").ap(),
    )
    with p.stage("moe"):
        emit_moe(p, io)
    p.finalize()
    return p



CHUNK = 512

def emit_mfront(p, io):
    H = 3
    TE = TLOC + 2 * H
    NCH = TLOC // CHUNK
    x_in = io["x"]
    Win = {d_: io[f"Win_{d_}"] for d_ in "fb"}
    convw = {d_: io[f"convw_{d_}"] for d_ in "fb"}
    convb = {d_: io[f"convb_{d_}"] for d_ in "fb"}
    xproj = {d_: io[f"xproj_{d_}"] for d_ in "fb"}
    dtw = {d_: io[f"dtw_{d_}"] for d_ in "fb"}
    dtb = {d_: io[f"dtb_{d_}"] for d_ in "fb"}
    ident_in = io["ident_in"]
    uT_o = {d_: io[f"uT_{d_}"] for d_ in "fb"}
    zT_o = {d_: io[f"zT_{d_}"] for d_ in "fb"}
    dT_o = {d_: io[f"dT_{d_}"] for d_ in "fb"}
    duT_o = {d_: io[f"duT_{d_}"] for d_ in "fb"}
    bcT_o = {d_: io[f"bcT_{d_}"] for d_ in "fb"}

    ident = p.sbuf("ident", [128, 128], F32)
    identb = p.sbuf("identb", [128, 128], BF16)
    p.dma("sync", ident[:], ident_in, writes=[ident])
    p.op("vector", lambda e: e.tensor_copy(out=identb[:], in_=ident[:]), reads=[ident], writes=[identb])

    lnT = p.sbuf("lnT", [128, 8, TE + 2], BF16)
    WinT = p.sbuf("WinT", [128, 8, 2048], BF16)
    xprojT = p.sbuf("xprojT", [128, 16, 96], BF16)
    dtwT = p.sbuf("dtwT", [64, 2048], BF16)
    cwcol = p.sbuf("cwcol", [128, 16, 4], F32)
    cbcol = p.sbuf("cbcol", [128, 16], F32)
    dtbcol = p.sbuf("dtbcol", [128, 16], F32)
    xsT = p.sbuf("xsT", [128, 16, CHUNK + H], BF16)
    uT = p.sbuf("uTs", [128, 16, CHUNK], BF16)
    dTc = p.sbuf("dTc", [128, 16, CHUNK], BF16)
    duTc = p.sbuf("duTc", [128, 16, CHUNK], BF16)
    conv_a = p.sbuf("conv_a", [128, CHUNK], BF16)
    conv_b2 = p.sbuf("conv_b2", [128, CHUNK], BF16)
    conv_c = p.sbuf("conv_c", [128, CHUNK], BF16)
    conv_d2 = p.sbuf("conv_d2", [128, CHUNK], BF16)
    zev = [p.sbuf(f"zev{i}", [128, CHUNK], BF16) for i in range(2)]
    xdb = p.sbuf("xdb", [96, CHUNK], BF16)
    bcs = p.sbuf("bcs", [32, CHUNK], BF16)
    sp0 = p.sbuf("sp0", [128, CHUNK], F32)
    sp1 = p.sbuf("sp1", [128, CHUNK], F32)
    sp2 = p.sbuf("sp2", [128, CHUNK], F32)
    xt = [p.sbuf(f"xt{i}", [128, D], F32) for i in range(2)]
    sq = [p.sbuf(f"sq{i}", [128, D], F32) for i in range(2)]
    lnb = [p.sbuf(f"lnb{i}", [128, D], BF16) for i in range(2)]
    stats = [make_stat(p, f"st{i}") for i in range(2)]
    wld = [p.sbuf(f"wld{i}", [128, 1024], F32) for i in range(2)]
    wsm = p.sbuf("wsm", [96, 2048], F32)
    mm_ps = [p.psum(f"mm_ps{i}", [128, 512], F32) for i in range(4)]
    _tr0 = p.psum("tr_ps0", [128, 128], F32)
    tr_ps = [_tr0, _tr0]
    trb_ps = [p.psum(f"trb_ps{i}", [128, 128], BF16) for i in range(2)]
    xd_ps = p.psum("xd_ps", [96, 512], F32)

    def wprep_win(d_, half):
        for rb in range(16):
            w = wld[rb % 2]
            p.dma("sync", w[:], Win[d_][half * 2048 + rb * 128:half * 2048 + (rb + 1) * 128, :], writes=[w])
            for kb in range(8):
                tp = tr_ps[kb % 2]
                sl = w[:, kb * 128:(kb + 1) * 128]
                p.op("tensor", lambda e, sl=sl, tp=tp: e.transpose(out=tp[:], in_=sl, identity=ident[:]),
                     reads=[w, ident], writes=[tp])
                dsl = WinT[:, kb, rb * 128:(rb + 1) * 128]
                p.op("scalar", lambda e, dsl=dsl, tp=tp: e.activation(out=dsl, in_=tp[:], func=AF.Copy),
                     reads=[tp], writes=[(WinT, rb)])
    def wprep_dir(d_):
        p.dma("sync", wsm[:], xproj[d_], writes=[wsm])
        for kb in range(16):
            tp = tr_ps[kb % 2]
            sl = wsm[:, kb * 128:(kb + 1) * 128]
            p.op("tensor", lambda e, sl=sl, tp=tp: e.transpose(out=tp[:, 0:96], in_=sl,
                                                               identity=ident[0:96, 0:96]),
                 reads=[wsm, ident], writes=[tp])
            dsl = xprojT[:, kb, :]
            p.op("scalar", lambda e, dsl=dsl, tp=tp: e.activation(out=dsl, in_=tp[:, 0:96], func=AF.Copy),
                 reads=[tp], writes=[(xprojT, kb)])
        for rb in range(16):
            w = wld[rb % 2]
            p.dma("sync", w[:, 0:64], dtw[d_][rb * 128:(rb + 1) * 128, :], writes=[w])
            tp = tr_ps[rb % 2]
            p.op("tensor", lambda e, w=w, tp=tp: e.transpose(out=tp[0:64, 0:128], in_=w[:, 0:64],
                                                             identity=ident[:]),
                 reads=[w, ident], writes=[tp])
            dsl = dtwT[:, rb * 128:(rb + 1) * 128]
            p.op("scalar", lambda e, dsl=dsl, tp=tp: e.activation(out=dsl, in_=tp[0:64, 0:128], func=AF.Copy),
                 reads=[tp], writes=[(dtwT, rb)])
        p.dma("sync", cwcol[:], convw[d_], writes=[cwcol])
        p.dma("sync", cbcol[:], convb[d_], writes=[cbcol])
        p.dma("sync", dtbcol[:], dtb[d_], writes=[dtbcol])

    def phase1_b(b):
        def do_tile(tok0, ntok, col0):
            par = (tok0 // 128) % 2
            X, SQ, LNB, st = xt[par], sq[par], lnb[par], stats[par]
            p.dma("sync", X[0:ntok, :], x_in[b, tok0:tok0 + ntok, :], writes=[X])
            ln_stats_n(p, X, SQ, st, ntok)
            p.op("scalar", lambda e, X=X, LNB=LNB, st=st, ntok=ntok:
                 e.activation(out=LNB[0:ntok, :], in_=X[0:ntok, :], func=AF.Identity,
                              bias=st["negmur"][0:ntok, 0:1], scale=st["rstd"][0:ntok, 0:1]),
                 reads=[X, st["negmur"], st["rstd"]], writes=[LNB])
            for kb in range(8):
                tp = trb_ps[kb % 2]
                sl = LNB[0:ntok, kb * 128:(kb + 1) * 128]
                idb = identb[0:ntok, 0:ntok]
                p.op("tensor", lambda e, sl=sl, tp=tp, ntok=ntok, idb=idb:
                     e.transpose(out=tp[:, 0:ntok], in_=sl, identity=idb),
                     reads=[LNB, identb], writes=[tp])
                dsl = lnT[:, kb, col0:col0 + ntok]
                p.op("scalar", lambda e, dsl=dsl, tp=tp, ntok=ntok:
                     e.activation(out=dsl, in_=tp[:, 0:ntok], func=AF.Copy),
                     reads=[tp], writes=[(lnT, (kb, col0))])
        do_tile(0, H, 0)
        for ti in range(NT):
            do_tile(H + ti * 128, 128, H + ti * 128)
        do_tile(H + TLOC, H, H + TLOC)

    def inproj_cols(b, col0, n, ech_list, dst, dst_off, psum_sel):
        for i, ech in enumerate(ech_list):
            ps = mm_ps[psum_sel(i)]
            def f(e, ech=ech, ps=ps, col0=col0, n=n):
                last = None
                for kb in range(8):
                    last = e.matmul(out=ps[:, 0:n], lhsT=WinT[:, kb, ech * 128:(ech + 1) * 128],
                                    rhs=lnT[:, kb, col0:col0 + n], start=(kb == 0), stop=(kb == 7))
                return last
            p.op("tensor", f, reads=[WinT, lnT], writes=[ps])
            yield i, ech, ps

    out_sems = []

    for b in range(B):
        phase1_b(b)
        for d_ in "fb":
            wprep_dir(d_)
            wprep_win(d_, 0)
            fwd = (d_ == "f")
            chunk_order = range(NCH) if fwd else range(NCH - 1, -1, -1)
            first = True
            for ci in chunk_order:
                c0 = ci * CHUNK
                if fwd:
                    if first:
                        for i, ech, ps in inproj_cols(b, c0, H, range(16), xsT, 0, lambda i: i % 2):
                            dsl = xsT[:, ech, 0:H]
                            p.op("vector", lambda e, dsl=dsl, ps=ps: e.tensor_copy(out=dsl, in_=ps[:, 0:H]),
                                 reads=[ps], writes=[(xsT, ("h", ech))])
                    else:
                        p.op("vector", lambda e: e.tensor_copy(out=xsT[:, :, 0:H],
                                                               in_=xsT[:, :, CHUNK:CHUNK + H]),
                             reads=[xsT], writes=[xsT])
                    body_off = H
                else:
                    if first:
                        for i, ech, ps in inproj_cols(b, c0 + CHUNK + H, H, range(16), xsT, 0, lambda i: i % 2):
                            dsl = xsT[:, ech, CHUNK:CHUNK + H]
                            p.op("vector", lambda e, dsl=dsl, ps=ps: e.tensor_copy(out=dsl, in_=ps[:, 0:H]),
                                 reads=[ps], writes=[(xsT, ("h", ech))])
                    else:
                        p.op("vector", lambda e: e.tensor_copy(out=xsT[:, :, CHUNK:CHUNK + H],
                                                               in_=xsT[:, :, 0:H]),
                             reads=[xsT], writes=[xsT])
                    body_off = 0
                first = False
                for i, ech, ps in inproj_cols(b, H + c0, CHUNK, range(16), xsT, body_off, lambda i: i % 2):
                    dsl = xsT[:, ech, body_off:body_off + CHUNK]
                    p.op("vector", lambda e, dsl=dsl, ps=ps: e.tensor_copy(out=dsl, in_=ps[:]),
                         reads=[ps], writes=[(xsT, ("b", ech))])
                bufs = [conv_a, conv_b2, conv_c, conv_d2]
                for kb in range(16):
                    ks = [0, 1, 2, 3] if fwd else [3, 2, 1, 0]
                    p.op("vector", lambda e, kb=kb, ks=ks, b0=bufs[0]: e.tensor_scalar(
                        out=b0[:], in0=xsT[:, kb, ks[0]:ks[0] + CHUNK],
                        scalar1=cwcol[:, kb, 0:1], scalar2=None, op0=ALU.mult),
                         reads=[xsT, cwcol], writes=[bufs[0]])
                    for j in range(1, 4):
                        p.op("vector", lambda e, kb=kb, j=j, ks=ks, bj=bufs[j], bp=bufs[j - 1]:
                             e.scalar_tensor_tensor(
                                 out=bj[:], in0=xsT[:, kb, ks[j]:ks[j] + CHUNK],
                                 scalar=cwcol[:, kb, j:j + 1], in1=bp[:],
                                 op0=ALU.mult, op1=ALU.add),
                             reads=[xsT, cwcol, bufs[j - 1]], writes=[bufs[j]])
                    p.op("scalar", lambda e, kb=kb: e.activation(
                        out=uT[:, kb, :], in_=conv_d2[:], func=AF.Silu, bias=cbcol[:, kb:kb + 1]),
                         reads=[conv_d2, cbcol], writes=[(uT, kb)])
                    p.dma("gpsimd", uT_o[d_][b, kb * 128:(kb + 1) * 128, c0:c0 + CHUNK], uT[:, kb, :],
                          reads=[(uT, kb)], sem=f"ust{kb}")
                def xp(e):
                    last = None
                    for kb in range(16):
                        last = e.matmul(out=xd_ps[:], lhsT=xprojT[:, kb, :], rhs=uT[:, kb, :],
                                        start=(kb == 0), stop=(kb == 15))
                    return last
                p.op("tensor", xp, reads=[uT, xprojT], writes=[xd_ps])
                p.op("vector", lambda e: e.tensor_copy(out=xdb[:], in_=xd_ps[:]),
                     reads=[xd_ps], writes=[xdb])
                p.op("vector", lambda e: e.tensor_copy(out=bcs[:], in_=xd_ps[64:96, :]),
                     reads=[xd_ps], writes=[bcs])
                p.dma("gpsimd", bcT_o[d_][b, :, c0:c0 + CHUNK], bcs[:], reads=[bcs], sem="bcst")
                for dch in range(16):
                    ps = mm_ps[dch % 2]
                    p.op("tensor", lambda e, dch=dch, ps=ps: e.matmul(
                        out=ps[:], lhsT=dtwT[:, dch * 128:(dch + 1) * 128], rhs=xdb[0:64, :],
                        start=True, stop=True),
                         reads=[dtwT, xdb], writes=[ps])
                    p.op("scalar", lambda e, ps=ps, dch=dch: e.activation(
                        out=sp0[:], in_=ps[:], func=AF.Identity, bias=dtbcol[:, dch:dch + 1]),
                         reads=[ps, dtbcol], writes=[sp0])
                    p.op("scalar", lambda e: e.activation(out=sp1[:], in_=sp0[:], func=AF.Abs),
                         reads=[sp0], writes=[sp1])
                    p.op("scalar", lambda e: e.activation(out=sp2[:], in_=sp1[:], func=AF.Exp, scale=-1.0),
                         reads=[sp1], writes=[sp2])
                    p.op("scalar", lambda e: e.activation(out=sp1[:], in_=sp2[:], func=AF.Ln, bias=1.0),
                         reads=[sp2], writes=[sp1])
                    p.op("vector", lambda e: e.tensor_scalar(out=sp2[:], in0=sp0[:], scalar1=0.0,
                                                             scalar2=None, op0=ALU.max),
                         reads=[sp0], writes=[sp2])
                    p.op("vector", lambda e, dch=dch: e.tensor_add(out=dTc[:, dch, :], in0=sp2[:], in1=sp1[:]),
                         reads=[sp2, sp1], writes=[(dTc, dch)])
                    p.op("vector", lambda e, dch=dch: e.tensor_mul(out=duTc[:, dch, :], in0=dTc[:, dch, :],
                                                                   in1=uT[:, dch, :]),
                         reads=[(dTc, dch), (uT, dch)], writes=[(duTc, dch)])
                    p.dma("gpsimd", dT_o[d_][b, dch * 128:(dch + 1) * 128, c0:c0 + CHUNK], dTc[:, dch, :],
                          reads=[(dTc, dch)], sem=f"dst{dch}")
                    p.dma("gpsimd", duT_o[d_][b, dch * 128:(dch + 1) * 128, c0:c0 + CHUNK], duTc[:, dch, :],
                          reads=[(duTc, dch)], sem=f"dust{dch}")
            wprep_win(d_, 1)
            for ci in range(NCH):
                c0 = ci * CHUNK
                for i, ech, ps in inproj_cols(b, H + c0, CHUNK, range(16), None, 0, lambda i: 2 + i % 2):
                    zv = zev[i % 2]
                    p.op("vector", lambda e, zv=zv, ps=ps: e.tensor_copy(out=zv[:], in_=ps[:]),
                         reads=[ps], writes=[zv])
                    p.dma("gpsimd", zT_o[d_][b, ech * 128:(ech + 1) * 128, c0:c0 + CHUNK], zv[:],
                          reads=[zv], sem=f"zst{i % 2}")
    return


def _mfront_io(p, kind_in="ExternalInput", kind_out="ExternalOutput", pre=""):
    io = dict(
        x=p.dram(pre + "x", [B, TLOC + 6, D], F32, kind=kind_in).ap(),
        ident_in=p.dram(pre + "ident_in", [128, 128], F32, kind="ExternalInput").ap()
        if pre else p.dram("ident_in", [128, 128], F32, kind="ExternalInput").ap(),
    )
    for d_ in "fb":
        io[f"Win_{d_}"] = p.dram(pre + f"Win_{d_}", [4096, D], F32, kind="ExternalInput").ap()
        io[f"convw_{d_}"] = p.dram(pre + f"convw_{d_}", [128, 16, 4], F32, kind="ExternalInput").ap()
        io[f"convb_{d_}"] = p.dram(pre + f"convb_{d_}", [128, 16], F32, kind="ExternalInput").ap()
        io[f"xproj_{d_}"] = p.dram(pre + f"xproj_{d_}", [96, 2048], F32, kind="ExternalInput").ap()
        io[f"dtw_{d_}"] = p.dram(pre + f"dtw_{d_}", [2048, 64], F32, kind="ExternalInput").ap()
        io[f"dtb_{d_}"] = p.dram(pre + f"dtb_{d_}", [128, 16], F32, kind="ExternalInput").ap()
        for nm in ("uT", "zT", "dT", "duT"):
            io[f"{nm}_{d_}"] = p.dram(pre + f"{nm}_{d_}", [B, 2048, TLOC], BF16, kind=kind_out).ap()
        io[f"bcT_{d_}"] = p.dram(pre + f"bcT_{d_}", [B, 32, TLOC], BF16, kind=kind_out).ap()
    return io


def build_mamba_front(tag="mfront"):
    p = Prog(tag)
    io = _mfront_io(p)
    with p.stage("mf"):
        emit_mfront(p, io)
    p.finalize()
    return p


def ln_stats_n(p, x_tile, sq_scratch, stat, ntok, n_feat=D, eps=1e-5):
    v, s = "vector", "scalar"
    p.op(v, lambda e: e.reduce_sum(out=stat["sum"][0:ntok, :], in_=x_tile[0:ntok, :], axis=AX.X),
         reads=[x_tile], writes=[stat["sum"]])
    p.op(s, lambda e: e.activation(out=sq_scratch[0:ntok, :], in_=x_tile[0:ntok, :], func=AF.Square,
                                   accum_out=stat["sumsq"][0:ntok, :]),
         reads=[x_tile], writes=[sq_scratch, stat["sumsq"]])
    p.op(v, lambda e: e.tensor_scalar(out=stat["mean"][0:ntok, :], in0=stat["sum"][0:ntok, :],
                                      scalar1=1.0 / n_feat, scalar2=None, op0=ALU.mult),
         reads=[stat["sum"]], writes=[stat["mean"]])
    p.op(v, lambda e: e.tensor_mul(out=stat["m2"][0:ntok, :], in0=stat["mean"][0:ntok, :],
                                   in1=stat["mean"][0:ntok, :]),
         reads=[stat["mean"]], writes=[stat["m2"]])
    p.op(v, lambda e: e.tensor_scalar(out=stat["nm2"][0:ntok, :], in0=stat["m2"][0:ntok, :], scalar1=-1.0,
                                      scalar2=None, op0=ALU.mult),
         reads=[stat["m2"]], writes=[stat["nm2"]])
    p.op(v, lambda e: e.scalar_tensor_tensor(out=stat["var"][0:ntok, :], in0=stat["sumsq"][0:ntok, :],
                                             scalar=1.0 / n_feat, in1=stat["nm2"][0:ntok, :],
                                             op0=ALU.mult, op1=ALU.add),
         reads=[stat["sumsq"], stat["nm2"]], writes=[stat["var"]])
    p.op(v, lambda e: e.tensor_scalar(out=stat["vare"][0:ntok, :], in0=stat["var"][0:ntok, :], scalar1=eps,
                                      scalar2=None, op0=ALU.add),
         reads=[stat["var"]], writes=[stat["vare"]])
    p.op(s, lambda e: e.activation(out=stat["std"][0:ntok, :], in_=stat["vare"][0:ntok, :], func=AF.Sqrt),
         reads=[stat["vare"]], writes=[stat["std"]])
    p.op(v, lambda e: e.reciprocal(out=stat["rstd"][0:ntok, :], in_=stat["std"][0:ntok, :]),
         reads=[stat["std"]], writes=[stat["rstd"]])
    p.op(v, lambda e: e.tensor_mul(out=stat["mr"][0:ntok, :], in0=stat["mean"][0:ntok, :],
                                   in1=stat["rstd"][0:ntok, :]),
         reads=[stat["mean"], stat["rstd"]], writes=[stat["mr"]])
    p.op(v, lambda e: e.tensor_scalar(out=stat["negmur"][0:ntok, :], in0=stat["mr"][0:ntok, :], scalar1=-1.0,
                                      scalar2=None, op0=ALU.mult),
         reads=[stat["mr"]], writes=[stat["negmur"]])



SCH = 512

def emit_scan(p, io, T=S):
    NCHS = T // SCH
    dT_i = io["dT"]; duT_i = io["duT"]; bcT_i = io["bcT"]
    alog_i = io["alog"]; sel_i = io["sel"]; yT_o = io["yT"]

    SEL = p.sbuf("SEL", [32, 32, 128], BF16)
    p.dma("sync", SEL[:], sel_i, writes=[SEL])
    acols = p.sbuf("acols", [128, 8, 16], F32)
    atmp = p.sbuf("atmp", [128, 16], F32)
    atmp2 = p.sbuf("atmp2", [128, 16], F32)
    for db in range(8):
        p.dma("sync", atmp[:], alog_i[db], writes=[atmp])
        p.op("scalar", lambda e: e.activation(out=atmp2[:], in_=atmp[:], func=AF.Exp),
             reads=[atmp], writes=[atmp2])
        p.op("vector", lambda e, db=db: e.tensor_scalar(out=acols[:, db, :], in0=atmp2[:], scalar1=-1.0,
                                                        scalar2=None, op0=ALU.mult),
             reads=[atmp2], writes=[(acols, db)])

    bcc = p.sbuf("bcc", [32, SCH], BF16)
    Bb = p.sbuf("Bb", [128, 16, SCH], BF16)
    Cb = p.sbuf("Cb", [128, 16, SCH], BF16)
    bc_ps = [p.psum(f"bc_ps{i}", [128, SCH], F32) for i in range(2)]
    dA = [p.sbuf(f"dA{i}", [128, 16, SCH], F32) for i in range(2)]
    dBu = [p.sbuf(f"dBu{i}", [128, 16, SCH], BF16) for i in range(2)]
    hb = p.sbuf("hb", [128, 16, SCH], BF16)
    ta = p.sbuf("ta", [128, 8, SCH], BF16)
    tb = p.sbuf("tb", [128, 4, SCH], BF16)
    tc = p.sbuf("tc", [128, 2, SCH], BF16)
    yv = p.sbuf("yv", [128, SCH], BF16)
    dTt = [p.sbuf(f"dTt{i}", [128, SCH], BF16) for i in range(2)]
    duTt = [p.sbuf(f"duTt{i}", [128, SCH], BF16) for i in range(2)]
    carry = p.sbuf("carry", [128, 8, 16], F32)
    p.op("vector", lambda e: e.memset(carry[:], 0.0), writes=[carry])

    for ci in range(NCHS):
        c0 = ci * SCH
        p.dma("sync", bcc[:], bcT_i[:, c0:c0 + SCH], writes=[bcc])
        for s in range(16):
            ps = bc_ps[s % 2]
            p.op("tensor", lambda e, s=s, ps=ps: e.matmul(out=ps[:], lhsT=SEL[:, s, :], rhs=bcc[:],
                                                          start=True, stop=True),
                 reads=[SEL, bcc], writes=[ps])
            p.op("scalar", lambda e, s=s, ps=ps: e.activation(out=Bb[:, s, :], in_=ps[:], func=AF.Copy),
                 reads=[ps], writes=[(Bb, s)])
            ps2 = bc_ps[(s + 1) % 2]
            p.op("tensor", lambda e, s=s, ps2=ps2: e.matmul(out=ps2[:], lhsT=SEL[:, 16 + s, :], rhs=bcc[:],
                                                            start=True, stop=True),
                 reads=[SEL, bcc], writes=[ps2])
            p.op("scalar", lambda e, s=s, ps2=ps2: e.activation(out=Cb[:, s, :], in_=ps2[:], func=AF.Copy),
                 reads=[ps2], writes=[(Cb, s)])
        for db in range(8):
            par = db % 2
            dt_, du_ = dTt[par], duTt[par]
            p.dma("sync", dt_[:], dT_i[db * 128:(db + 1) * 128, c0:c0 + SCH], writes=[dt_])
            p.dma("sync", du_[:], duT_i[db * 128:(db + 1) * 128, c0:c0 + SCH], writes=[du_])
            DA, DB = dA[par], dBu[par]
            for s in range(16):
                p.op("scalar", lambda e, s=s, db=db, DA=DA, dt_=dt_: e.activation(
                    out=DA[:, s, :], in_=dt_[:], func=AF.Exp, scale=acols[:, db, s:s + 1]),
                     reads=[dt_, (acols, db)], writes=[(DA, s)])
                p.op("vector", lambda e, s=s, DB=DB, du_=du_: e.tensor_mul(
                    out=DB[:, s, :], in0=du_[:], in1=Bb[:, s, :]),
                     reads=[du_, (Bb, s)], writes=[(DB, s)])
            for s in range(16):
                p.op("vector", lambda e, s=s, db=db, DA=DA, DB=DB: e.tensor_tensor_scan(
                    out=hb[:, s, :], data0=DA[:, s, :], data1=DB[:, s, :],
                    initial=carry[:, db, s:s + 1], op0=ALU.mult, op1=ALU.add),
                     reads=[(DA, s), (DB, s), (carry, db)], writes=[(hb, s)])
            p.op("vector", lambda e, db=db: e.tensor_copy(
                out=carry[:, db, :], in_=bass.AP(hb[:].tensor, hb[:].offset + SCH - 1,
                                                 [[hb[:].ap[0][0], 128], [SCH, 16]])),
                 reads=[hb], writes=[(carry, db)])
            p.op("vector", lambda e, DB=DB: e.tensor_mul(out=DB[:], in0=hb[:], in1=Cb[:]),
                 reads=[hb, Cb], writes=[DB])
            p.op("vector", lambda e, DB=DB: e.tensor_add(out=ta[:], in0=DB[:, 0:8, :], in1=DB[:, 8:16, :]),
                 reads=[DB], writes=[ta])
            p.op("vector", lambda e: e.tensor_add(out=tb[:], in0=ta[:, 0:4, :], in1=ta[:, 4:8, :]),
                 reads=[ta], writes=[tb])
            p.op("vector", lambda e: e.tensor_add(out=tc[:], in0=tb[:, 0:2, :], in1=tb[:, 2:4, :]),
                 reads=[tb], writes=[tc])
            p.op("vector", lambda e: e.tensor_add(out=yv[:], in0=tc[:, 0, :], in1=tc[:, 1, :]),
                 reads=[tc], writes=[yv])
            p.dma("gpsimd", yT_o[db * 128:(db + 1) * 128, c0:c0 + SCH], yv[:], reads=[yv], sem=f"yst{par}")
    return


def build_scan(T=S, tag="scan"):
    p = Prog(tag)
    io = dict(
        dT=p.dram("dT", [1024, T], BF16, kind="ExternalInput").ap(),
        duT=p.dram("duT", [1024, T], BF16, kind="ExternalInput").ap(),
        bcT=p.dram("bcT", [32, T], BF16, kind="ExternalInput").ap(),
        alog=p.dram("alog", [8, 128, 16], F32, kind="ExternalInput").ap(),
        sel=p.dram("sel", [32, 32, 128], BF16, kind="ExternalInput").ap(),
        yT=p.dram("yT", [1024, T], BF16, kind="ExternalOutput").ap(),
    )
    with p.stage("sc"):
        emit_scan(p, io, T=T)
    p.finalize()
    return p



def emit_mback(p, io, yT_colf=None):
    NCH = TLOC // CHUNK
    x_in = io["x"]
    yT_i = {d_: io.get(f"yT_{d_}") for d_ in "fb"}
    uT_i = {d_: io[f"uT_{d_}"] for d_ in "fb"}
    zT_i = {d_: io[f"zT_{d_}"] for d_ in "fb"}
    Dv = {d_: io[f"Dv_{d_}"] for d_ in "fb"}
    outw = {d_: io[f"outw_{d_}"] for d_ in "fb"}
    fw = io["fw"]; fb_ = io["fb"]; ident_in = io["ident_in"]; x_out = io["xo"]

    ident = p.sbuf("ident", [128, 128], F32)
    p.dma("sync", ident[:], ident_in, writes=[ident])
    ones_row = p.sbuf("ones_row", [1, 128], BF16)
    p.op("vector", lambda e: e.memset(ones_row[:], 1.0), writes=[ones_row])
    fbS = p.sbuf("fbS", [1, 1024], BF16)
    fbf = p.sbuf("fbf", [1, 1024], F32)
    p.dma("sync", fbf[:], fb_, writes=[fbf])
    p.op("vector", lambda e: e.tensor_copy(out=fbS[:], in_=fbf[:]), reads=[fbf], writes=[fbS])
    Dcol = {}
    for d_ in "fb":
        Dcol[d_] = p.sbuf(f"Dcol{d_}", [128, 16], F32)
        p.dma("sync", Dcol[d_][:], Dv[d_], writes=[Dcol[d_]])

    owT = {d_: p.sbuf(f"owT{d_}", [128, 16, 1024], BF16) for d_ in "fb"}
    fwT = p.sbuf("fwT", [128, 16, 1024], BF16)
    wld = p.sbuf("wld", [128, 2048], F32)
    tr_ps = p.psum("tr_ps", [128, 128], F32)
    def prep_T(dst, W, nrow_blocks, ncol_blocks):
        for rb in range(nrow_blocks):
            p.dma("sync", wld[:, 0:ncol_blocks * 128], W[rb * 128:(rb + 1) * 128, :], writes=[wld])
            for kb in range(ncol_blocks):
                sl = wld[:, kb * 128:(kb + 1) * 128]
                p.op("tensor", lambda e, sl=sl: e.transpose(out=tr_ps[:], in_=sl, identity=ident[:]),
                     reads=[wld, ident], writes=[tr_ps])
                dsl = dst[:, kb, rb * 128:(rb + 1) * 128]
                p.op("scalar", lambda e, dsl=dsl: e.activation(out=dsl, in_=tr_ps[:], func=AF.Copy),
                     reads=[tr_ps], writes=[(dst, (kb, rb))])
    for d_ in "fb":
        prep_T(owT[d_], outw[d_], 8, 16)
    prep_T(fwT, fw, 8, 16)

    yt = p.sbuf("yt", [128, CHUNK], BF16)
    ut = p.sbuf("ut", [128, CHUNK], BF16)
    zt = p.sbuf("zt", [128, CHUNK], BF16)
    sz = p.sbuf("sz", [128, CHUNK], F32)
    g1 = p.sbuf("g1", [128, CHUNK], F32)
    yg = p.sbuf("yg", [128, 16, CHUNK], BF16)
    oT = p.sbuf("oT", [128, 16, CHUNK], BF16)
    xr = [p.sbuf(f"xr{i}", [128, D], F32) for i in range(2)]
    xo_t = [p.sbuf(f"xo_t{i}", [128, D], F32) for i in range(2)]
    mm_ps = [p.psum(f"mm_ps{i}", [128, 512], F32) for i in range(4)]

    for b in range(B):
        for ci in range(NCH):
            c0 = ci * CHUNK
            for di, d_ in enumerate("fb"):
                for kb in range(16):
                    ysrc = (yT_colf(d_, b, kb, c0) if yT_colf is not None
                            else yT_i[d_][b, kb * 128:(kb + 1) * 128, c0:c0 + CHUNK])
                    p.dma("sync", yt[:], ysrc, writes=[yt])
                    p.dma("sync", ut[:], uT_i[d_][b, kb * 128:(kb + 1) * 128, c0:c0 + CHUNK], writes=[ut])
                    p.dma("sync", zt[:], zT_i[d_][b, kb * 128:(kb + 1) * 128, c0:c0 + CHUNK], writes=[zt])
                    p.op("scalar", lambda e: e.activation(out=sz[:], in_=zt[:], func=AF.Silu),
                         reads=[zt], writes=[sz])
                    p.op("vector", lambda e, kb=kb, d_=d_: e.scalar_tensor_tensor(
                        out=g1[:], in0=ut[:], scalar=Dcol[d_][:, kb:kb + 1], in1=yt[:],
                        op0=ALU.mult, op1=ALU.add),
                         reads=[ut, Dcol[d_], yt], writes=[g1])
                    p.op("vector", lambda e, kb=kb: e.tensor_mul(out=yg[:, kb, :], in0=g1[:], in1=sz[:]),
                         reads=[g1, sz], writes=[(yg, kb)])
                for och in range(8):
                    ps = mm_ps[och % 2]
                    def f(e, och=och, d_=d_, ps=ps):
                        last = None
                        for kb in range(16):
                            last = e.matmul(out=ps[:], lhsT=owT[d_][:, kb, och * 128:(och + 1) * 128],
                                            rhs=yg[:, kb, :], start=(kb == 0), stop=(kb == 15))
                        return last
                    p.op("tensor", f, reads=[yg, owT[d_]], writes=[ps])
                    p.op("scalar", lambda e, och=och, di=di, ps=ps: e.activation(
                        out=oT[:, di * 8 + och, :], in_=ps[:], func=AF.Copy),
                         reads=[ps], writes=[(oT, di * 8 + och)])
            for tt in range(CHUNK // 128):
                X = xr[tt % 2]
                XO = xo_t[tt % 2]
                p.dma("sync", X[:], x_in[b, c0 + tt * 128:c0 + (tt + 1) * 128, :], writes=[X])
                for oc in range(2):
                    ps = mm_ps[2 + oc]
                    def f2(e, tt=tt, oc=oc, ps=ps):
                        e.matmul(out=ps[:], lhsT=ones_row[:], rhs=fbS[:, oc * 512:(oc + 1) * 512],
                                 start=True, stop=False)
                        last = None
                        for kb in range(16):
                            last = e.matmul(out=ps[:], lhsT=oT[:, kb, tt * 128:(tt + 1) * 128],
                                            rhs=fwT[:, kb, oc * 512:(oc + 1) * 512],
                                            start=False, stop=(kb == 15))
                        return last
                    p.op("tensor", f2, reads=[oT, fwT, ones_row, fbS], writes=[ps])
                    p.op("vector", lambda e, oc=oc, ps=ps, X=X, XO=XO: e.tensor_add(
                        out=XO[:, oc * 512:(oc + 1) * 512], in0=ps[:], in1=X[:, oc * 512:(oc + 1) * 512]),
                         reads=[ps, X], writes=[(XO, oc)])
                p.dma("gpsimd", x_out[b, c0 + tt * 128:c0 + (tt + 1) * 128, :], XO[:],
                      reads=[XO], sem=f"xst{tt % 2}")
    return


def _mback_w_io(p):
    io = dict(
        fw=p.dram("fw", [1024, 2048], F32, kind="ExternalInput").ap(),
        fb=p.dram("fb", [1, 1024], F32, kind="ExternalInput").ap(),
    )
    for d_ in "fb":
        io[f"Dv_{d_}"] = p.dram(f"Dv_{d_}", [128, 16], F32, kind="ExternalInput").ap()
        io[f"outw_{d_}"] = p.dram(f"outw_{d_}", [1024, 2048], F32, kind="ExternalInput").ap()
    return io


def build_mamba_back(tag="mback"):
    p = Prog(tag)
    io = _mback_w_io(p)
    io["x"] = p.dram("x", [B, TLOC, D], F32, kind="ExternalInput").ap()
    io["ident_in"] = p.dram("ident_in", [128, 128], F32, kind="ExternalInput").ap()
    io["xo"] = p.dram("xo", [B, TLOC, D], F32, kind="ExternalOutput").ap()
    for d_ in "fb":
        for nm in ("yT", "uT", "zT"):
            io[f"{nm}_{d_}"] = p.dram(f"{nm}_{d_}", [B, 2048, TLOC], BF16, kind="ExternalInput").ap()
    with p.stage("mb"):
        emit_mback(p, io)
    p.finalize()
    return p



def emit_l1ffn(p, io):
    NCH = TLOC // CHUNK
    x_in = io["x"]
    Wr_i = io["Wr"]; Wk_i = io["Wk"]; Wv_i = io["Wv"]
    mur_i = io["mur"]; muk_i = io["muk"]
    ident_in = io["ident_in"]; x_out = io["xo"]

    ident = p.sbuf("ident", [128, 128], F32)
    identb = p.sbuf("identb", [128, 128], BF16)
    p.dma("sync", ident[:], ident_in, writes=[ident])
    p.op("vector", lambda e: e.tensor_copy(out=identb[:], in_=ident[:]), reads=[ident], writes=[identb])
    mur = p.sbuf("mur_s", [128, 8], F32)
    muk = p.sbuf("muk_s", [128, 8], F32)
    murm = p.sbuf("murm", [128, 8], F32)
    mukm = p.sbuf("mukm", [128, 8], F32)
    p.dma("sync", mur[:], mur_i, writes=[mur])
    p.dma("sync", muk[:], muk_i, writes=[muk])
    for src_, dst_ in ((mur, murm), (muk, mukm)):
        p.op("vector", lambda e, src_=src_, dst_=dst_: e.tensor_scalar(
            out=dst_[:], in0=src_[:], scalar1=-1.0, scalar2=1.0, op0=ALU.mult, op1=ALU.add),
             reads=[src_], writes=[dst_])

    WT = {n: p.sbuf(f"WT{n}", [128, 8, D], BF16) for n in "rkv"}
    wld = p.sbuf("wld", [128, D], F32)
    tr_ps = p.psum("tr_ps", [128, 128], F32)
    for n, W in (("r", Wr_i), ("k", Wk_i), ("v", Wv_i)):
        for rb in range(8):
            p.dma("sync", wld[:], W[rb * 128:(rb + 1) * 128, :], writes=[wld])
            for kb in range(8):
                sl = wld[:, kb * 128:(kb + 1) * 128]
                p.op("tensor", lambda e, sl=sl: e.transpose(out=tr_ps[:], in_=sl, identity=ident[:]),
                     reads=[wld, ident], writes=[tr_ps])
                dsl = WT[n][:, kb, rb * 128:(rb + 1) * 128]
                p.op("scalar", lambda e, dsl=dsl: e.activation(out=dsl, in_=tr_ps[:], func=AF.Copy),
                     reads=[tr_ps], writes=[(WT[n], (kb, rb))])

    xt = [p.sbuf(f"xt{i}", [128, D], F32) for i in range(2)]
    sq = [p.sbuf(f"sq{i}", [128, D], F32) for i in range(2)]
    lnb = [p.sbuf(f"lnb{i}", [128, D], BF16) for i in range(2)]
    stats = [make_stat(p, f"st{i}") for i in range(2)]
    trb_ps = [p.psum(f"trb_ps{i}", [128, 128], BF16) for i in range(2)]
    lnT = p.sbuf("lnT", [128, 8, CHUNK + 1], BF16)
    xrT = p.sbuf("xrT", [128, 8, CHUNK], BF16)
    xkT = p.sbuf("xkT", [128, 8, CHUNK], BF16)
    t1 = p.sbuf("t1", [128, CHUNK], BF16)
    kaT = p.sbuf("kaT", [128, 8, CHUNK], BF16)
    krel = p.sbuf("krel", [128, CHUNK], F32)
    rS = p.sbuf("rS", [128, 1024], F32)
    xc = [p.sbuf(f"xc{i}", [128, D], F32) for i in range(2)]
    vo = p.sbuf("vo", [128, D], F32)
    xo_t = [p.sbuf(f"xo_t{i}", [128, D], F32) for i in range(2)]
    mm_ps = [p.psum(f"mm_ps{i}", [128, 512], F32) for i in range(4)]

    for b in range(B):
        for ci in range(NCH):
            c0 = ci * CHUNK
            if ci == 0:
                X = xt[0]
                p.dma("sync", X[0:1, :], x_in[b, 0:1, :], writes=[X])
                st0 = stats[0]
                ln_stats_n(p, X, sq[0], st0, 1)
                p.op("scalar", lambda e, X=X, st0=st0: e.activation(
                    out=lnb[0][0:1, :], in_=X[0:1, :], func=AF.Identity,
                    bias=st0["negmur"][0:1, 0:1], scale=st0["rstd"][0:1, 0:1]),
                     reads=[X, st0["negmur"], st0["rstd"]], writes=[lnb[0]])
                for kb in range(8):
                    tp = trb_ps[kb % 2]
                    p.op("tensor", lambda e, kb=kb, tp=tp: e.transpose(
                        out=tp[:, 0:1], in_=lnb[0][0:1, kb * 128:(kb + 1) * 128], identity=identb[0:1, 0:1]),
                         reads=[lnb[0], identb], writes=[tp])
                    p.op("scalar", lambda e, kb=kb, tp=tp: e.activation(
                        out=lnT[:, kb, 0:1], in_=tp[:, 0:1], func=AF.Copy),
                         reads=[tp], writes=[(lnT, ("h", kb))])
            else:
                p.op("vector", lambda e: e.tensor_copy(out=lnT[:, :, 0:1], in_=lnT[:, :, CHUNK:CHUNK + 1]),
                     reads=[lnT], writes=[lnT])
            for tt in range(CHUNK // 128):
                par = tt % 2
                X, SQ, LNB, st = xt[par], sq[par], lnb[par], stats[par]
                p.dma("sync", X[:], x_in[b, 1 + c0 + tt * 128:1 + c0 + (tt + 1) * 128, :], writes=[X])
                ln_stats_n(p, X, SQ, st, 128)
                p.op("scalar", lambda e, X=X, LNB=LNB, st=st: e.activation(
                    out=LNB[:], in_=X[:], func=AF.Identity,
                    bias=st["negmur"][:, 0:1], scale=st["rstd"][:, 0:1]),
                     reads=[X, st["negmur"], st["rstd"]], writes=[LNB])
                for kb in range(8):
                    tp = trb_ps[kb % 2]
                    p.op("tensor", lambda e, kb=kb, tp=tp, LNB=LNB: e.transpose(
                        out=tp[:], in_=LNB[:, kb * 128:(kb + 1) * 128], identity=identb[:]),
                         reads=[LNB, identb], writes=[tp])
                    p.op("scalar", lambda e, kb=kb, tp=tp, tt=tt: e.activation(
                        out=lnT[:, kb, 1 + tt * 128:1 + (tt + 1) * 128], in_=tp[:], func=AF.Copy),
                         reads=[tp], writes=[(lnT, (kb, tt))])
            for kb in range(8):
                p.op("vector", lambda e, kb=kb: e.tensor_scalar(
                    out=t1[:], in0=lnT[:, kb, 1:1 + CHUNK], scalar1=mur[:, kb:kb + 1],
                    scalar2=None, op0=ALU.mult),
                     reads=[lnT, mur], writes=[t1])
                p.op("vector", lambda e, kb=kb: e.scalar_tensor_tensor(
                    out=xrT[:, kb, :], in0=lnT[:, kb, 0:CHUNK], scalar=murm[:, kb:kb + 1],
                    in1=t1[:], op0=ALU.mult, op1=ALU.add),
                     reads=[lnT, murm, t1], writes=[(xrT, kb)])
                p.op("vector", lambda e, kb=kb: e.tensor_scalar(
                    out=t1[:], in0=lnT[:, kb, 1:1 + CHUNK], scalar1=muk[:, kb:kb + 1],
                    scalar2=None, op0=ALU.mult),
                     reads=[lnT, muk], writes=[t1])
                p.op("vector", lambda e, kb=kb: e.scalar_tensor_tensor(
                    out=xkT[:, kb, :], in0=lnT[:, kb, 0:CHUNK], scalar=mukm[:, kb:kb + 1],
                    in1=t1[:], op0=ALU.mult, op1=ALU.add),
                     reads=[lnT, mukm, t1], writes=[(xkT, kb)])
            for och in range(8):
                ps = mm_ps[och % 2]
                def fk(e, och=och, ps=ps):
                    last = None
                    for kb in range(8):
                        last = e.matmul(out=ps[:], lhsT=WT["k"][:, kb, och * 128:(och + 1) * 128],
                                        rhs=xkT[:, kb, :], start=(kb == 0), stop=(kb == 7))
                    return last
                p.op("tensor", fk, reads=[xkT, WT["k"]], writes=[ps])
                p.op("scalar", lambda e, ps=ps: e.activation(out=krel[:], in_=ps[:], func=AF.Relu),
                     reads=[ps], writes=[krel])
                p.op("scalar", lambda e, och=och: e.activation(out=kaT[:, och, :], in_=krel[:], func=AF.Square),
                     reads=[krel], writes=[(kaT, och)])
            for tt in range(CHUNK // 128):
                for oc in range(2):
                    ps = mm_ps[oc]
                    def fr(e, tt=tt, oc=oc, ps=ps):
                        last = None
                        for kb in range(8):
                            last = e.matmul(out=ps[:], lhsT=xrT[:, kb, tt * 128:(tt + 1) * 128],
                                            rhs=WT["r"][:, kb, oc * 512:(oc + 1) * 512],
                                            start=(kb == 0), stop=(kb == 7))
                        return last
                    p.op("tensor", fr, reads=[xrT, WT["r"]], writes=[ps])
                    p.op("scalar", lambda e, oc=oc, ps=ps: e.activation(
                        out=rS[:, oc * 512:(oc + 1) * 512], in_=ps[:], func=AF.Sigmoid),
                         reads=[ps], writes=[(rS, oc)])
                for oc in range(2):
                    ps = mm_ps[2 + oc]
                    def fv(e, tt=tt, oc=oc, ps=ps):
                        last = None
                        for kb in range(8):
                            last = e.matmul(out=ps[:], lhsT=kaT[:, kb, tt * 128:(tt + 1) * 128],
                                            rhs=WT["v"][:, kb, oc * 512:(oc + 1) * 512],
                                            start=(kb == 0), stop=(kb == 7))
                        return last
                    p.op("tensor", fv, reads=[kaT, WT["v"]], writes=[ps])
                    p.op("vector", lambda e, oc=oc, ps=ps: e.tensor_mul(
                        out=vo[:, oc * 512:(oc + 1) * 512],
                        in0=rS[:, oc * 512:(oc + 1) * 512], in1=ps[:]),
                         reads=[(rS, oc), ps], writes=[(vo, oc)])
                XO = xo_t[tt % 2]
                X2 = xc[tt % 2]
                p.dma("sync", X2[:], x_in[b, 1 + c0 + tt * 128:1 + c0 + (tt + 1) * 128, :], writes=[X2])
                p.op("vector", lambda e, XO=XO, X2=X2: e.scalar_tensor_tensor(
                    out=XO[:], in0=vo[:], scalar=0.5, in1=X2[:], op0=ALU.mult, op1=ALU.add),
                     reads=[vo, X2], writes=[XO])
                p.dma("gpsimd", x_out[b, c0 + tt * 128:c0 + (tt + 1) * 128, :], XO[:],
                      reads=[XO], sem=f"xst{tt % 2}")
    return


def build_l1ffn(tag="l1ffn"):
    p = Prog(tag)
    io = dict(
        x=p.dram("x", [B, TLOC + 1, D], F32, kind="ExternalInput").ap(),
        Wr=p.dram("Wr", [D, D], F32, kind="ExternalInput").ap(),
        Wk=p.dram("Wk", [D, D], F32, kind="ExternalInput").ap(),
        Wv=p.dram("Wv", [D, D], F32, kind="ExternalInput").ap(),
        mur=p.dram("mur", [128, 8], F32, kind="ExternalInput").ap(),
        muk=p.dram("muk", [128, 8], F32, kind="ExternalInput").ap(),
        ident_in=p.dram("ident_in", [128, 128], F32, kind="ExternalInput").ap(),
        xo=p.dram("xo", [B, TLOC, D], F32, kind="ExternalOutput").ap(),
    )
    with p.stage("lf"):
        emit_l1ffn(p, io)
    p.finalize()
    return p



def emit_l1af(p, io):
    H = 3
    NCH = TLOC // CHUNK
    x_in = io["x"]
    Wq_i = io["Wq"]; Wk_i = io["Wk"]; Wv_i = io["Wv"]
    bqkv_i = io["bqkv"]; cw_i = io["cw"]; cb_i = io["cb"]; hmask_i = io["hmask"]
    lncols_i = io["lncols"]; ident_in = io["ident_in"]
    g1_o = io["g1"]; qT_o = io["qT"]; kT_o = io["kT"]; vT_o = io["vT"]

    ident = p.sbuf("ident", [128, 128], F32)
    identb = p.sbuf("identb", [128, 128], BF16)
    p.dma("sync", ident[:], ident_in, writes=[ident])
    p.op("vector", lambda e: e.tensor_copy(out=identb[:], in_=ident[:]), reads=[ident], writes=[identb])
    bqkv = p.sbuf("bqkv_s", [128, 3, 8], F32)
    p.dma("sync", bqkv[:], bqkv_i, writes=[bqkv])
    lncols = p.sbuf("lncols_s", [128, 6, 8], F32)
    p.dma("sync", lncols[:], lncols_i, writes=[lncols])
    cbS = p.sbuf("cbS", [1, 1024], BF16)
    cbf = p.sbuf("cbf", [1, 1024], F32)
    p.dma("sync", cbf[:], cb_i, writes=[cbf])
    p.op("vector", lambda e: e.tensor_copy(out=cbS[:], in_=cbf[:]), reads=[cbf], writes=[cbS])
    ones_row = p.sbuf("ones_row", [1, 128], BF16)
    p.op("vector", lambda e: e.memset(ones_row[:], 1.0), writes=[ones_row])

    WT = {n: p.sbuf(f"WT{n}", [128, 8, D], BF16) for n in "qkv"}
    wld = p.sbuf("wld", [128, D], F32)
    tr_ps = p.psum("tr_ps", [128, 128], F32)
    for n, W in (("q", Wq_i), ("k", Wk_i), ("v", Wv_i)):
        for rb in range(8):
            p.dma("sync", wld[:], W[rb * 128:(rb + 1) * 128, :], writes=[wld])
            for kb in range(8):
                sl = wld[:, kb * 128:(kb + 1) * 128]
                p.op("tensor", lambda e, sl=sl: e.transpose(out=tr_ps[:], in_=sl, identity=ident[:]),
                     reads=[wld, ident], writes=[tr_ps])
                dsl = WT[n][:, kb, rb * 128:(rb + 1) * 128]
                p.op("scalar", lambda e, dsl=dsl: e.activation(out=dsl, in_=tr_ps[:], func=AF.Copy),
                     reads=[tr_ps], writes=[(WT[n], (kb, rb))])
    WcT = p.sbuf("WcT", [128, 4, 4, 2, 256], BF16)
    for g in range(4):
        for ob in range(2):
            for k_ in range(4):
                p.dma("sync", wld[:, 0:256], cw_i[k_, g * 256 + ob * 128:g * 256 + (ob + 1) * 128, :],
                      writes=[wld], sem="wld_c")
                for cb in range(2):
                    sl = wld[:, cb * 128:(cb + 1) * 128]
                    p.op("tensor", lambda e, sl=sl: e.transpose(out=tr_ps[:], in_=sl, identity=ident[:]),
                         reads=[wld, ident], writes=[tr_ps])
                    dsl = WcT[:, g, k_, cb, ob * 128:(ob + 1) * 128]
                    p.op("scalar", lambda e, dsl=dsl: e.activation(out=dsl, in_=tr_ps[:], func=AF.Copy),
                         reads=[tr_ps], writes=[(WcT, (g, k_, cb, ob))])

    hmask = p.sbuf("hmask_s", [128, 1], F32)
    p.dma("sync", hmask[:], hmask_i, writes=[hmask])
    hhalo = p.sbuf("hhalo", [128, 8, 4], BF16)
    xt = [p.sbuf(f"xt{i}", [128, D], F32) for i in range(2)]
    sq = [p.sbuf(f"sq{i}", [128, D], F32) for i in range(2)]
    lnb = [p.sbuf(f"lnb{i}", [128, D], BF16) for i in range(2)]
    stats = [make_stat(p, f"st{i}") for i in range(2)]
    trb_ps = [p.psum(f"trb_ps{i}", [128, 128], BF16) for i in range(2)]
    hcT = p.sbuf("hcT", [128, 8, CHUNK + H], BF16)
    haT = p.sbuf("haT", [128, 8, CHUNK], BF16)
    g1t = [p.sbuf(f"g1t{i}", [128, D], F32) for i in range(2)]
    qkv_ev = p.sbuf("qkv_ev", [128, CHUNK], BF16)
    cps = [p.psum(f"cps{i}", [128, 512], F32) for i in range(2)]
    mm_ps = [p.psum(f"mm_ps{i}", [128, 512], F32) for i in range(2)]

    g0buf = [p.sbuf(f"g0buf{i}", [128, D], F32) for i in range(2)]
    g0keep = p.sbuf("g0keep", [128, CHUNK // 128, D], F32)

    def ln_tr(b, tok0, ntok, col0, dst, scale_idx, bias_idx, x_from=None, par=0, keep_tt=None):
        X, SQ, LNB, st = xt[par], sq[par], lnb[par], stats[par]
        if x_from is None:
            p.dma("sync", X[0:ntok, :], x_in[b, tok0:tok0 + ntok, :], writes=[X])
            G0 = g0buf[par]
            ln_stats_n(p, X, SQ, st, ntok)
            p.op("scalar", lambda e, X=X, G0=G0, st=st, ntok=ntok: e.activation(
                out=G0[0:ntok, :], in_=X[0:ntok, :], func=AF.Identity,
                bias=st["negmur"][0:ntok, 0:1], scale=st["rstd"][0:ntok, 0:1]),
                 reads=[X, st["negmur"], st["rstd"]], writes=[G0])
            if keep_tt is not None:
                p.op("vector", lambda e, G0=G0, keep_tt=keep_tt: e.tensor_copy(
                    out=g0keep[0:ntok, keep_tt, :], in_=G0[0:ntok, :]),
                     reads=[G0], writes=[(g0keep, keep_tt)])
            src_t = G0
        else:
            src_t = x_from
        ln_stats_n(p, src_t, SQ, st, ntok)
        p.op("scalar", lambda e, src_t=src_t, LNB=LNB, st=st, ntok=ntok: e.activation(
            out=LNB[0:ntok, :], in_=src_t[0:ntok, :], func=AF.Identity,
            bias=st["negmur"][0:ntok, 0:1], scale=st["rstd"][0:ntok, 0:1]),
             reads=[src_t, st["negmur"], st["rstd"]], writes=[LNB])
        for kb in range(8):
            tp = trb_ps[kb % 2]
            idb = identb[0:ntok, 0:ntok]
            p.op("tensor", lambda e, kb=kb, tp=tp, LNB=LNB, ntok=ntok, idb=idb: e.transpose(
                out=tp[:, 0:ntok], in_=LNB[0:ntok, kb * 128:(kb + 1) * 128], identity=idb),
                 reads=[LNB, identb], writes=[tp])
            dsl = dst[:, kb, col0:col0 + ntok]
            p.op("scalar", lambda e, dsl=dsl, tp=tp, kb=kb, ntok=ntok: e.activation(
                out=dsl, in_=tp[:, 0:ntok], func=AF.Identity,
                scale=lncols[:, scale_idx, kb:kb + 1], bias=lncols[:, bias_idx, kb:kb + 1]),
                 reads=[tp, lncols], writes=[(dst, (kb, col0))])

    for b in range(B):
        for ci in range(NCH):
            c0 = ci * CHUNK
            if ci == 0:
                ln_tr(b, 0, H, 0, hhalo, 0, 1)
                p.op("vector", lambda e: e.tensor_scalar(out=hcT[:, :, 0:H], in0=hhalo[:, :, 0:H],
                                                         scalar1=hmask[:, 0:1], scalar2=None, op0=ALU.mult),
                     reads=[hhalo, hmask], writes=[hcT])
            else:
                p.op("vector", lambda e: e.tensor_copy(out=hcT[:, :, 0:H], in_=hcT[:, :, CHUNK:CHUNK + H]),
                     reads=[hcT], writes=[hcT])
            for tt in range(CHUNK // 128):
                ln_tr(b, H + c0 + tt * 128, 128, H + tt * 128, hcT, 0, 1, par=tt % 2, keep_tt=tt)
            for tt in range(CHUNK // 128):
                for g in range(4):
                    ps = cps[g // 2]
                    colr = (g % 2) * 256
                    def fc(e, g=g, tt=tt, ps=ps, colr=colr):
                        e.matmul(out=ps[:, colr:colr + 256], lhsT=ones_row[:],
                                 rhs=cbS[:, g * 256:(g + 1) * 256], start=True, stop=False)
                        last = None
                        for k_ in range(4):
                            for cb in range(2):
                                last = e.matmul(out=ps[:, colr:colr + 256],
                                                lhsT=hcT[:, g * 2 + cb, tt * 128 + k_:tt * 128 + k_ + 128],
                                                rhs=WcT[:, g, k_, cb, :],
                                                start=False, stop=(k_ == 3 and cb == 1))
                        return last
                    p.op("tensor", fc, reads=[hcT, WcT, ones_row, cbS], writes=[(cps[g // 2], g % 2)])
                G1 = g1t[tt % 2]
                for half in range(2):
                    p.op("vector", lambda e, half=half, G1=G1, tt=tt: e.tensor_add(
                        out=G1[:, half * 512:(half + 1) * 512], in0=cps[half][:],
                        in1=g0keep[:, tt, half * 512:(half + 1) * 512]),
                         reads=[(cps[half], 0), (cps[half], 1), (g0keep, tt)], writes=[(G1, half)])
                p.dma("gpsimd", g1_o[b, c0 + tt * 128:c0 + (tt + 1) * 128, :], G1[:],
                      reads=[G1], sem=f"g1st{tt % 2}")
                ln_tr(b, 0, 128, tt * 128, haT, 2, 3, x_from=G1, par=tt % 2)
            for ni, n in enumerate("qkv"):
                dramo = {"q": qT_o, "k": kT_o, "v": vT_o}[n]
                for och in range(8):
                    ps = mm_ps[och % 2]
                    def fq(e, n=n, och=och, ps=ps):
                        last = None
                        for kb in range(8):
                            last = e.matmul(out=ps[:], lhsT=WT[n][:, kb, och * 128:(och + 1) * 128],
                                            rhs=haT[:, kb, :], start=(kb == 0), stop=(kb == 7))
                        return last
                    p.op("tensor", fq, reads=[haT, WT[n]], writes=[ps])
                    p.op("scalar", lambda e, ni=ni, och=och, ps=ps: e.activation(
                        out=qkv_ev[:], in_=ps[:], func=AF.Identity, bias=bqkv[:, ni, och:och + 1]),
                         reads=[ps, bqkv], writes=[qkv_ev])
                    p.dma("gpsimd", dramo[b, och * 128:(och + 1) * 128, c0:c0 + CHUNK], qkv_ev[:],
                          reads=[qkv_ev], sem="qkvst")
    return


def _l1af_io(p):
    return dict(
        x=p.dram("x", [B, TLOC + 3, D], F32, kind="ExternalInput").ap(),
        Wq=p.dram("Wq", [D, D], F32, kind="ExternalInput").ap(),
        Wk=p.dram("Wk", [D, D], F32, kind="ExternalInput").ap(),
        Wv=p.dram("Wv", [D, D], F32, kind="ExternalInput").ap(),
        bqkv=p.dram("bqkv", [128, 3, 8], F32, kind="ExternalInput").ap(),
        cw=p.dram("cw", [4, 1024, 256], F32, kind="ExternalInput").ap(),
        cb=p.dram("cb", [1, 1024], F32, kind="ExternalInput").ap(),
        hmask=p.dram("hmask", [128, 1], F32, kind="ExternalInput").ap(),
        lncols=p.dram("lncols", [128, 6, 8], F32, kind="ExternalInput").ap(),
        ident_in=p.dram("ident_in", [128, 128], F32, kind="ExternalInput").ap(),
        g1=p.dram("g1", [B, TLOC, D], F32, kind="ExternalOutput").ap(),
        qT=p.dram("qT", [B, D, TLOC], BF16, kind="ExternalOutput").ap(),
        kT=p.dram("kT", [B, D, TLOC], BF16, kind="ExternalOutput").ap(),
        vT=p.dram("vT", [B, D, TLOC], BF16, kind="ExternalOutput").ap(),
    )


def build_l1attn_front(tag="l1af"):
    p = Prog(tag)
    io = _l1af_io(p)
    with p.stage("la"):
        emit_l1af(p, io)
    p.finalize()
    return p



def emit_pooled(p, io, T=S):
    PCH = 2048
    NCHP = T // PCH
    qT_i = io["qT"]; kT_i = io["kT"]; vT_i = io["vT"]; rs_i = io["rs"]; pkvT_o = io["pkvT"]

    ones = p.sbuf("ones", [128, PCH], F32)
    p.op("vector", lambda e: e.memset(ones[:], 1.0), writes=[ones])
    rsb = p.sbuf("rsb", [128, PCH], F32)
    rsl = p.sbuf("rsl", [1, PCH], BF16)
    rslf = p.sbuf("rslf", [1, PCH], F32)
    one_col = p.sbuf("one_col", [1, 128], BF16)
    p.op("vector", lambda e: e.memset(one_col[:], 1.0), writes=[one_col])
    rs_ps = p.psum("rs_ps", [128, 512], F32)
    qt = p.sbuf("qt", [128, PCH], BF16)
    kt = p.sbuf("kt", [128, PCH], BF16)
    vt = p.sbuf("vt", [128, PCH], BF16)
    cum = p.sbuf("cum", [128, PCH], F32)
    pq = p.sbuf("pq", [128, PCH], F32)
    pqk = p.sbuf("pqk", [128, PCH], F32)
    cum2 = p.sbuf("cum2", [128, PCH], F32)
    pk = p.sbuf("pk", [128, PCH], F32)
    pkv = p.sbuf("pkv", [128, PCH], BF16)
    qcar = p.sbuf("qcar", [128, 1], F32)
    kcar = p.sbuf("kcar", [128, 1], F32)

    for b in range(B):
        p.op("vector", lambda e: e.memset(qcar[:], 0.0), writes=[qcar])
        p.op("vector", lambda e: e.memset(kcar[:], 0.0), writes=[kcar])
        for ci in range(NCHP):
            c0 = ci * PCH
            p.dma("sync", qt[:], qT_i[b, :, c0:c0 + PCH], writes=[qt])
            p.dma("sync", kt[:], kT_i[b, :, c0:c0 + PCH], writes=[kt])
            p.dma("sync", vt[:], vT_i[b, :, c0:c0 + PCH], writes=[vt])
            rbview = bass.AP(rs_i.tensor, rs_i.offset + c0, [[0, 128], [1, PCH]])
            p.dma("sync", rsb[:], rbview, writes=[rsb])
            p.op("vector", lambda e: e.tensor_tensor_scan(out=cum[:], data0=ones[:], data1=qt[:],
                                                          initial=qcar[:, 0:1], op0=ALU.mult, op1=ALU.add),
                 reads=[ones, qt, qcar], writes=[cum])
            p.op("vector", lambda e: e.tensor_copy(out=qcar[:], in_=cum[:, PCH - 1:PCH]),
                 reads=[cum], writes=[qcar])
            p.op("vector", lambda e: e.tensor_mul(out=pq[:], in0=cum[:], in1=rsb[:]),
                 reads=[cum, rsb], writes=[pq])
            p.op("vector", lambda e: e.tensor_mul(out=pqk[:], in0=pq[:], in1=kt[:]),
                 reads=[pq, kt], writes=[pqk])
            p.op("vector", lambda e: e.tensor_tensor_scan(out=cum2[:], data0=ones[:], data1=pqk[:],
                                                          initial=kcar[:, 0:1], op0=ALU.mult, op1=ALU.add),
                 reads=[ones, pqk, kcar], writes=[cum2])
            p.op("vector", lambda e: e.tensor_copy(out=kcar[:], in_=cum2[:, PCH - 1:PCH]),
                 reads=[cum2], writes=[kcar])
            p.op("vector", lambda e: e.tensor_mul(out=pk[:], in0=cum2[:], in1=rsb[:]),
                 reads=[cum2, rsb], writes=[pk])
            p.op("vector", lambda e: e.tensor_mul(out=pkv[:], in0=pk[:], in1=vt[:]),
                 reads=[pk, vt], writes=[pkv])
            p.dma("gpsimd", pkvT_o[b, :, c0:c0 + PCH], pkv[:], reads=[pkv], sem="pkvst")
    return


def build_pooled(T=S, tag="pooled"):
    p = Prog(tag)
    io = dict(
        qT=p.dram("qT", [B, 128, T], BF16, kind="ExternalInput").ap(),
        kT=p.dram("kT", [B, 128, T], BF16, kind="ExternalInput").ap(),
        vT=p.dram("vT", [B, 128, T], BF16, kind="ExternalInput").ap(),
        rs=p.dram("rs", [1, T], F32, kind="ExternalInput").ap(),
        pkvT=p.dram("pkvT", [B, 128, T], BF16, kind="ExternalOutput").ap(),
    )
    with p.stage("po"):
        emit_pooled(p, io, T=T)
    p.finalize()
    return p



def emit_l1ab(p, io):
    CHI = 256
    NCH = TLOC // CHI
    x_in = io["x"]; g1_i = io["g1"]; pkvT_i = io["pkvT"]
    Wb_i = io["Wb"]; Wu_i = io["Wu"]; bb_i = io["bb"]; bu_i = io["bu"]
    lncols_i = io["lncols"]; ident_in = io["ident_in"]; x_out = io["xo"]

    ident = p.sbuf("ident", [128, 128], F32)
    identb = p.sbuf("identb", [128, 128], BF16)
    p.dma("sync", ident[:], ident_in, writes=[ident])
    p.op("vector", lambda e: e.tensor_copy(out=identb[:], in_=ident[:]), reads=[ident], writes=[identb])
    bbcol = p.sbuf("bbcol", [128, 32], F32)
    p.dma("sync", bbcol[:], bb_i, writes=[bbcol])
    lncols = p.sbuf("lncols_s", [128, 2, 8], F32)
    p.dma("sync", lncols[:], lncols_i, writes=[lncols])
    buS = p.sbuf("buS", [1, D], BF16)
    buf_ = p.sbuf("buf_", [1, D], F32)
    p.dma("sync", buf_[:], bu_i, writes=[buf_])
    p.op("vector", lambda e: e.tensor_copy(out=buS[:], in_=buf_[:]), reads=[buf_], writes=[buS])
    ones_row = p.sbuf("ones_row", [1, 128], BF16)
    p.op("vector", lambda e: e.memset(ones_row[:], 1.0), writes=[ones_row])

    WbT = p.sbuf("WbT", [128, 8, 4096], BF16)
    WuT = p.sbuf("WuT", [128, 32, D], BF16)
    wld = p.sbuf("wld", [128, 1024], F32)
    tr_ps = p.psum("tr_ps", [128, 128], F32)
    for rb in range(32):
        p.dma("sync", wld[:], Wb_i[rb * 128:(rb + 1) * 128, :], writes=[wld])
        for kb in range(8):
            sl = wld[:, kb * 128:(kb + 1) * 128]
            p.op("tensor", lambda e, sl=sl: e.transpose(out=tr_ps[:], in_=sl, identity=ident[:]),
                 reads=[wld, ident], writes=[tr_ps])
            dsl = WbT[:, kb, rb * 128:(rb + 1) * 128]
            p.op("scalar", lambda e, dsl=dsl: e.activation(out=dsl, in_=tr_ps[:], func=AF.Copy),
                 reads=[tr_ps], writes=[(WbT, (kb, rb))])
    for rb in range(8):
        for qc in range(4):
            p.dma("sync", wld[:], Wu_i[rb * 128:(rb + 1) * 128, qc * 1024:(qc + 1) * 1024], writes=[wld])
            for kq in range(8):
                kb = qc * 8 + kq
                sl = wld[:, kq * 128:(kq + 1) * 128]
                p.op("tensor", lambda e, sl=sl: e.transpose(out=tr_ps[:], in_=sl, identity=ident[:]),
                     reads=[wld, ident], writes=[tr_ps])
                dsl = WuT[:, kb, rb * 128:(rb + 1) * 128]
                p.op("scalar", lambda e, dsl=dsl: e.activation(out=dsl, in_=tr_ps[:], func=AF.Copy),
                     reads=[tr_ps], writes=[(WuT, (kb, rb))])

    xt = [p.sbuf(f"xt{i}", [128, D], F32) for i in range(2)]
    sq = [p.sbuf(f"sq{i}", [128, D], F32) for i in range(2)]
    lnb = [p.sbuf(f"lnb{i}", [128, D], BF16) for i in range(2)]
    stats = [make_stat(p, f"st{i}") for i in range(2)]
    trb_ps = [p.psum(f"trb_ps{i}", [128, 128], BF16) for i in range(2)]
    pkvt = p.sbuf("pkvt", [128, 128], BF16)
    g2t = [p.sbuf(f"g2t{i}", [128, D], F32) for i in range(2)]
    g2keep = p.sbuf("g2keep", [128, CHI // 128, D], F32)
    hbT = p.sbuf("hbT", [128, 8, CHI], BF16)
    bhT = p.sbuf("bhT", [128, 32, CHI], BF16)
    mm_ps = [p.psum(f"mm_ps{i}", [128, 512], F32) for i in range(4)]
    xo_t = [p.sbuf(f"xo_t{i}", [128, D], F32) for i in range(2)]

    def ln_tr_aff(src_t, col0, dst, par):
        SQ, LNB, st = sq[par], lnb[par], stats[par]
        ln_stats_n(p, src_t, SQ, st, 128)
        p.op("scalar", lambda e, src_t=src_t, LNB=LNB, st=st: e.activation(
            out=LNB[:], in_=src_t[:], func=AF.Identity,
            bias=st["negmur"][:, 0:1], scale=st["rstd"][:, 0:1]),
             reads=[src_t, st["negmur"], st["rstd"]], writes=[LNB])
        for kb in range(8):
            tp = trb_ps[kb % 2]
            p.op("tensor", lambda e, kb=kb, tp=tp, LNB=LNB: e.transpose(
                out=tp[:], in_=LNB[:, kb * 128:(kb + 1) * 128], identity=identb[:]),
                 reads=[LNB, identb], writes=[tp])
            dsl = dst[:, kb, col0:col0 + 128]
            p.op("scalar", lambda e, dsl=dsl, tp=tp, kb=kb: e.activation(
                out=dsl, in_=tp[:], func=AF.Identity,
                scale=lncols[:, 0, kb:kb + 1], bias=lncols[:, 1, kb:kb + 1]),
                 reads=[tp, lncols], writes=[(dst, (kb, col0))])

    for b in range(B):
        for ci in range(NCH):
            c0 = ci * CHI
            for tt in range(CHI // 128):
                par = tt % 2
                G2 = g2t[par]
                X = xt[par]
                p.dma("sync", X[:], g1_i[b, c0 + tt * 128:c0 + (tt + 1) * 128, :], writes=[X])
                for kb in range(8):
                    p.dma("sync", pkvt[:], pkvT_i[b, kb * 128:(kb + 1) * 128,
                                                         c0 + tt * 128:c0 + (tt + 1) * 128],
                          writes=[pkvt])
                    tp = trb_ps[kb % 2]
                    p.op("tensor", lambda e, tp=tp: e.transpose(out=tp[:], in_=pkvt[:],
                                                                identity=identb[:]),
                         reads=[pkvt, identb], writes=[tp])
                    p.op("vector", lambda e, kb=kb, tp=tp, G2=G2, X=X: e.tensor_add(
                        out=G2[:, kb * 128:(kb + 1) * 128], in0=tp[:], in1=X[:, kb * 128:(kb + 1) * 128]),
                         reads=[tp, X], writes=[(G2, kb)])
                p.op("vector", lambda e, tt=tt, G2=G2: e.tensor_copy(out=g2keep[:, tt, :], in_=G2[:]),
                     reads=[G2], writes=[(g2keep, tt)])
                ln_tr_aff(G2, tt * 128, hbT, par)
            for och in range(32):
                ps = mm_ps[och % 2]
                def fb1(e, och=och, ps=ps):
                    last = None
                    for kb in range(8):
                        last = e.matmul(out=ps[:, 0:CHI], lhsT=WbT[:, kb, och * 128:(och + 1) * 128],
                                        rhs=hbT[:, kb, :], start=(kb == 0), stop=(kb == 7))
                    return last
                p.op("tensor", fb1, reads=[hbT, WbT], writes=[ps])
                p.op("scalar", lambda e, och=och, ps=ps: e.activation(
                    out=bhT[:, och, :], in_=ps[:, 0:CHI], func=AF.Gelu, bias=bbcol[:, och:och + 1]),
                     reads=[ps, bbcol], writes=[(bhT, och)])
            for tt in range(CHI // 128):
                XO = xo_t[tt % 2]
                X4 = xt[tt % 2]
                p.dma("sync", X4[:], x_in[b, c0 + tt * 128:c0 + (tt + 1) * 128, :], writes=[X4],
                      sem=f"x4ld{tt % 2}")
                for oc in range(2):
                    ps = mm_ps[2 + oc]
                    def fo(e, tt=tt, oc=oc, ps=ps):
                        e.matmul(out=ps[:], lhsT=ones_row[:], rhs=buS[:, oc * 512:(oc + 1) * 512],
                                 start=True, stop=False)
                        last = None
                        for kb in range(32):
                            last = e.matmul(out=ps[:], lhsT=bhT[:, kb, tt * 128:(tt + 1) * 128],
                                            rhs=WuT[:, kb, oc * 512:(oc + 1) * 512],
                                            start=False, stop=(kb == 31))
                        return last
                    p.op("tensor", fo, reads=[bhT, WuT, ones_row, buS], writes=[ps])
                    p.op("vector", lambda e, oc=oc, ps=ps, tt=tt, XO=XO: e.tensor_add(
                        out=XO[:, oc * 512:(oc + 1) * 512], in0=ps[:],
                        in1=g2keep[:, tt, oc * 512:(oc + 1) * 512]),
                         reads=[ps, (g2keep, tt)], writes=[(XO, oc)])
                p.op("vector", lambda e, XO=XO, X4=X4: e.tensor_add(out=XO[:], in0=XO[:], in1=X4[:]),
                     reads=[XO, X4], writes=[XO])
                p.dma("gpsimd", x_out[b, c0 + tt * 128:c0 + (tt + 1) * 128, :], XO[:],
                      reads=[XO], sem=f"xst{tt % 2}")
    return


def build_l1attn_back(tag="l1ab"):
    p = Prog(tag)
    io = dict(
        x=p.dram("x", [B, TLOC, D], F32, kind="ExternalInput").ap(),
        g1=p.dram("g1", [B, TLOC, D], F32, kind="ExternalInput").ap(),
        pkvT=p.dram("pkvT", [B, D, TLOC], BF16, kind="ExternalInput").ap(),
        Wb=p.dram("Wb", [4096, D], F32, kind="ExternalInput").ap(),
        Wu=p.dram("Wu", [D, 4096], F32, kind="ExternalInput").ap(),
        bb=p.dram("bb", [128, 32], F32, kind="ExternalInput").ap(),
        bu=p.dram("bu", [1, D], F32, kind="ExternalInput").ap(),
        lncols=p.dram("lncols", [128, 2, 8], F32, kind="ExternalInput").ap(),
        ident_in=p.dram("ident_in", [128, 128], F32, kind="ExternalInput").ap(),
        xo=p.dram("xo", [B, TLOC, D], F32, kind="ExternalOutput").ap(),
    )
    with p.stage("lb"):
        emit_l1ab(p, io)
    p.finalize()
    return p



_PROG_CACHE = {}

def _prog(name, builder):
    if name not in _PROG_CACHE:
        _PROG_CACHE[name] = builder()
    return _PROG_CACHE[name]

def _ident_np():
    return np.eye(128, dtype=np.float32)

def _col8(v):
    return np.ascontiguousarray(np.asarray(v).reshape(8, 128).T)

def _col16(v):
    return np.ascontiguousarray(np.asarray(v).reshape(16, 128).T)

def _col32(v):
    return np.ascontiguousarray(np.asarray(v).reshape(32, 128).T)

def _sel_np():
    sel = np.zeros((32, 32, 128), np.float32)
    for j in range(32):
        sel[j, j, :] = 1.0
    return sel.astype(BF16NP)

def _run(p, in_maps):
    return run_bass_kernel_spmd(p.nc, in_maps, core_ids=list(range(N_CORES)))


def run_moe(x_full, We, be, Wr, br):
    p = _prog("moe", build_moe)
    in_maps = []
    for c in range(N_CORES):
        sl = x_full[:, c * TLOC:(c + 1) * TLOC, :]
        in_maps.append(dict(x=np.ascontiguousarray(sl), We=We, be=be, Wr=Wr, br=br,
                            ident_in=_ident_np()))
    res = _run(p, in_maps)
    out = np.empty_like(x_full)
    for c in range(N_CORES):
        out[:, c * TLOC:(c + 1) * TLOC, :] = res.results[c]["xo"]
    return out


def run_mfront(x1, ins):
    p = _prog("mfront", build_mamba_front)
    in_maps = []
    for c in range(N_CORES):
        lo, hi = c * TLOC, (c + 1) * TLOC
        xh = np.zeros((B, TLOC + 6, D), np.float32)
        xh[:, 3:3 + TLOC] = x1[:, lo:hi]
        if lo > 0:
            xh[:, 0:3] = x1[:, lo - 3:lo]
        if hi < S:
            xh[:, 3 + TLOC:] = x1[:, hi:hi + 3]
        d = dict(x=xh, ident_in=_ident_np())
        for d_, pref in (("f", "fwd_"), ("b", "bwd_")):
            d[f"Win_{d_}"] = ins[f"L0_attn_{pref}in_proj_w"]
            d[f"convw_{d_}"] = np.ascontiguousarray(
                ins[f"L0_attn_{pref}conv_w"][:, 0, :].reshape(16, 128, 4).transpose(1, 0, 2))
            d[f"convb_{d_}"] = _col16(ins[f"L0_attn_{pref}conv_b"])
            d[f"xproj_{d_}"] = ins[f"L0_attn_{pref}x_proj_w"]
            d[f"dtw_{d_}"] = ins[f"L0_attn_{pref}dt_proj_w"]
            d[f"dtb_{d_}"] = _col16(ins[f"L0_attn_{pref}dt_proj_b"])
        in_maps.append(d)
    res = _run(p, in_maps)
    st = {}
    for name in ("uT", "zT", "dT", "duT", "bcT"):
        for d_ in "fb":
            st[f"{name}_{d_}"] = np.concatenate(
                [np.asarray(res.results[c][f"{name}_{d_}"]) for c in range(N_CORES)], axis=2)
    return st


def run_scan(st, ins):
    p = _prog("scan", lambda: build_scan(T=S))
    in_maps, meta = [], []
    for d_ in "fb":
        alog = np.asarray(ins[f'L0_attn_{"fwd" if d_ == "f" else "bwd"}_A_log'], np.float32)
        for b in range(B):
            for half in range(2):
                dsl = slice(half * 1024, (half + 1) * 1024)
                dT = st[f'dT_{d_}'][b, dsl, :]
                duT = st[f'duT_{d_}'][b, dsl, :]
                bcT = st[f'bcT_{d_}'][b]
                if d_ == "b":
                    dT = dT[:, ::-1]; duT = duT[:, ::-1]; bcT = bcT[:, ::-1]
                in_maps.append(dict(dT=np.ascontiguousarray(dT), duT=np.ascontiguousarray(duT),
                                    bcT=np.ascontiguousarray(bcT),
                                    alog=np.ascontiguousarray(alog[dsl].reshape(8, 128, 16)),
                                    sel=_sel_np()))
                meta.append((d_, b, half))
    res = _run(p, in_maps)
    out = {d_: np.zeros((B, 2048, S), BF16NP) for d_ in "fb"}
    for c, (d_, b, half) in enumerate(meta):
        y = np.asarray(res.results[c]["yT"])
        if d_ == "b":
            y = y[:, ::-1]
        out[d_][b, half * 1024:(half + 1) * 1024, :] = y
    return out


def run_mback(x1, st, ys, ins):
    p = _prog("mback", build_mamba_back)
    in_maps = []
    for c in range(N_CORES):
        lo, hi = c * TLOC, (c + 1) * TLOC
        d = dict(x=np.ascontiguousarray(x1[:, lo:hi]),
                 outw_f=ins['L0_attn_fwd_out_proj_w'], outw_b=ins['L0_attn_bwd_out_proj_w'],
                 fw=ins['L0_attn_fusion_w'], fb=np.asarray(ins['L0_attn_fusion_b'])[None, :],
                 ident_in=_ident_np(),
                 Dv_f=_col16(ins['L0_attn_fwd_D']), Dv_b=_col16(ins['L0_attn_bwd_D']))
        for d_ in "fb":
            d[f"yT_{d_}"] = np.ascontiguousarray(ys[d_][:, :, lo:hi])
            d[f"uT_{d_}"] = np.ascontiguousarray(st[f'uT_{d_}'][:, :, lo:hi])
            d[f"zT_{d_}"] = np.ascontiguousarray(st[f'zT_{d_}'][:, :, lo:hi])
        in_maps.append(d)
    res = _run(p, in_maps)
    out = np.empty_like(x1)
    for c in range(N_CORES):
        out[:, c * TLOC:(c + 1) * TLOC, :] = res.results[c]["xo"]
    return out


def run_l1ffn_launch(x_full, ins):
    p = _prog("l1ffn", build_l1ffn)
    in_maps = []
    for c in range(N_CORES):
        lo, hi = c * TLOC, (c + 1) * TLOC
        xh = np.zeros((B, TLOC + 1, D), np.float32)
        xh[:, 1:] = x_full[:, lo:hi]
        if lo > 0:
            xh[:, 0] = x_full[:, lo - 1]
        in_maps.append(dict(x=xh, Wr=ins['L1_ffn_W_r'], Wk=ins['L1_ffn_W_k'], Wv=ins['L1_ffn_W_v'],
                            mur=_col8(ins['L1_ffn_mu_r']), muk=_col8(ins['L1_ffn_mu_k']),
                            ident_in=_ident_np()))
    res = _run(p, in_maps)
    out = np.empty_like(x_full)
    for c in range(N_CORES):
        out[:, c * TLOC:(c + 1) * TLOC, :] = res.results[c]["xo"]
    return out


def run_l1af(x4, ins):
    p = _prog("l1af", build_l1attn_front)
    lncols = np.zeros((128, 6, 8), np.float32)
    lncols[:, 0] = _col8(ins['L1_attn_ln_conv_g']); lncols[:, 1] = _col8(ins['L1_attn_ln_conv_b'])
    lncols[:, 2] = _col8(ins['L1_attn_ln_attn_g']); lncols[:, 3] = _col8(ins['L1_attn_ln_attn_b'])
    bqkv = np.zeros((128, 3, 8), np.float32)
    bqkv[:, 0] = _col8(ins['L1_attn_bq']); bqkv[:, 1] = _col8(ins['L1_attn_bk'])
    bqkv[:, 2] = _col8(ins['L1_attn_bv'])
    in_maps = []
    for c in range(N_CORES):
        lo, hi = c * TLOC, (c + 1) * TLOC
        xh = np.zeros((B, TLOC + 3, D), np.float32)
        xh[:, 3:] = x4[:, lo:hi]
        if lo > 0:
            xh[:, 0:3] = x4[:, lo - 3:lo]
        in_maps.append(dict(x=xh, Wq=ins['L1_attn_Wq'], Wk=ins['L1_attn_Wk'], Wv=ins['L1_attn_Wv'],
                            bqkv=bqkv, cw=np.ascontiguousarray(np.asarray(ins['L1_attn_conv_w']).transpose(2, 0, 1)),
                            cb=np.asarray(ins['L1_attn_conv_b'])[None, :], lncols=lncols,
                            ident_in=_ident_np(),
                            hmask=np.full((128, 1), 0.0 if lo == 0 else 1.0, np.float32)))
    res = _run(p, in_maps)
    g1 = np.concatenate([np.asarray(res.results[c]["g1"]) for c in range(N_CORES)], axis=1)
    qT = np.concatenate([np.asarray(res.results[c]["qT"]) for c in range(N_CORES)], axis=2)
    kT = np.concatenate([np.asarray(res.results[c]["kT"]) for c in range(N_CORES)], axis=2)
    vT = np.concatenate([np.asarray(res.results[c]["vT"]) for c in range(N_CORES)], axis=2)
    return g1, qT, kT, vT


def run_pooled(qT, kT, vT):
    p = _prog("pooled", lambda: build_pooled(T=S))
    rs = (1.0 / np.arange(1, S + 1, dtype=np.float32))[None, :]
    in_maps = [dict(qT=np.ascontiguousarray(qT[:, c * 128:(c + 1) * 128]),
                    kT=np.ascontiguousarray(kT[:, c * 128:(c + 1) * 128]),
                    vT=np.ascontiguousarray(vT[:, c * 128:(c + 1) * 128]), rs=rs)
               for c in range(N_CORES)]
    res = _run(p, in_maps)
    return np.concatenate([np.asarray(res.results[c]["pkvT"]) for c in range(N_CORES)], axis=1)


def run_l1ab(x4, g1, pkvT, ins):
    p = _prog("l1ab", build_l1attn_back)
    lncols = np.zeros((128, 2, 8), np.float32)
    lncols[:, 0] = _col8(ins['L1_attn_ln_boom_g']); lncols[:, 1] = _col8(ins['L1_attn_ln_boom_b'])
    in_maps = []
    for c in range(N_CORES):
        lo, hi = c * TLOC, (c + 1) * TLOC
        in_maps.append(dict(x=np.ascontiguousarray(x4[:, lo:hi]),
                            g1=np.ascontiguousarray(g1[:, lo:hi]),
                            pkvT=np.ascontiguousarray(pkvT[:, :, lo:hi]),
                            Wb=ins['L1_attn_Wb'], Wu=ins['L1_attn_Wu'],
                            bb=_col32(ins['L1_attn_bb']),
                            bu=np.asarray(ins['L1_attn_bu'])[None, :], lncols=lncols,
                            ident_in=_ident_np()))
    res = _run(p, in_maps)
    out = np.empty_like(x4)
    for c in range(N_CORES):
        out[:, c * TLOC:(c + 1) * TLOC, :] = res.results[c]["xo"]
    return out


def kernel(**inputs):
    x0 = np.ascontiguousarray(np.asarray(inputs["L0_attn_x"], np.float32))
    moe_args = (np.asarray(inputs["L0_ffn_W_experts"], np.float32),
                np.asarray(inputs["L0_ffn_b_experts"], np.float32),
                np.asarray(inputs["L0_ffn_W_router"], np.float32),
                np.asarray(inputs["L0_ffn_b_router"], np.float32))
    ins = {k: np.asarray(v) for k, v in inputs.items()}
    x1 = run_moe(x0, *moe_args)
    st = run_mfront(x1, ins)
    ys = run_scan(st, ins)
    x2 = run_mback(x1, st, ys, ins)
    x3 = run_moe(x2, *moe_args)
    x4 = run_l1ffn_launch(x3, ins)
    g1, qT, kT, vT = run_l1af(x4, ins)
    pkvT = run_pooled(qT, kT, vT)
    x5 = run_l1ab(x4, g1, pkvT, ins)
    x6 = run_l1ffn_launch(x5, ins)
    return x6
